# version 03
# baseline (speedup 1.0000x reference)
import numpy as np
import ml_dtypes
from contextlib import ExitStack

import concourse.bass as bass
import concourse.mybir as mybir
from concourse.bass_utils import run_bass_kernel_spmd
from concourse.replica_groups import maybe_share_collective_output_space
from concourse.bass_isa import ReduceOp

F32 = mybir.dt.float32
I32 = mybir.dt.int32
AF = mybir.ActivationFunctionType
ALU = mybir.AluOpType
AX = mybir.AxisListType

NCORES = 8
D = 1024
T = 4096
DC = D // 128
LSEG = 512
NSEG = T // LSEG
H = 16
DK = 64
HID = 3072
HC = HID // 128
RG8 = [[0, 1, 2, 3, 4, 5, 6, 7]]
C127 = float(np.float32(1.0 / 127.0))

ENGS = ("PE", "ACT", "DVE", "POOL", "SP")
_BA = {"PE": "tensor", "ACT": "scalar", "DVE": "vector", "POOL": "gpsimd", "SP": "sync"}

DEBUG = False
_DBG_SHAPES = {}
BUILD_STAGE = 2
ATT_SUB = 7
B_STAGE = 8


class Sched:

    def __init__(self, nc, block, ctx):
        self.nc, self.block, self.ctx = nc, block, ctx
        self.psem = {e: ctx.enter_context(nc.semaphore(f"P_{e}")) for e in ENGS}
        self.pcnt = {e: 0 for e in ENGS}
        self.drain_at = {e: 0 for e in ENGS}
        self.synced = {}
        self.dma_synced = {}
        self.writer = {}
        self.readers = {}
        self.dma_sem = {}

    def _slot(self, slot):
        if slot not in self.dma_sem:
            sem = self.ctx.enter_context(self.nc.semaphore(f"dma_{slot}"))
            self.dma_sem[slot] = (sem, [0])
        return self.dma_sem[slot]

    def _wait(self, eng, h, tok):
        if tok[0] == "E":
            _, peng, cnt = tok
            if peng == eng:
                if self.drain_at[eng] < cnt:
                    h.drain()
                    self.drain_at[eng] = self.pcnt[eng]
            else:
                key = (eng, peng)
                if self.synced.get(key, 0) < cnt:
                    h.wait_ge(self.psem[peng], cnt)
                    self.synced[key] = cnt
        else:
            _, slot, cnt16 = tok
            sem, _ = self._slot(slot)
            key = (eng, slot)
            if self.dma_synced.get(key, 0) < cnt16:
                h.wait_ge(sem, cnt16)
                self.dma_synced[key] = cnt16

    def _deps(self, reads, writes):
        toks = []
        for b in reads:
            w = self.writer.get(b)
            if w is not None:
                toks.append(w)
        for b in writes:
            w = self.writer.get(b)
            if w is not None:
                toks.append(w)
            toks.extend(self.readers.get(b, ()))
        best = {}
        for t in toks:
            k = t[:2]
            if k not in best or best[k][2] < t[2]:
                best[k] = t
        return list(best.values())

    def _commit(self, tok, reads, writes):
        for b in writes:
            self.writer[b] = tok
            self.readers[b] = []
        for b in reads:
            self.readers.setdefault(b, []).append(tok)

    def run(self, eng, emit, reads=(), writes=()):
        deps = self._deps(reads, writes)
        sem = self.psem[eng]

        def body(h):
            for t in deps:
                self._wait(eng, h, t)
            last = emit(h)
            assert last is not None, "emit() must return its last instruction"
            last.then_inc(sem, 1)

        getattr(self.block, _BA[eng])(body)
        self.pcnt[eng] += 1
        tok = ("E", eng, self.pcnt[eng])
        self._commit(tok, reads, writes)
        return tok

    def dma(self, eng, emit, reads=(), writes=(), slot=None, serialize=True):
        assert slot is not None
        deps = self._deps(reads, writes)
        sem, cnt = self._slot(slot)
        cnt[0] += 16
        target = cnt[0]
        if serialize and target > 16:
            deps = deps + [("D", slot, target - 16)]

        def body(h):
            for t in deps:
                self._wait(eng, h, t)
            emit(h).then_inc(sem, 16)

        getattr(self.block, _BA[eng])(body)
        tok = ("D", slot, target)
        self._commit(tok, reads, writes)
        return tok

    def wait_tokens(self, eng, toks):
        def body(h):
            for t in toks:
                self._wait(eng, h, t)
        getattr(self.block, _BA[eng])(body)



def _l0_mask():
    t = np.arange(512)
    gp, rem = t // 256, t % 256
    g2, r = rem // 32, rem % 32
    lh = 16 * r + 2 * g2 + gp
    return (lh[None, :] >= lh[:, None]).astype(np.float32)


def _prep_inputs(inputs):
    x = np.ascontiguousarray(np.asarray(inputs["L0_attn_x"], dtype=np.float32))
    ident = np.eye(128, dtype=np.float32)
    ones = np.ones((128, 128), dtype=np.float32)
    dup64 = np.zeros((64, 128), dtype=np.float32)
    dup64[np.arange(64), np.arange(64)] = 1.0
    dup64[np.arange(64), 64 + np.arange(64)] = 1.0
    swap = np.zeros((128, 128), dtype=np.float32)
    swap[np.arange(128), np.arange(128) ^ 64] = 1.0
    dupswap = np.zeros((128, 128), dtype=np.float32)
    dupswap[np.arange(128), np.arange(128)] = 1.0
    dupswap[np.arange(128), np.arange(128) ^ 64] = 1.0
    M0 = np.asarray(inputs["L0_attn_memory0"], np.float32).reshape(H, DK, DK)
    Mdup = np.zeros((128, H * 64), dtype=np.float32)
    for h in range(H):
        Mdup[0:64, 64 * h:64 * h + 64] = M0[h]
        Mdup[64:128, 64 * h:64 * h + 64] = M0[h]
    Z0 = np.asarray(inputs["L0_attn_z0"], np.float32).reshape(H, DK)
    Zdup = np.zeros((128, H), dtype=np.float32)
    for h in range(H):
        Zdup[0:64, h] = Z0[h]
        Zdup[64:128, h] = Z0[h]
    betas = np.asarray(inputs["L0_attn_betas"], np.float32).reshape(H, DK)
    sel8 = np.zeros((64, 64), dtype=np.float32)
    for c in range(8):
        sel8[:, 8 * c + c] = 1.0
    sel8t = np.zeros((8, 512), dtype=np.float32)
    for c in range(8):
        sel8t[c, 64 * c:64 * c + 64] = 1.0
    base = {
        "ident": ident, "ones": ones, "dup64": dup64, "swapm": swap, "dupswap": dupswap,
        "sel8": sel8, "sel8t": sel8t,
        "mask01": np.ascontiguousarray(_l0_mask().astype(ml_dtypes.bfloat16)),
        "betasc": np.ascontiguousarray(betas.T.copy()),
        "Mdup0": np.ascontiguousarray(Mdup),
        "Zdup0": np.ascontiguousarray(Zdup),
        "Wq0": np.ascontiguousarray(np.asarray(inputs["L0_attn_Wq"], np.float32)),
        "Wk0": np.ascontiguousarray(np.asarray(inputs["L0_attn_Wk"], np.float32)),
        "Wv0": np.ascontiguousarray(np.asarray(inputs["L0_attn_Wv"], np.float32)),
        "Wo0": np.ascontiguousarray(np.asarray(inputs["L0_attn_Wo"], np.float32)),
        "wgT0": np.ascontiguousarray(np.asarray(inputs["L0_ffn_w_gate"], np.int32).T),
        "wuT0": np.ascontiguousarray(np.asarray(inputs["L0_ffn_w_up"], np.int32).T),
        "wdT0": np.ascontiguousarray(np.asarray(inputs["L0_ffn_w_down"], np.int32).T),
        "Wq1": np.ascontiguousarray(np.asarray(inputs["L1_attn_Wq"], np.float32)),
        "Wk1": np.ascontiguousarray(np.asarray(inputs["L1_attn_Wk"], np.float32)),
        "Wv1": np.ascontiguousarray(np.asarray(inputs["L1_attn_Wv"], np.float32)),
        "Wo1": np.ascontiguousarray(np.asarray(inputs["L1_attn_Wo"], np.float32)),
        "bo1": np.ascontiguousarray(np.asarray(inputs["L1_attn_bo"], np.float32).reshape(1, D)),
        "bo1fm": np.ascontiguousarray(np.asarray(inputs["L1_attn_bo"], np.float32).reshape(DC, 128).T.copy()),
        "wg1": np.ascontiguousarray(np.asarray(inputs["L1_ffn_w_gate"], np.float32)),
        "wu1": np.ascontiguousarray(np.asarray(inputs["L1_ffn_w_up"], np.float32)),
        "wd1": np.ascontiguousarray(np.asarray(inputs["L1_ffn_w_down"], np.float32)),
    }
    maps = []
    for b in range(NCORES):
        m = dict(base)
        m["xT"] = np.ascontiguousarray(x[b].T)
        maps.append(m)
    return maps


class Ctx:
    pass


def _hbm(nc, C):
    C.ap = {}

    def din(name, shape, dtype=F32):
        C.ap[name] = nc.dram_tensor(name, list(shape), dtype, kind="ExternalInput").ap()

    def dsc(name, shape, dtype=F32):
        C.ap[name] = nc.dram_tensor(name, list(shape), dtype, kind="Internal").ap()

    din("xT", (D, T))
    for nm, sh in (("ident", (128, 128)), ("ones", (128, 128)), ("dup64", (64, 128)),
                   ("swapm", (128, 128)), ("dupswap", (128, 128)),
                   ("betasc", (64, H)), ("Mdup0", (128, H * 64)), ("Zdup0", (128, H)),
                   ("bo1", (1, D)), ("bo1fm", (128, DC)), ("sel8", (64, 64)),
                   ("sel8t", (8, 512))):
        din(nm, sh)
    din("mask01", (512, 512), mybir.dt.bfloat16)
    for w in ("Wq0", "Wk0", "Wv0", "Wo0", "Wq1", "Wk1", "Wv1", "Wo1"):
        din(w, (D, D))
    din("wgT0", (D, HID), I32)
    din("wuT0", (D, HID), I32)
    din("wdT0", (HID, D), I32)
    din("wg1", (HID, D))
    din("wu1", (HID, D))
    din("wd1", (D, HID))
    C.out_fm = nc.dram_tensor("out_fm", [D, T], F32, kind="ExternalOutput").ap()
    for nm, sh in (("wgT0_f", (D, HID)), ("wuT0_f", (D, HID)), ("wdT0_f", (HID, D)),
                   ("g_sp", (D, T)), ("x1_sp", (D, T)), ("xn1_sp", (D, T)), ("att_sp", (D, T)),
                   ("up_sp", (HID, T)), ("gate_sp", (HID, T)), ("t_sp", (HID, T)),
                   ("upq_sp", (HID, T)), ("u_sp", (HID, T)),
                   ("wgT1_f", (D, HID)), ("wuT1_f", (D, HID)), ("wdT1_f", (HID, D))):
        dsc(nm, sh)
    space = maybe_share_collective_output_space("AllReduce", RG8)
    for i in range(4):
        dsc(f"cc_in{i}", (1, 8))
        C.ap[f"cc_out{i}"] = nc.dram_tensor(f"cc_out{i}", [1, 8], F32, kind="Internal",
                                            addr_space=space).ap()
    C.dbg = {}
    if DEBUG:
        for name, sh in _DBG_SHAPES.items():
            C.dbg[name] = nc.dram_tensor("dbg_" + name, list(sh), F32, kind="ExternalOutput").ap()


def _dbg(S, C, name, src_ap, reads):
    if DEBUG and name in C.dbg:
        S.dma("POOL", lambda h: h.dma_start(out=C.dbg[name][:, :], in_=src_ap),
              reads=list(reads), writes=[], slot="dbgw")


def _acc(S, bank_ap, pairs, reads, writes):
    def emit(h):
        last = None
        n = len(pairs)
        for i, (lhsT, rhs) in enumerate(pairs):
            last = h.matmul(bank_ap, lhsT, rhs, start=(i == 0), stop=(i == n - 1))
        return last
    return S.run("PE", emit, reads=reads, writes=writes)


def build_program():
    nc = bass.Bass("TRN2", target_bir_lowering=False)
    C = Ctx()
    _hbm(nc, C)

    with ExitStack() as ctx:
        E = ctx.enter_context
        C.consts = E(nc.sbuf_tensor("consts", [128, 640], F32))
        C.selc = E(nc.sbuf_tensor("selc", [128, 64 + 512], F32))
        C.mask = E(nc.sbuf_tensor("mask", [128, 4 * 512], mybir.dt.bfloat16))
        C.gates = E(nc.sbuf_tensor("gates", [64, 3 * H], F32))
        C.Zd = E(nc.sbuf_tensor("Zd", [128, H], F32))
        C.Md = E(nc.sbuf_tensor("Md", [128, H * 64], F32))
        C.row = E(nc.sbuf_tensor("row", [1, 2048], F32))
        C.srow = E(nc.sbuf_tensor("srow", [128, 16], F32))
        C.cb = E(nc.sbuf_tensor("cb", [128, 8], F32))
        C.xtile = E(nc.sbuf_tensor("xtile", [128, DC * 512], F32))
        C.xn = E(nc.sbuf_tensor("xn", [128, DC * 512], F32))
        C.stg = E(nc.sbuf_tensor("stg", [128, DC * 512], F32))
        C.qg = E(nc.sbuf_tensor("qg", [128, H * 256], F32))
        C.kg = E(nc.sbuf_tensor("kg", [128, H * 256], F32))
        C.vg = E(nc.sbuf_tensor("vg", [128, H * 256], F32))
        C.qsw = E(nc.sbuf_tensor("qsw", [128, H * 256], F32))
        C.upool = E(nc.sbuf_tensor("upool", [128, 4 * 512], F32))
        C.tokp = E(nc.sbuf_tensor("tokp", [128, 8 * 128], F32))
        C.scr = E(nc.sbuf_tensor("scr", [128, 4 * 512], F32))
        C.pw = [E(nc.sbuf_tensor(f"pw{i}", [128, DC * 128], F32)) for i in range(2)]
        C.fw = [E(nc.sbuf_tensor(f"fw{i}", [128, HID], F32)) for i in range(2)]
        C.fwi = E(nc.sbuf_tensor("fwi", [128, 2048], I32))
        C.hdb = E(nc.sbuf_tensor("hdb", [128, 2 * 512], F32))
        C.ysb = E(nc.sbuf_tensor("ysb", [128, 2 * 512], F32))
        C.ps = [E(nc.psum_tensor(f"ps{i}", [128, 512], F32)) for i in range(8)]

        block = E(nc.Block())
        S = Sched(nc, block, ctx)

        C.IDENT = C.consts[:, 0:128]
        C.ONES = C.consts[:, 128:256]
        C.DUP64 = C.consts[0:64, 256:384]
        C.SWAP = C.consts[:, 384:512]
        C.DUPSW = C.consts[:, 512:640]
        C.SEL8 = C.selc[0:64, 0:64]
        C.SEL8T = C.selc[0:8, 64:576]
        C.scrA = C.scr[:, 0:512]
        C.scrB = C.scr[:, 512:1024]
        C.scrC = C.scr[:, 1024:1536]
        C.scrD = C.scr[:, 1536:2048]

        _init(nc, S, C)
        _l0_weight_convert(nc, S, C)
        _phase_a(nc, S, C)

        if B_STAGE >= 1:
            _phase_b(nc, S, C)
        else:
            for dc in range(DC):
                for tt in range(NSEG):
                    S.dma("POOL", lambda h, dc=dc, tt=tt: h.dma_start(
                        out=C.out_fm[128 * dc:128 * dc + 128, 512 * tt:512 * tt + 512],
                        in_=C.ap["x1_sp"][128 * dc:128 * dc + 128, 512 * tt:512 * tt + 512]),
                        reads=[f"x1sp_{dc}_{tt}"], writes=[], slot="outw", serialize=False)
        sem, cnt = S._slot("outw")
        S.wait_tokens("POOL", [("D", "outw", cnt[0])])

        if DEBUG:
            for nm in _DBG_SHAPES:
                S.dma("POOL", lambda h, nm=nm: h.dma_start(
                    out=C.dbg[nm][:, :],
                    in_=(C.srow[:, :] if nm == "srow" else C.ap[nm][:, :])),
                    reads=[], writes=[], slot="dbgw")
            sem2, cnt2 = S._slot("dbgw")
            S.wait_tokens("POOL", [("D", "dbgw", cnt2[0])])

    return nc


def _init(nc, S, C):
    ap = C.ap
    S.dma("SP", lambda h: h.dma_start(out=C.consts[:, 0:128], in_=ap["ident"][:, :]), writes=["ident"], slot="init")
    S.dma("SP", lambda h: h.dma_start(out=C.consts[:, 128:256], in_=ap["ones"][:, :]), writes=["ones"], slot="init")
    S.dma("SP", lambda h: h.dma_start(out=C.consts[0:64, 256:384], in_=ap["dup64"][:, :]), writes=["dup64"], slot="init")
    S.dma("SP", lambda h: h.dma_start(out=C.consts[:, 384:512], in_=ap["swapm"][:, :]), writes=["swapm"], slot="init")
    S.dma("SP", lambda h: h.dma_start(out=C.consts[:, 512:640], in_=ap["dupswap"][:, :]), writes=["dupsw"], slot="init")
    S.dma("SP", lambda h: h.dma_start(out=C.selc[0:64, 0:64], in_=ap["sel8"][:, :]), writes=["sel8"], slot="init")
    S.dma("SP", lambda h: h.dma_start(out=C.selc[0:8, 64:576], in_=ap["sel8t"][:, :]), writes=["sel8t"], slot="init")
    for jc in range(4):
        S.dma("SP", lambda h, jc=jc: h.dma_start(
            out=C.mask[:, 512 * jc:512 * jc + 512], in_=ap["mask01"][128 * jc:128 * jc + 128, :]),
            writes=["mask"], slot="init")
    S.dma("SP", lambda h: h.dma_start(out=C.gates[:, 0:H], in_=ap["betasc"][:, :]), writes=["betas"], slot="init")
    S.dma("SP", lambda h: h.dma_start(out=C.Zd[:, :], in_=ap["Zdup0"][:, :]), writes=["Zd"], slot="init")
    S.dma("SP", lambda h: h.dma_start(out=C.Md[:, :], in_=ap["Mdup0"][:, :]), writes=["Md"], slot="init")
    S.run("DVE", lambda h: h.memset(C.cb[:, 2:3], 1e-5), reads=[], writes=["cb_eps"])
    S.run("DVE", lambda h: h.memset(C.srow[0:1, 8:16], 0.0), reads=[], writes=["srow", "srow_init"])
    S.run("ACT", lambda h: h.activation(out=C.gates[:, H:2 * H], in_=C.gates[:, 0:H], func=AF.Sigmoid),
          reads=["betas"], writes=["gate"])
    S.run("DVE", lambda h: h.tensor_scalar(C.gates[:, 2 * H:3 * H], C.gates[:, H:2 * H], -1.0, 1.0, ALU.mult, ALU.add),
          reads=["gate"], writes=["gate1m"])


def _l0_weight_convert(nc, S, C):
    for src, dst, nchunk, width in (("wgT0", "wgT0_f", DC, HID),
                                    ("wuT0", "wuT0_f", DC, HID),
                                    ("wdT0", "wdT0_f", HC, D)):
        for c in range(nchunk):
            ib = c % 2
            for o in range(0, width, 2048):
                w2 = min(2048, width - o)
                S.dma("SP", lambda h, src=src, c=c, o=o, w2=w2: h.dma_start(
                    out=C.fwi[:, 0:w2], in_=C.ap[src][128 * c:128 * c + 128, o:o + w2]),
                    reads=["fwi"], writes=["fwi"], slot="cvt_i")
                S.run("DVE", lambda h, o=o, w2=w2, ib=ib: h.tensor_copy(
                    C.fw[ib][:, o:o + w2], C.fwi[:, 0:w2]),
                    reads=["fwi", f"fw{ib}"], writes=[f"fw{ib}"])
            S.dma("POOL", lambda h, dst=dst, c=c, width=width, ib=ib: h.dma_start(
                out=C.ap[dst][128 * c:128 * c + 128, :], in_=C.fw[ib][:, 0:width]),
                reads=[f"fw{ib}"], writes=[f"{dst}_c{c}"], slot=f"cvto{ib}")


XN = [f"xn_{i}" for i in range(DC)]
XT = [f"xt_{i}" for i in range(DC)]
STG = [f"stg_{i}" for i in range(DC)]


def _load_xtile(S, C, tt):
    for dc in range(DC):
        S.dma("SP", lambda h, tt=tt, dc=dc: h.dma_start(
            out=C.xtile[:, 512 * dc:512 * dc + 512],
            in_=C.ap["xT"][128 * dc:128 * dc + 128, 512 * tt:512 * tt + 512]),
            writes=[f"xt_{dc}"], slot="xin")


def _ln_tile(S, C):
    ONES = C.ONES
    _acc(S, C.ps[0][0:1, :],
         [(ONES[:, 0:1], C.xtile[:, 512 * dc:512 * dc + 512]) for dc in range(DC)],
         reads=XT + ["ones"], writes=["ps0"])
    S.run("ACT", lambda h: h.activation(out=C.xn[:, :], in_=C.xtile[:, :], func=AF.Square),
          reads=XT, writes=XN)
    _acc(S, C.ps[1][0:1, :],
         [(ONES[:, 0:1], C.xn[:, 512 * dc:512 * dc + 512]) for dc in range(DC)],
         reads=XN + ["ones"], writes=["ps1"])
    mrow = C.row[0:1, 0:512]
    rrow = C.row[0:1, 512:1024]
    S.run("DVE", lambda h: h.tensor_scalar(mrow, C.ps[0][0:1, :], 1.0 / 1024.0, None, ALU.mult, ALU.bypass),
          reads=["ps0"], writes=["rowM"])
    S.run("DVE", lambda h: h.tensor_scalar(rrow, C.ps[1][0:1, :], 1.0 / 1024.0, None, ALU.mult, ALU.bypass),
          reads=["ps1"], writes=["rowR"])
    S.run("DVE", lambda h: h.tensor_tensor(C.row[0:1, 1024:1536], mrow, mrow, ALU.mult),
          reads=["rowM"], writes=["rowC"])
    S.run("DVE", lambda h: h.tensor_tensor(rrow, rrow, C.row[0:1, 1024:1536], ALU.subtract),
          reads=["rowR", "rowC"], writes=["rowR"])
    S.run("ACT", lambda h: h.activation(out=rrow, in_=rrow, func=AF.Sqrt, bias=C.cb[0:1, 2:3]),
          reads=["rowR", "cb_eps"], writes=["rowR"])
    S.run("DVE", lambda h: h.reciprocal(rrow, rrow), reads=["rowR"], writes=["rowR"])
    S.run("PE", lambda h: h.matmul(C.ps[0][:, :], ONES[0:1, 0:128], mrow, start=True, stop=True),
          reads=["rowM", "ones", "ps0"], writes=["ps0"])
    S.run("PE", lambda h: h.matmul(C.ps[1][:, :], ONES[0:1, 0:128], rrow, start=True, stop=True),
          reads=["rowR", "ones", "ps1"], writes=["ps1"])
    S.run("ACT", lambda h: h.activation(out=C.scrA, in_=C.ps[0][:, :], func=AF.Copy),
          reads=["ps0"], writes=["scrA"])
    S.run("ACT", lambda h: h.activation(out=C.scrB, in_=C.ps[1][:, :], func=AF.Copy),
          reads=["ps1"], writes=["scrB"])
    for dc in range(DC):
        S.run("DVE", lambda h, dc=dc: h.tensor_tensor(
            C.xn[:, 512 * dc:512 * dc + 512], C.xtile[:, 512 * dc:512 * dc + 512], C.scrA, ALU.subtract),
            reads=[f"xt_{dc}", "scrA"], writes=[f"xn_{dc}"])
        S.run("DVE", lambda h, dc=dc: h.tensor_tensor(
            C.xn[:, 512 * dc:512 * dc + 512], C.xn[:, 512 * dc:512 * dc + 512], C.scrB, ALU.mult),
            reads=[f"xn_{dc}", "scrB"], writes=[f"xn_{dc}"])


def _wcol_load(S, C, mat, oc, ws, rkeys=()):
    dst = C.pw[ws][:].rearrange("p (dc c) -> p dc c", dc=DC, c=128)
    src = C.ap[mat][:, 128 * oc:128 * oc + 128].rearrange("(dc p) c -> p dc c", dc=DC, p=128)
    return S.dma("SP", lambda h: h.dma_start(out=dst, in_=src),
                 reads=list(rkeys), writes=[f"pw{ws}"], slot=f"pw{ws}")


def _phase_a(nc, S, C):
    if BUILD_STAGE < 1:
        return
    for tt in range(NSEG):
        _load_xtile(S, C, tt)
        _ln_tile(S, C)
        for hc in range(HC):
            S.dma("SP", lambda h, hc=hc: h.dma_start(
                out=C.pw[0][:].rearrange("p (dc c) -> p dc c", dc=DC, c=128),
                in_=C.ap["wgT0_f"][:, 128 * hc:128 * hc + 128].rearrange("(dc p) c -> p dc c", dc=DC, p=128)),
                reads=[f"wgT0_f_c{i}" for i in range(DC)], writes=["pw0"], slot="pw0")
            _acc(S, C.ps[6][:, :],
                 [(C.pw[0][:, 128 * dc:128 * dc + 128], C.xn[:, 512 * dc:512 * dc + 512]) for dc in range(DC)],
                 reads=["pw0"] + XN, writes=["ps6"])
            S.run("ACT", lambda h: h.activation(out=C.scrC, in_=C.ps[6][:, :], func=AF.Silu, scale=C127),
                  reads=["ps6"], writes=["scrC"])
            S.dma("SP", lambda h, hc=hc: h.dma_start(
                out=C.pw[1][:].rearrange("p (dc c) -> p dc c", dc=DC, c=128),
                in_=C.ap["wuT0_f"][:, 128 * hc:128 * hc + 128].rearrange("(dc p) c -> p dc c", dc=DC, p=128)),
                reads=[f"wuT0_f_c{i}" for i in range(DC)], writes=["pw1"], slot="pw1")
            _acc(S, C.ps[7][:, :],
                 [(C.pw[1][:, 128 * dc:128 * dc + 128], C.xn[:, 512 * dc:512 * dc + 512]) for dc in range(DC)],
                 reads=["pw1"] + XN, writes=["ps7"])
            hsl = C.hdb[:, 512 * (hc % 2):512 * (hc % 2) + 512]
            S.run("DVE", lambda h, hsl=hsl: h.scalar_tensor_tensor(
                hsl, C.ps[7][:, :], C127, C.scrC, ALU.mult, ALU.mult),
                reads=["ps7", "scrC"], writes=[f"hdb{hc % 2}"])
            S.dma("SP", lambda h, hc=hc: h.dma_start(
                out=C.fw[hc % 2][:, 0:D], in_=C.ap["wdT0_f"][128 * hc:128 * hc + 128, :]),
                reads=[f"wdT0_f_c{hc}", f"fw{hc % 2}"], writes=[f"fw{hc % 2}"], slot=f"fwd{hc % 2}")
            def mm_down(h, hc=hc, hsl=hsl):
                last = None
                for do in range(6):
                    last = h.matmul(C.ps[do][:, :], C.fw[hc % 2][:, 128 * do:128 * do + 128], hsl,
                                    start=(hc == 0), stop=(hc == HC - 1))
                return last
            S.run("PE", mm_down, reads=[f"fw{hc % 2}", f"hdb{hc % 2}"] + [f"ps{i}" for i in range(6)],
                  writes=[f"ps{i}" for i in range(6)])
            for j, do in enumerate((6, 7)):
                pb = C.ps[6 + j]
                S.run("PE", lambda h, do=do, pb=pb, hc=hc, hsl=hsl: h.matmul(
                    pb[:, :], C.fw[hc % 2][:, 128 * do:128 * do + 128], hsl, start=True, stop=True),
                    reads=[f"fw{hc % 2}", f"hdb{hc % 2}", f"ps{6 + j}"], writes=[f"ps{6 + j}"])
                ysl = C.ysb[:, 512 * j:512 * j + 512]
                if hc == 0:
                    S.run("DVE", lambda h, ysl=ysl, pb=pb: h.tensor_copy(ysl, pb[:, :]),
                          reads=[f"ps{6 + j}"], writes=["ysb"])
                else:
                    S.run("DVE", lambda h, ysl=ysl, pb=pb: h.tensor_tensor(ysl, ysl, pb[:, :], ALU.add),
                          reads=[f"ps{6 + j}", "ysb"], writes=["ysb"])
        for do in range(6):
            S.run("ACT", lambda h, do=do: h.activation(
                out=C.stg[:, 512 * do:512 * do + 512], in_=C.ps[do][:, :], func=AF.Silu, scale=C127),
                reads=[f"ps{do}"], writes=[f"stg_{do}"])
        for j in range(2):
            S.run("ACT", lambda h, j=j: h.activation(
                out=C.stg[:, 512 * (6 + j):512 * (6 + j) + 512], in_=C.ysb[:, 512 * j:512 * j + 512],
                func=AF.Silu, scale=C127),
                reads=["ysb"], writes=[f"stg_{6 + j}"])
        for do in range(DC):
            S.dma("POOL", lambda h, do=do, tt=tt: h.dma_start(
                out=C.ap["g_sp"][128 * do:128 * do + 128, 512 * tt:512 * tt + 512],
                in_=C.stg[:, 512 * do:512 * do + 512]),
                reads=[f"stg_{do}"], writes=[f"gsp_{do}_{tt}"], slot="gsp")

    if BUILD_STAGE < 2:
        return
    for sg in range(NSEG):
        _load_xtile(S, C, sg)
        _ln_tile(S, C)

        for mat, dst, nm in (("Wq0", C.qg, "qg"), ("Wk0", C.kg, "kg"), ("Wv0", C.vg, "vg")):
            for oc in range(DC):
                ws = oc % 2
                _wcol_load(S, C, mat, oc, ws)
                _acc(S, C.ps[oc % 2][:, :],
                     [(C.pw[ws][:, 128 * dc:128 * dc + 128], C.xn[:, 512 * dc:512 * dc + 512]) for dc in range(DC)],
                     reads=[f"pw{ws}"] + XN, writes=[f"ps{oc % 2}"])
                S.run("ACT", lambda h, oc=oc: h.activation(
                    out=C.stg[:, 512 * oc:512 * oc + 512], in_=C.ps[oc % 2][:, :], func=AF.Copy),
                    reads=[f"ps{oc % 2}"], writes=[f"stg_{oc}"])
            def gat(h, dst=dst):
                last = None
                for gp in range(2):
                    src = C.stg[64 * gp:64 * gp + 64, :].rearrange(
                        "p (g2 hh rr) -> p hh g2 rr", g2=8, hh=16, rr=32)
                    dd = dst[64 * gp:64 * gp + 64, :].rearrange(
                        "p (hh g2 rr) -> p hh g2 rr", hh=16, g2=8, rr=32)
                    last = h.tensor_copy(dd, src)
                return last
            S.run("DVE", gat, reads=STG, writes=[nm])

        for c8 in range(8):
            S.run("PE", lambda h, c8=c8: h.matmul(
                C.ps[2 + (c8 % 2)][:, :], C.SWAP, C.qg[:, 512 * c8:512 * c8 + 512], start=True, stop=True),
                reads=["qg", "swapm", f"ps{2 + (c8 % 2)}"], writes=[f"ps{2 + (c8 % 2)}"])
            S.run("ACT", lambda h, c8=c8: h.activation(
                out=C.qsw[:, 512 * c8:512 * c8 + 512], in_=C.ps[2 + (c8 % 2)][:, :], func=AF.Copy),
                reads=[f"ps{2 + (c8 % 2)}"], writes=["qsw"])

        if DEBUG:
            _dbg(S, C, f"qg_{sg}", C.qg[:, :], ["qg"])
            _dbg(S, C, f"kg_{sg}", C.kg[:, :], ["kg"])
            _dbg(S, C, f"vg_{sg}", C.vg[:, :], ["vg"])
            _dbg(S, C, f"xn_{sg}", C.xn[:, :], XN)

        if ATT_SUB >= 1:
            for head in range(H):
                _l0_head(nc, S, C, sg, head)

        if DEBUG:
            _dbg(S, C, f"sfm_{sg}", C.stg[:, :], [f"attw_{hh}" for hh in range(H)])

        if ATT_SUB < 7:
            continue
        attr = [f"attw_{hh}" for hh in range(H)]
        for oc in range(DC):
            ws = oc % 2
            _wcol_load(S, C, "Wo0", oc, ws)
            _acc(S, C.ps[oc % 2][:, :],
                 [(C.pw[ws][:, 128 * dc:128 * dc + 128], C.stg[:, 512 * dc:512 * dc + 512]) for dc in range(DC)],
                 reads=[f"pw{ws}"] + attr + STG, writes=[f"ps{oc % 2}"])
            S.dma("SP", lambda h, oc=oc, sg=sg: h.dma_start(
                out=C.scrC, in_=C.ap["g_sp"][128 * oc:128 * oc + 128, 512 * sg:512 * sg + 512]),
                reads=[f"gsp_{oc}_{sg}", "scrC"], writes=["scrC"], slot="gbk")
            S.run("DVE", lambda h, oc=oc: h.tensor_tensor(
                C.scrD, C.ps[oc % 2][:, :], C.scrC, ALU.mult),
                reads=[f"ps{oc % 2}", "scrC", "scrD"], writes=["scrD"])
            S.run("DVE", lambda h, oc=oc: h.tensor_tensor(
                C.scrD, C.scrD, C.xtile[:, 512 * oc:512 * oc + 512], ALU.add),
                reads=["scrD", f"xt_{oc}"], writes=["scrD"])
            S.dma("POOL", lambda h, oc=oc, sg=sg: h.dma_start(
                out=C.ap["x1_sp"][128 * oc:128 * oc + 128, 512 * sg:512 * sg + 512], in_=C.scrD),
                reads=["scrD"], writes=[f"x1sp_{oc}_{sg}"], slot="x1o")


def _l0_head(nc, S, C, sg, head):
    sub = ATT_SUB
    hof = 256 * head
    qh = C.qg[:, hof:hof + 256]
    qsh = C.qsw[:, hof:hof + 256]
    kh = C.kg[:, hof:hof + 256]
    vh = C.vg[:, hof:hof + 256]
    ONES = C.ONES
    IDENT = C.IDENT

    S.run("ACT", lambda h: h.activation(out=C.scrA[:, 0:256], in_=qh, func=AF.Square),
          reads=["qg", "scrA"], writes=["scrA"])
    def mm_qq(h):
        h.matmul(C.ps[4][0:1, 0:256], ONES[0:64, 0:1], C.scrA[0:64, 0:256], start=True, stop=True)
        return h.matmul(C.ps[5][0:1, 0:256], ONES[64:128, 64:65], C.scrA[64:128, 0:256],
                        start=True, stop=True)
    S.run("PE", mm_qq, reads=["scrA", "ones", "ps4", "ps5"], writes=["ps4", "ps5"])
    S.run("DVE", lambda h: h.reduce_max(C.srow[0:1, 0:1], C.ps[4][0:1, 0:256], axis=AX.X),
          reads=["ps4"], writes=["srow_qq"])
    S.run("DVE", lambda h: h.reduce_max(C.srow[0:1, 6:7], C.ps[5][0:1, 0:256], axis=AX.X),
          reads=["ps5"], writes=["srow_qq2"])
    S.run("ACT", lambda h: h.activation(out=C.scrA[:, 0:256], in_=kh, func=AF.Square),
          reads=["kg", "scrA", "ps4", "ps5"], writes=["scrA"])
    def mm_kk(h):
        h.matmul(C.ps[4][0:1, 0:256], ONES[0:64, 0:1], C.scrA[0:64, 0:256], start=True, stop=True)
        return h.matmul(C.ps[5][0:1, 0:256], ONES[64:128, 64:65], C.scrA[64:128, 0:256],
                        start=True, stop=True)
    S.run("PE", mm_kk, reads=["scrA", "ones", "ps4", "ps5", "srow_qq", "srow_qq2"], writes=["ps4", "ps5"])
    S.run("DVE", lambda h: h.reduce_max(C.srow[0:1, 1:2], C.ps[4][0:1, 0:256], axis=AX.X),
          reads=["ps4"], writes=["srow_kk"])
    S.run("DVE", lambda h: h.reduce_max(C.srow[0:1, 7:8], C.ps[5][0:1, 0:256], axis=AX.X),
          reads=["ps5"], writes=["srow_kk2"])
    S.run("DVE", lambda h: h.tensor_tensor(C.srow[0:1, 0:1], C.srow[0:1, 0:1], C.srow[0:1, 6:7], ALU.max),
          reads=["srow_qq", "srow_qq2"], writes=["srow_qq"])
    S.run("DVE", lambda h: h.tensor_tensor(C.srow[0:1, 1:2], C.srow[0:1, 1:2], C.srow[0:1, 7:8], ALU.max),
          reads=["srow_kk", "srow_kk2"], writes=["srow_kk"])
    S.run("DVE", lambda h: h.tensor_tensor(C.srow[0:1, 2:3], C.srow[0:1, 0:1], C.srow[0:1, 1:2], ALU.mult),
          reads=["srow_qq", "srow_kk"], writes=["srow_c"])
    S.run("ACT", lambda h: h.activation(out=C.srow[0:1, 3:4], in_=C.srow[0:1, 2:3], func=AF.Sqrt,
                                        scale=float(0.125 * 0.125)),
          reads=["srow_c"], writes=["srow_c"])
    S.run("DVE", lambda h: h.tensor_scalar(C.srow[0:1, 4:5], C.srow[0:1, 3:4], -1.0, None, ALU.mult, ALU.bypass),
          reads=["srow_c"], writes=["srow_cn"])
    S.run("PE", lambda h: h.matmul(C.ps[4][:, 0:1], ONES[0:1, 0:128], C.srow[0:1, 4:5], start=True, stop=True),
          reads=["srow_cn", "ones", "ps4", "srow_kk"], writes=["ps4"])
    S.run("ACT", lambda h: h.activation(out=C.cb[:, 0:1], in_=C.ps[4][:, 0:1], func=AF.Copy),
          reads=["ps4"], writes=["cb"])

    if sub < 2:
        return
    for jc in range(4):
        gpj, blk = jc // 2, jc % 2
        kslice = kh[64 * gpj:64 * gpj + 64, 128 * blk:128 * blk + 128]
        def mm_st(h, kslice=kslice, gpj=gpj):
            q_same = (qh if gpj == 0 else qsh)[64 * gpj:64 * gpj + 64, :]
            q_other = (qsh if gpj == 0 else qh)[64 * gpj:64 * gpj + 64, :]
            h.matmul(C.ps[2][:, 0:256], kslice, q_same, start=True, stop=True)
            return h.matmul(C.ps[3][:, 0:256], kslice, q_other, start=True, stop=True)
        S.run("PE", mm_st, reads=["qg", "qsw", "kg", "ps2", "ps3"], writes=["ps2", "ps3"])
        S.run("ACT", lambda h, jc=jc: h.activation(
            out=C.upool[:, 512 * jc:512 * jc + 256], in_=C.ps[2][:, 0:256], func=AF.Exp,
            bias=C.cb[:, 0:1], scale=0.125),
            reads=["ps2", "cb"], writes=[f"U{jc}"])
        S.run("ACT", lambda h, jc=jc: h.activation(
            out=C.upool[:, 512 * jc + 256:512 * jc + 512], in_=C.ps[3][:, 0:256], func=AF.Exp,
            bias=C.cb[:, 0:1], scale=0.125),
            reads=["ps3", "cb"], writes=[f"U{jc}"])
        S.run("DVE", lambda h, jc=jc: h.tensor_tensor(
            C.upool[:, 512 * jc:512 * jc + 512], C.upool[:, 512 * jc:512 * jc + 512],
            C.mask[:, 512 * jc:512 * jc + 512], ALU.mult),
            reads=[f"U{jc}", "mask"], writes=[f"U{jc}"])
    Ur = [f"U{jc}" for jc in range(4)]

    if sub < 3:
        return
    for jc in range(4):
        gpj, blk = jc // 2, jc % 2
        S.run("PE", lambda h, gpj=gpj, blk=blk: h.transpose(
            C.ps[4][0:128, 0:64], vh[64 * gpj:64 * gpj + 64, 128 * blk:128 * blk + 128],
            IDENT[64 * gpj:64 * gpj + 64, 64 * gpj:64 * gpj + 64]),
            reads=["vg", "ident", "ps4", "cb"], writes=["ps4"])
        S.run("ACT", lambda h, jc=jc: h.activation(out=C.tokp[:, 128 * jc:128 * jc + 64],
                                                   in_=C.ps[4][0:128, 0:64], func=AF.Copy),
              reads=["ps4"], writes=[f"vtok{jc}"])
    S.run("DVE", lambda h: h.tensor_scalar(C.scrA[:, 0:256], kh, 0.0, None, ALU.min, ALU.bypass),
          reads=["kg", "scrA"], writes=["scrA"])
    S.run("ACT", lambda h: h.activation(out=C.scrA[:, 0:256], in_=C.scrA[:, 0:256], func=AF.Exp),
          reads=["scrA"], writes=["scrA"])
    S.run("DVE", lambda h: h.tensor_scalar(kh, kh, 0.0, None, ALU.max, ALU.bypass),
          reads=["kg"], writes=["kg"])
    S.run("DVE", lambda h: h.tensor_tensor(kh, kh, C.scrA[:, 0:256], ALU.add),
          reads=["kg", "scrA"], writes=["kg", "sk_h"])
    for jc in range(4):
        gpj, blk = jc // 2, jc % 2
        S.run("PE", lambda h, gpj=gpj, blk=blk: h.transpose(
            C.ps[4][0:128, 0:64], kh[64 * gpj:64 * gpj + 64, 128 * blk:128 * blk + 128],
            IDENT[64 * gpj:64 * gpj + 64, 64 * gpj:64 * gpj + 64]),
            reads=["sk_h", "ident", "ps4"], writes=["ps4"])
        S.run("ACT", lambda h, jc=jc: h.activation(out=C.tokp[:, 512 + 128 * jc:512 + 128 * jc + 64],
                                                   in_=C.ps[4][0:128, 0:64], func=AF.Copy),
              reads=["ps4"], writes=[f"sktok{jc}"])
    S.run("DVE", lambda h: h.tensor_scalar(C.scrA[:, 0:256], qh, 0.0, None, ALU.min, ALU.bypass),
          reads=["qg", "scrA"], writes=["scrA"])
    S.run("ACT", lambda h: h.activation(out=C.scrA[:, 0:256], in_=C.scrA[:, 0:256], func=AF.Exp),
          reads=["scrA"], writes=["scrA"])
    S.run("DVE", lambda h: h.tensor_scalar(qh, qh, 0.0, None, ALU.max, ALU.bypass),
          reads=["qg"], writes=["qg"])
    S.run("DVE", lambda h: h.tensor_tensor(qh, qh, C.scrA[:, 0:256], ALU.add),
          reads=["qg", "scrA"], writes=["qg", "sq_h"])

    if sub < 4:
        return
    _acc(S, C.ps[5][0:1, :],
         [(ONES[:, 0:1], C.upool[:, 512 * jc:512 * jc + 512]) for jc in range(4)],
         reads=Ur + ["ones", "ps5"], writes=["ps5"])
    _acc(S, C.ps[6][0:64, :],
         [(C.tokp[:, 128 * jc:128 * jc + 64], C.upool[:, 512 * jc:512 * jc + 512]) for jc in range(4)],
         reads=Ur + [f"vtok{jc}" for jc in range(4)] + ["ps6"], writes=["ps6"])
    def mm_num(h):
        h.matmul(C.ps[7][0:64, 0:256], C.Md[0:64, 64 * head:64 * head + 64], qh[0:64, :],
                 start=True, stop=True)
        return h.matmul(C.ps[3][0:64, 0:256], C.Md[64:128, 64 * head:64 * head + 64], qh[64:128, :],
                        start=True, stop=True)
    S.run("PE", mm_num, reads=["sq_h", "Md", "ps7", "ps3"], writes=["ps7", "ps3"])
    S.run("DVE", lambda h: h.reciprocal(C.row[0:1, 1024:1536], C.ps[5][0:1, :]),
          reads=["ps5"], writes=["rowC"])
    def mm_den(h):
        h.matmul(C.ps[5][0:1, 0:256], C.Zd[0:64, head:head + 1], qh[0:64, :], start=True, stop=True)
        return h.matmul(C.ps[4][0:1, 0:256], C.Zd[64:128, head:head + 1], qh[64:128, :],
                        start=True, stop=True)
    S.run("PE", mm_den, reads=["sq_h", "Zd", "ps5", "ps4"], writes=["ps5", "ps4"])
    S.run("DVE", lambda h: h.reciprocal(C.row[0:1, 1536:1792], C.ps[5][0:1, 0:256]),
          reads=["ps5"], writes=["rowD"])
    S.run("DVE", lambda h: h.reciprocal(C.row[0:1, 1792:2048], C.ps[4][0:1, 0:256]),
          reads=["ps4"], writes=["rowD"])
    S.run("PE", lambda h: h.matmul(C.ps[2][0:64, :], ONES[0:1, 0:64], C.row[0:1, 1024:1536],
                                   start=True, stop=True),
          reads=["rowC", "ones", "ps2"], writes=["ps2"])
    S.run("PE", lambda h: h.matmul(C.ps[5][0:64, :], ONES[0:1, 0:64], C.row[0:1, 1536:2048],
                                   start=True, stop=True),
          reads=["rowD", "ones", "ps5"], writes=["ps5"])
    S.run("ACT", lambda h: h.activation(out=C.scrB[0:64, :], in_=C.ps[2][0:64, :], func=AF.Copy),
          reads=["ps2", "scrB"], writes=["scrB"])
    S.run("ACT", lambda h: h.activation(out=C.scrC[0:64, :], in_=C.ps[5][0:64, :], func=AF.Copy),
          reads=["ps5", "scrC"], writes=["scrC"])
    if sub < 5:
        return
    gcol = C.gates[:, H + head:H + head + 1]
    g1col = C.gates[:, 2 * H + head:2 * H + head + 1]
    S.run("DVE", lambda h: h.tensor_tensor(C.scrD[0:64, 0:256], C.ps[7][0:64, 0:256],
                                           C.scrC[0:64, 0:256], ALU.mult),
          reads=["ps7", "scrC", "scrD"], writes=["scrD"])
    S.run("DVE", lambda h: h.tensor_tensor(C.scrD[0:64, 256:512], C.ps[3][0:64, 0:256],
                                           C.scrC[0:64, 256:512], ALU.mult),
          reads=["ps3", "scrC", "scrD"], writes=["scrD"])
    S.run("DVE", lambda h: h.tensor_tensor(C.scrA[0:64, 0:512], C.ps[6][0:64, :], C.scrB[0:64, :], ALU.mult),
          reads=["ps6", "scrB", "scrA"], writes=["scrA"])
    S.run("DVE", lambda h: h.tensor_scalar(C.scrA[0:64, 0:512], C.scrA[0:64, 0:512], g1col, None,
                                           ALU.mult, ALU.bypass),
          reads=["scrA", "gate1m"], writes=["scrA"])
    S.run("DVE", lambda h: h.scalar_tensor_tensor(C.scrD[0:64, :], C.scrD[0:64, :], gcol,
                                                  C.scrA[0:64, 0:512], ALU.mult, ALU.add),
          reads=["scrD", "scrA", "gate"], writes=["scrD"])
    S.run("PE", lambda h: h.matmul(C.ps[2][:, :], C.DUP64, C.scrD[0:64, :], start=True, stop=True),
          reads=["scrD", "dup64", "ps2"], writes=["ps2"])
    def scat(h):
        last = None
        for gp in range(2):
            src = C.ps[2][64 * gp:64 * gp + 64, 256 * gp:256 * gp + 256].rearrange(
                "p (g2 rr) -> p g2 rr", g2=8, rr=32)
            dd = C.stg[64 * gp:64 * gp + 64, :].rearrange(
                "p (g2 f) -> p g2 f", g2=8, f=512)[:, :, 32 * head:32 * head + 32]
            last = h.tensor_copy(dd, src)
        return last
    S.run("DVE", scat, reads=["ps2"] + (STG if head == 0 else []),
          writes=[f"attw_{head}"] + (STG if head == 0 else []))

    if sub < 6:
        return
    _acc(S, C.ps[3][0:64, 0:64],
         [(C.tokp[:, 512 + 128 * jc:512 + 128 * jc + 64], C.tokp[:, 128 * jc:128 * jc + 64]) for jc in range(4)],
         reads=[f"sktok{jc}" for jc in range(4)] + [f"vtok{jc}" for jc in range(4)] + ["ps3"],
         writes=["ps3"])
    S.run("ACT", lambda h: h.activation(out=C.scrA[0:64, 0:64], in_=C.ps[3][0:64, 0:64], func=AF.Copy),
          reads=["ps3", "scrA"], writes=["scrA"])
    S.run("PE", lambda h: h.matmul(C.ps[3][:, 0:64], C.DUP64, C.scrA[0:64, 0:64], start=True, stop=True),
          reads=["scrA", "dup64", "ps3"], writes=["ps3"])
    S.run("DVE", lambda h: h.tensor_tensor(C.Md[:, 64 * head:64 * head + 64],
                                           C.Md[:, 64 * head:64 * head + 64],
                                           C.ps[3][:, 0:64], ALU.add),
          reads=["ps3", "Md", "ps7"], writes=["Md"])
    S.run("DVE", lambda h: h.tensor_reduce(C.srow[:, 5:6], kh, axis=AX.X, op=ALU.add),
          reads=["sk_h"], writes=["srow_dz"])
    S.run("PE", lambda h: h.matmul(C.ps[3][:, 64:65], C.DUPSW, C.srow[:, 5:6], start=True, stop=True),
          reads=["srow_dz", "dupsw", "ps3"], writes=["ps3"])
    S.run("DVE", lambda h: h.tensor_tensor(C.Zd[:, head:head + 1], C.Zd[:, head:head + 1],
                                           C.ps[3][:, 64:65], ALU.add),
          reads=["ps3", "Zd"], writes=["Zd"])


_nc_cache = None


def kernel(**inputs):
    global _nc_cache
    if _nc_cache is None:
        _nc_cache = build_program()
    in_maps = _prep_inputs(inputs)
    res = run_bass_kernel_spmd(_nc_cache, in_maps, core_ids=list(range(NCORES)))
    out = np.stack([np.ascontiguousarray(np.asarray(r["out_fm"]).T) for r in res.results])
    return out.astype(np.float32)



def _qdq_tile(S, C, dst_ap, src_ap, inv_col, sc_col, lo, hi, keyr, keyw, width):
    for o in range(0, width, 2048):
        w = min(2048, width - o)
        S.run("DVE", lambda h, o=o, w=w: h.tensor_scalar(
            C.fwi[:, 0:w], src_ap[:, o:o + w], inv_col, None, ALU.mult, ALU.bypass),
            reads=list(keyr) + ["fwi", "cb"], writes=["fwi"])
        S.run("DVE", lambda h, o=o, w=w: h.tensor_scalar(
            dst_ap[:, o:o + w], C.fwi[:, 0:w], float(lo), float(hi), ALU.max, ALU.min),
            reads=["fwi"] + list(keyw), writes=list(keyw))
    S.run("DVE", lambda h: h.tensor_scalar(dst_ap, dst_ap, sc_col, None, ALU.mult, ALU.bypass),
          reads=list(keyw) + ["cb"], writes=list(keyw))


def _absmax_track(S, C, src_ap, col, first, keyr):
    S.run("DVE", lambda h: h.reduce_max(C.srow[:, 15:16], src_ap, axis=AX.X, apply_absolute_value=True),
          reads=list(keyr) + ["srow"], writes=["srow"])
    if first:
        S.run("DVE", lambda h: h.tensor_copy(C.srow[:, col:col + 1], C.srow[:, 15:16]),
              reads=["srow"], writes=["srow"])
    else:
        S.run("DVE", lambda h: h.tensor_tensor(C.srow[:, col:col + 1], C.srow[:, col:col + 1],
                                               C.srow[:, 15:16], ALU.max),
              reads=["srow"], writes=["srow"])


def _finish_absmax(S, C, col, cell):
    S.run("PE", lambda h: h.transpose(C.ps[4][0:1, 0:128], C.srow[:, col:col + 1], C.IDENT[:, :]),
          reads=["srow", "ident", "ps4"], writes=["ps4"])
    S.run("DVE", lambda h: h.reduce_max(C.srow[0:1, 8 + cell:9 + cell], C.ps[4][0:1, 0:128], axis=AX.X),
          reads=["ps4", "srow"], writes=["srow"])


def _allreduce(S, C, idx):
    S.dma("POOL", lambda h: h.dma_start(out=C.ap[f"cc_in{idx}"][:, :], in_=C.srow[0:1, 8:16]),
          reads=["srow", f"ccin{idx}"], writes=[f"ccin{idx}"], slot="cc")
    tok = S.run("POOL", lambda h: h.collective_compute(
        "AllReduce", ALU.max, replica_groups=RG8,
        ins=[C.ap[f"cc_in{idx}"][:, :]], outs=[C.ap[f"cc_out{idx}"][:, :]]),
        reads=[f"ccin{idx}", f"ccout{idx}"], writes=[f"ccout{idx}"])
    def body(h):
        h.wait_ge(S.psem["POOL"], tok[2])
    getattr(S.block, "gpsimd")(body)
    S.dma("POOL", lambda h: h.dma_start(out=C.srow[0:1, 8:16], in_=C.ap[f"cc_out{idx}"][:, :]),
          reads=[f"ccout{idx}", "srow"], writes=["srow"], slot="cc")


def _scale_prep(S, C, cell, inv_col, sc_col):
    S.run("DVE", lambda h: h.tensor_scalar(C.srow[0:1, 14:15], C.srow[0:1, 8 + cell:9 + cell],
                                           float(np.float32(1.0 / 32767.0)), 1e-12, ALU.mult, ALU.max),
          reads=["srow"], writes=["srow"])
    S.run("DVE", lambda h: h.reciprocal(C.srow[0:1, 13:14], C.srow[0:1, 14:15]),
          reads=["srow"], writes=["srow"])
    S.run("PE", lambda h: h.matmul(C.ps[1][:, 0:2], C.ONES[0:1, 0:128], C.srow[0:1, 13:15],
                                   start=True, stop=True),
          reads=["srow", "ones", "ps1"], writes=["ps1"])
    S.run("ACT", lambda h: h.activation(out=C.cb[:, inv_col:inv_col + 1], in_=C.ps[1][:, 0:1], func=AF.Copy),
          reads=["ps1", "cb"], writes=["cb"])
    S.run("ACT", lambda h: h.activation(out=C.cb[:, sc_col:sc_col + 1], in_=C.ps[1][:, 1:2], func=AF.Copy),
          reads=["ps1", "cb"], writes=["cb"])


def _phase_b(nc, S, C):
    ap = C.ap
    ONES = C.ONES
    IDENT = C.IDENT

    for tt in range(NSEG):
        for dc in range(DC):
            S.dma("SP", lambda h, tt=tt, dc=dc: h.dma_start(
                out=C.xtile[:, 512 * dc:512 * dc + 512],
                in_=ap["x1_sp"][128 * dc:128 * dc + 128, 512 * tt:512 * tt + 512]),
                reads=[f"x1sp_{dc}_{tt}", f"xt_{dc}"], writes=[f"xt_{dc}"], slot="xin")
        _ln_tile(S, C)
        for dc in range(DC):
            S.dma("POOL", lambda h, tt=tt, dc=dc: h.dma_start(
                out=ap["xn1_sp"][128 * dc:128 * dc + 128, 512 * tt:512 * tt + 512],
                in_=C.xn[:, 512 * dc:512 * dc + 512]),
                reads=[f"xn_{dc}"], writes=[f"xn1_{dc}_{tt}"], slot="xn1o")
        if tt == 0:
            S.run("DVE", lambda h: h.memset(C.srow[0:1, 15:16], 0.0),
                  reads=[], writes=["srow", "srow_qq", "srow_kk", "srow_c", "srow_cn",
                                    "srow_dz", "srow_qq2", "srow_kk2"])
        _absmax_track(S, C, C.xn[:, :], 0, tt == 0, XN)
    _finish_absmax(S, C, 0, 0)
    _allreduce(S, C, 0)

    if B_STAGE < 2:
        return
    for src, dst, nchunk, width in (("wg1", "wgT1_f", HC, D), ("wu1", "wuT1_f", HC, D),
                                    ("wd1", "wdT1_f", DC, HID)):
        nblk = width // 32
        nj = width // 128
        for c in range(nchunk):
            S.dma("SP", lambda h, src=src, c=c, width=width: h.dma_start(
                out=C.fw[0][:, 0:width], in_=ap[src][128 * c:128 * c + 128, :]),
                reads=["fw0"], writes=["fw0"], slot="deq_i")
            S.run("DVE", lambda h, width=width, nblk=nblk: h.reduce_max(
                C.scr[:, 0:nblk],
                C.fw[0][:, 0:width].rearrange("p (b j) -> p b j", b=nblk, j=32),
                axis=AX.X, apply_absolute_value=True),
                reads=["fw0", "scrA", "scrB"], writes=["scrA"])
            S.run("DVE", lambda h, nblk=nblk: h.tensor_scalar(
                C.scr[:, 0:nblk], C.scr[:, 0:nblk], float(np.float32(1.0 / 7.0)), 1e-12, ALU.mult, ALU.max),
                reads=["scrA"], writes=["scrA"])
            S.run("DVE", lambda h, nblk=nblk: h.reciprocal(C.scr[:, 1024:1024 + nblk], C.scr[:, 0:nblk]),
                reads=["scrA", "scrC"], writes=["scrC"])
            for o in range(0, width, 2048):
                w2 = min(2048, width - o)
                nb2 = w2 // 32
                S.run("DVE", lambda h, o=o, w2=w2, nb2=nb2: h.tensor_tensor(
                    C.fwi[:, 0:w2].rearrange("p (b j) -> p b j", b=nb2, j=32),
                    C.fw[0][:, o:o + w2].rearrange("p (b j) -> p b j", b=nb2, j=32),
                    C.scr[:, 1024 + o // 32:1024 + o // 32 + nb2].rearrange(
                        "p (b o2) -> p b o2", b=nb2, o2=1).broadcast_to((128, nb2, 32)),
                    ALU.mult),
                    reads=["fw0", "scrC", "fwi"], writes=["fwi"])
                S.run("DVE", lambda h, o=o, w2=w2: h.tensor_scalar(
                    C.fw[1][:, o:o + w2], C.fwi[:, 0:w2], -8.0, 7.0, ALU.max, ALU.min),
                    reads=["fwi", "fw1"], writes=["fw1"])
            S.run("DVE", lambda h, width=width, nblk=nblk: h.tensor_tensor(
                C.fw[1][:, 0:width].rearrange("p (b j) -> p b j", b=nblk, j=32),
                C.fw[1][:, 0:width].rearrange("p (b j) -> p b j", b=nblk, j=32),
                C.scr[:, 0:nblk].rearrange("p (b o) -> p b o", b=nblk, o=1).broadcast_to((128, nblk, 32)),
                ALU.mult),
                reads=["fw1", "scrA"], writes=["fw1"])
            for j in range(nj):
                pb = C.ps[2 + (j % 2)]
                S.run("PE", lambda h, j=j, pb=pb: h.transpose(
                    pb[:, 0:128], C.fw[1][:, 128 * j:128 * j + 128], IDENT[:, :]),
                    reads=["fw1", "ident", f"ps{2 + (j % 2)}"], writes=[f"ps{2 + (j % 2)}"])
                S.run("ACT", lambda h, j=j, pb=pb: h.activation(
                    out=C.qg[:, 128 * j:128 * j + 128], in_=pb[:, 0:128], func=AF.Copy),
                    reads=[f"ps{2 + (j % 2)}", "qg"], writes=["qg"])
            S.dma("POOL", lambda h, dst=dst, c=c, nj=nj: h.dma_start(
                out=ap[dst][:, 128 * c:128 * c + 128].rearrange("(j p) c2 -> p j c2", j=nj, p=128),
                in_=C.qg[:, 0:128 * nj].rearrange("p (j c2) -> p j c2", j=nj, c2=128)),
                reads=["qg"], writes=[f"{dst}_c{c}"], slot="deq_o")

    if B_STAGE < 3:
        return
    _scale_prep(S, C, 0, 4, 3)
    for tt in range(NSEG):
        for dc in range(DC):
            S.dma("SP", lambda h, tt=tt, dc=dc: h.dma_start(
                out=C.stg[:, 512 * dc:512 * dc + 512],
                in_=ap["xn1_sp"][128 * dc:128 * dc + 128, 512 * tt:512 * tt + 512]),
                reads=[f"xn1_{dc}_{tt}", f"stg_{dc}"], writes=[f"stg_{dc}"], slot="xin")
        for dc in range(DC):
            _qdq_tile(S, C, C.xtile[:, 512 * dc:512 * dc + 512], C.stg[:, 512 * dc:512 * dc + 512],
                      C.cb[:, 4:5], C.cb[:, 3:4], -32768.0, 32767.0,
                      [f"stg_{dc}"], [f"xt_{dc}"], 512)
        for hc in range(HC):
            for mat, spill, bank, col in (("wuT1_f", "up_sp", 6, 1), ("wgT1_f", "gate_sp", 7, 2)):
                pwi = 0 if bank == 6 else 1
                S.dma("SP", lambda h, mat=mat, hc=hc, pwi=pwi: h.dma_start(
                    out=C.pw[pwi][:].rearrange("p (dc c) -> p dc c", dc=DC, c=128),
                    in_=ap[mat][:, 128 * hc:128 * hc + 128].rearrange("(dc p) c -> p dc c", dc=DC, p=128)),
                    reads=[f"{mat}_c{i}" for i in range(HC)] + [f"pw{pwi}"], writes=[f"pw{pwi}"], slot=f"pw{pwi}")
                _acc(S, C.ps[bank][:, :],
                     [(C.pw[pwi][:, 128 * dc:128 * dc + 128], C.xtile[:, 512 * dc:512 * dc + 512]) for dc in range(DC)],
                     reads=[f"pw{pwi}"] + XT + [f"ps{bank}"], writes=[f"ps{bank}"])
                dstscr = C.scrC if bank == 6 else C.scrD
                dk = "scrC" if bank == 6 else "scrD"
                S.run("ACT", lambda h, bank=bank, dstscr=dstscr: h.activation(
                    out=dstscr, in_=C.ps[bank][:, :], func=AF.Copy),
                    reads=[f"ps{bank}", dk], writes=[dk])
                _absmax_track(S, C, dstscr, col, (tt == 0 and hc == 0), [dk])
                S.dma("POOL", lambda h, spill=spill, hc=hc, tt=tt, dstscr=dstscr: h.dma_start(
                    out=ap[spill][128 * hc:128 * hc + 128, 512 * tt:512 * tt + 512], in_=dstscr),
                    reads=[dk], writes=[f"{spill}_{hc}_{tt}"], slot="upo")
    _finish_absmax(S, C, 1, 1)
    _finish_absmax(S, C, 2, 2)
    _allreduce(S, C, 1)

    if B_STAGE < 4:
        return
    _scale_prep(S, C, 1, 4, 3)
    _scale_prep(S, C, 2, 6, 5)
    for hc in range(HC):
        S.dma("SP", lambda h, hc=hc: h.dma_start(
            out=C.stg[:, 0:4096], in_=ap["up_sp"][128 * hc:128 * hc + 128, :]),
            reads=[f"up_sp_{hc}_{t2}" for t2 in range(NSEG)] + STG, writes=STG, slot="xin")
        S.dma("SP", lambda h, hc=hc: h.dma_start(
            out=C.xtile[:, 0:4096], in_=ap["gate_sp"][128 * hc:128 * hc + 128, :]),
            reads=[f"gate_sp_{hc}_{t2}" for t2 in range(NSEG)] + XT, writes=XT, slot="xin2")
        _qdq_tile(S, C, C.stg[:, 0:4096], C.stg[:, 0:4096], C.cb[:, 4:5], C.cb[:, 3:4],
                  -32768.0, 32767.0, STG, STG, 4096)
        _qdq_tile(S, C, C.xtile[:, 0:4096], C.xtile[:, 0:4096], C.cb[:, 6:7], C.cb[:, 5:6],
                  -32768.0, 32767.0, XT, XT, 4096)
        S.run("ACT", lambda h: h.activation(out=C.qsw[:, 0:4096], in_=C.xtile[:, 0:4096], func=AF.Sigmoid),
              reads=XT + ["qsw"], writes=["qsw"])
        for o in range(0, 4096, 2048):
            S.run("DVE", lambda h, o=o: h.tensor_scalar(C.fwi[:, 0:2048], C.qsw[:, o:o + 2048], 65536.0, None,
                                                        ALU.mult, ALU.bypass),
                  reads=["qsw", "fwi"], writes=["fwi"])
            S.run("DVE", lambda h, o=o: h.tensor_scalar(C.qsw[:, o:o + 2048], C.fwi[:, 0:2048], 0.0, 65535.0,
                                                        ALU.max, ALU.min),
                  reads=["fwi", "qsw"], writes=["qsw"])
        S.run("DVE", lambda h: h.scalar_tensor_tensor(C.qsw[:, 0:4096], C.qsw[:, 0:4096],
                                                      float(np.float32(1.0 / 65536.0)),
                                                      C.xtile[:, 0:4096], ALU.mult, ALU.mult),
              reads=["qsw"] + XT, writes=["qsw"])
        _absmax_track(S, C, C.qsw[:, 0:4096], 3, hc == 0, ["qsw"])
        S.dma("POOL", lambda h, hc=hc: h.dma_start(out=ap["t_sp"][128 * hc:128 * hc + 128, :],
                                                   in_=C.qsw[:, 0:4096]),
              reads=["qsw"], writes=[f"t_sp_{hc}"], slot="upo")
        S.dma("POOL", lambda h, hc=hc: h.dma_start(out=ap["upq_sp"][128 * hc:128 * hc + 128, :],
                                                   in_=C.stg[:, 0:4096]),
              reads=STG, writes=[f"upq_sp_{hc}"], slot="upo2")
    _finish_absmax(S, C, 3, 3)
    _allreduce(S, C, 2)

    if B_STAGE < 5:
        return
    _scale_prep(S, C, 3, 4, 3)
    for hc in range(HC):
        S.dma("SP", lambda h, hc=hc: h.dma_start(out=C.stg[:, 0:4096],
                                                 in_=ap["t_sp"][128 * hc:128 * hc + 128, :]),
              reads=[f"t_sp_{hc}"] + STG, writes=STG, slot="xin")
        S.dma("SP", lambda h, hc=hc: h.dma_start(out=C.xtile[:, 0:4096],
                                                 in_=ap["upq_sp"][128 * hc:128 * hc + 128, :]),
              reads=[f"upq_sp_{hc}"] + XT, writes=XT, slot="xin2")
        _qdq_tile(S, C, C.stg[:, 0:4096], C.stg[:, 0:4096], C.cb[:, 4:5], C.cb[:, 3:4],
                  -32768.0, 32767.0, STG, STG, 4096)
        S.run("DVE", lambda h: h.tensor_tensor(C.stg[:, 0:4096], C.stg[:, 0:4096],
                                               C.xtile[:, 0:4096], ALU.mult),
              reads=STG + XT, writes=STG)
        _absmax_track(S, C, C.stg[:, 0:4096], 4, hc == 0, STG)
        S.dma("POOL", lambda h, hc=hc: h.dma_start(out=ap["u_sp"][128 * hc:128 * hc + 128, :],
                                                   in_=C.stg[:, 0:4096]),
              reads=STG, writes=[f"u_sp_{hc}"], slot="upo")
    _finish_absmax(S, C, 4, 4)
    _allreduce(S, C, 3)

    if B_STAGE < 6:
        return
    _scale_prep(S, C, 4, 4, 3)
    for hc in range(HC):
        S.dma("SP", lambda h, hc=hc: h.dma_start(out=C.stg[:, 0:4096],
                                                 in_=ap["u_sp"][128 * hc:128 * hc + 128, :]),
              reads=[f"u_sp_{hc}"] + STG, writes=STG, slot="xin")
        _qdq_tile(S, C, C.stg[:, 0:4096], C.stg[:, 0:4096], C.cb[:, 4:5], C.cb[:, 3:4],
                  -32768.0, 32767.0, STG, STG, 4096)
        S.dma("POOL", lambda h, hc=hc: h.dma_start(out=ap["upq_sp"][128 * hc:128 * hc + 128, :],
                                                   in_=C.stg[:, 0:4096]),
              reads=STG, writes=[f"oq_{hc}"], slot="upo")
    for tt in range(NSEG):
        for hc in range(HC):
            hs = hc % 2
            S.dma("SP", lambda h, hc=hc, tt=tt, hs=hs: h.dma_start(
                out=C.hdb[:, 512 * hs:512 * hs + 512],
                in_=ap["upq_sp"][128 * hc:128 * hc + 128, 512 * tt:512 * tt + 512]),
                reads=[f"oq_{hc}", f"hdb{hs}"], writes=[f"hdb{hs}"], slot=f"oqi{hs}")
            S.dma("SP", lambda h, hc=hc, hs=hs: h.dma_start(
                out=C.pw[hs][:, 0:D], in_=ap["wdT1_f"][128 * hc:128 * hc + 128, :]),
                reads=[f"wdT1_f_c{i}" for i in range(DC)] + [f"pw{hs}"], writes=[f"pw{hs}"], slot=f"pw{hs}")
            def mmd(h, hc=hc, hs=hs):
                last = None
                for do in range(DC):
                    last = h.matmul(C.ps[do][:, :], C.pw[hs][:, 128 * do:128 * do + 128],
                                    C.hdb[:, 512 * hs:512 * hs + 512],
                                    start=(hc == 0), stop=(hc == HC - 1))
                return last
            S.run("PE", mmd, reads=[f"pw{hs}", f"hdb{hs}"] + [f"ps{i}" for i in range(DC)],
                  writes=[f"ps{i}" for i in range(DC)])
        for do in range(DC):
            S.run("ACT", lambda h, do=do: h.activation(
                out=C.stg[:, 512 * do:512 * do + 512], in_=C.ps[do][:, :], func=AF.Silu),
                reads=[f"ps{do}", f"stg_{do}"], writes=[f"stg_{do}"])
            S.dma("POOL", lambda h, do=do, tt=tt: h.dma_start(
                out=ap["g_sp"][128 * do:128 * do + 128, 512 * tt:512 * tt + 512],
                in_=C.stg[:, 512 * do:512 * do + 512]),
                reads=[f"stg_{do}"], writes=[f"g1_{do}_{tt}"], slot="gsp")

    if B_STAGE < 7:
        return
    for pair in range(8):
        _l1_pair(nc, S, C, pair)

    if B_STAGE < 8:
        return
    S.dma("SP", lambda h: h.dma_start(out=C.srow[:, 0:DC], in_=ap["bo1fm"][:, :]),
          reads=["srow"], writes=["srow", "srow_bo"], slot="xin2")
    for tt in range(NSEG):
        for dc in range(DC):
            S.dma("SP", lambda h, tt=tt, dc=dc: h.dma_start(
                out=C.stg[:, 512 * dc:512 * dc + 512],
                in_=ap["att_sp"][128 * dc:128 * dc + 128, 512 * tt:512 * tt + 512]),
                reads=[f"att_{2 * dc}", f"att_{2 * dc + 1}", f"stg_{dc}"],
                writes=[f"stg_{dc}"], slot="xin")
        for oc in range(DC):
            ws = oc % 2
            _wcol_load(S, C, "Wo1", oc, ws)
            _acc(S, C.ps[2 + ws][:, :],
                 [(C.pw[ws][:, 128 * dc:128 * dc + 128], C.stg[:, 512 * dc:512 * dc + 512]) for dc in range(DC)],
                 reads=[f"pw{ws}"] + STG + [f"ps{2 + ws}"], writes=[f"ps{2 + ws}"])
            S.dma("SP", lambda h, oc=oc, tt=tt: h.dma_start(
                out=C.scrC, in_=ap["g_sp"][128 * oc:128 * oc + 128, 512 * tt:512 * tt + 512]),
                reads=[f"g1_{oc}_{tt}", "scrC"], writes=["scrC"], slot="gbk")
            S.dma("SP", lambda h, oc=oc, tt=tt: h.dma_start(
                out=C.scrD, in_=ap["x1_sp"][128 * oc:128 * oc + 128, 512 * tt:512 * tt + 512]),
                reads=[f"x1sp_{oc}_{tt}", "scrD"], writes=["scrD"], slot="gbk2")
            S.run("DVE", lambda h, oc=oc, ws=ws: h.tensor_scalar(
                C.scrB, C.ps[2 + ws][:, :], C.srow[:, oc:oc + 1], None, ALU.add, ALU.bypass),
                reads=[f"ps{2 + ws}", "srow_bo", "scrB"], writes=["scrB"])
            S.run("DVE", lambda h: h.tensor_tensor(C.scrB, C.scrB, C.scrC, ALU.mult),
                  reads=["scrB", "scrC"], writes=["scrB"])
            S.run("DVE", lambda h: h.tensor_tensor(C.scrB, C.scrB, C.scrD, ALU.add),
                  reads=["scrB", "scrD"], writes=["scrB"])
            S.dma("POOL", lambda h, oc=oc, tt=tt: h.dma_start(
                out=C.out_fm[128 * oc:128 * oc + 128, 512 * tt:512 * tt + 512], in_=C.scrB),
                reads=["scrB"], writes=[f"out_{oc}_{tt}"], slot="outw")


def _l1_pair(nc, S, C, pair):
    ap = C.ap
    ONES = C.ONES
    IDENT = C.IDENT

    for tt in range(NSEG):
        for dc in range(DC):
            S.dma("SP", lambda h, tt=tt, dc=dc: h.dma_start(
                out=C.stg[:, 512 * dc:512 * dc + 512],
                in_=ap["xn1_sp"][128 * dc:128 * dc + 128, 512 * tt:512 * tt + 512]),
                reads=[f"xn1_{dc}_{tt}", f"stg_{dc}"], writes=[f"stg_{dc}"], slot="xin")
        for mat, dst, nm, pwi, qscale in (("Wq1", C.qg, "qg", 0, 0.125),
                                          ("Wk1", C.kg, "kg", 1, 1.0)):
            if tt == 0:
                S.dma("SP", lambda h, mat=mat, pair=pair, pwi=pwi: h.dma_start(
                    out=C.pw[pwi][:].rearrange("p (dc c) -> p dc c", dc=DC, c=128),
                    in_=ap[mat][:, 128 * pair:128 * pair + 128].rearrange("(dc p) c -> p dc c", dc=DC, p=128)),
                    reads=[f"pw{pwi}"], writes=[f"pw{pwi}"], slot=f"pw{pwi}")
            _acc(S, C.ps[2 + pwi][:, :],
                 [(C.pw[pwi][:, 128 * dc:128 * dc + 128], C.stg[:, 512 * dc:512 * dc + 512]) for dc in range(DC)],
                 reads=[f"pw{pwi}"] + STG + [f"ps{2 + pwi}"], writes=[f"ps{2 + pwi}"])
            S.run("ACT", lambda h, tt=tt, dst=dst, pwi=pwi, nm=nm, qscale=qscale: h.activation(
                out=dst[:, 512 * tt:512 * tt + 512], in_=C.ps[2 + pwi][:, :], func=AF.Copy, scale=qscale),
                reads=[f"ps{2 + pwi}", nm], writes=[nm])
        if tt == 0:
            S.dma("SP", lambda h, pair=pair: h.dma_start(
                out=C.fw[0][:, 0:D].rearrange("p (dc c) -> p dc c", dc=DC, c=128),
                in_=ap["Wv1"][:, 128 * pair:128 * pair + 128].rearrange("(dc p) c -> p dc c", dc=DC, p=128)),
                reads=["fw0"], writes=["fw0"], slot="pw0")
        _acc(S, C.ps[6][:, :],
             [(C.fw[0][:, 128 * dc:128 * dc + 128], C.stg[:, 512 * dc:512 * dc + 512]) for dc in range(DC)],
             reads=["fw0"] + STG + ["ps6"], writes=["ps6"])
        S.run("ACT", lambda h, tt=tt: h.activation(
            out=C.vg[:, 512 * tt:512 * tt + 512], in_=C.ps[6][:, :], func=AF.Copy),
            reads=["ps6", "vg"], writes=["vg"])

    S.run("DVE", lambda h: h.tensor_reduce(
        C.tokp[:, 0:64], C.qg[:, :].rearrange("p (m j) -> p m j", m=64, j=64), axis=AX.X, op=ALU.add),
        reads=["qg", "tokp"],
        writes=["tokp"] + [f"vtok{j}" for j in range(4)] + [f"sktok{j}" for j in range(4)])
    S.run("DVE", lambda h: h.tensor_reduce(
        C.tokp[:, 64:128], C.kg[:, :].rearrange("p (m j) -> p m j", m=64, j=64), axis=AX.X, op=ALU.add),
        reads=["kg", "tokp"], writes=["tokp"])
    S.run("DVE", lambda h: h.tensor_scalar(C.tokp[:, 0:128], C.tokp[:, 0:128],
                                           float(np.float32(1.0 / 64.0)), None, ALU.mult, ALU.bypass),
          reads=["tokp"], writes=["tokp"])

    for jc in range(32):
        pb = C.ps[2 + (jc % 2)]
        S.run("PE", lambda h, jc=jc, pb=pb: h.transpose(
            pb[:, 0:128], C.vg[:, 128 * jc:128 * jc + 128], IDENT[:, :]),
            reads=["vg", "ident", f"ps{2 + (jc % 2)}"], writes=[f"ps{2 + (jc % 2)}"])
        S.run("ACT", lambda h, jc=jc, pb=pb: h.activation(
            out=C.fw[jc % 2][:, 128 * (jc // 2):128 * (jc // 2) + 128], in_=pb[:, 0:128], func=AF.Copy),
            reads=[f"ps{2 + (jc % 2)}", f"fw{jc % 2}"], writes=[f"fw{jc % 2}"])

    for s in range(2):
        _l1_head(nc, S, C, pair, s, 2 * pair + s)


def _l1_head(nc, S, C, pair, s, head):
    ap = C.ap
    ONES = C.ONES
    IDENT = C.IDENT
    ql = C.tokp[64 * s:64 * s + 64, 0:64]
    kl = C.tokp[64 * s:64 * s + 64, 64:128]
    qh = C.qg[64 * s:64 * s + 64, :]
    khh = C.kg[64 * s:64 * s + 64, :]
    U1 = (C.xtile if s == 0 else C.xn)[0:64, :]
    U1k = (XT if s == 0 else XN)

    S.run("ACT", lambda h: h.activation(out=C.qsw[64 * s:64 * s + 64, :], in_=qh, func=AF.Square),
          reads=["qg", "qsw"], writes=["qsw"])
    for i2 in range(8):
        S.run("PE", lambda h, i2=i2: h.matmul(C.ps[4][0:1, :], ONES[64 * s:64 * s + 64, 0:1],
                                              C.qsw[64 * s:64 * s + 64, 512 * i2:512 * i2 + 512],
                                              start=True, stop=True),
              reads=["qsw", "ones", "ps4"], writes=["ps4"])
        S.run("DVE", lambda h: h.reduce_max(C.srow[0:1, 15:16], C.ps[4][0:1, 0:512], axis=AX.X),
              reads=["ps4", "srow"], writes=["srow"])
        if i2 == 0:
            S.run("DVE", lambda h: h.tensor_copy(C.srow[0:1, 0:1], C.srow[0:1, 15:16]),
                  reads=["srow"], writes=["srow"])
        else:
            S.run("DVE", lambda h: h.tensor_tensor(C.srow[0:1, 0:1], C.srow[0:1, 0:1],
                                                   C.srow[0:1, 15:16], ALU.max),
                  reads=["srow"], writes=["srow"])
    S.run("ACT", lambda h: h.activation(out=C.qsw[64 * s:64 * s + 64, 0:64], in_=kl, func=AF.Square),
          reads=["tokp", "qsw"], writes=["qsw"])
    S.run("PE", lambda h: h.matmul(C.ps[4][0:1, 0:64], ONES[64 * s:64 * s + 64, 0:1],
                                   C.qsw[64 * s:64 * s + 64, 0:64], start=True, stop=True),
          reads=["qsw", "ones", "ps4"], writes=["ps4"])
    S.run("DVE", lambda h: h.reduce_max(C.srow[0:1, 1:2], C.ps[4][0:1, 0:64], axis=AX.X),
          reads=["ps4", "srow"], writes=["srow"])
    S.run("DVE", lambda h: h.tensor_tensor(C.srow[0:1, 2:3], C.srow[0:1, 0:1], C.srow[0:1, 1:2], ALU.mult),
          reads=["srow"], writes=["srow"])
    S.run("ACT", lambda h: h.activation(out=C.srow[0:1, 3:4], in_=C.srow[0:1, 2:3], func=AF.Sqrt),
          reads=["srow"], writes=["srow"])
    S.run("DVE", lambda h: h.tensor_scalar(C.srow[0:1, 4:5], C.srow[0:1, 3:4], -1.0, None, ALU.mult, ALU.bypass),
          reads=["srow"], writes=["srow"])
    S.run("PE", lambda h: h.matmul(C.ps[4][:, 0:1], ONES[0:1, 0:128], C.srow[0:1, 4:5], start=True, stop=True),
          reads=["srow", "ones", "ps4"], writes=["ps4"])
    S.run("ACT", lambda h: h.activation(out=C.cb[:, 0:1], in_=C.ps[4][:, 0:1], func=AF.Copy),
          reads=["ps4", "cb"], writes=["cb"])

    for i2 in range(8):
        pb = C.ps[2 + (i2 % 2)]
        S.run("PE", lambda h, i2=i2, pb=pb: h.matmul(
            pb[0:64, :], kl, qh[:, 512 * i2:512 * i2 + 512], start=True, stop=True),
            reads=["tokp", "qg", f"ps{2 + (i2 % 2)}"], writes=[f"ps{2 + (i2 % 2)}"])
        S.run("ACT", lambda h, i2=i2, pb=pb: h.activation(
            out=U1[:, 512 * i2:512 * i2 + 512], in_=pb[0:64, :], func=AF.Exp, bias=C.cb[0:64, 0:1]),
            reads=[f"ps{2 + (i2 % 2)}", "cb"] + U1k, writes=U1k)

    for i2 in range(8):
        pb = C.ps[2 + (i2 % 2)]
        S.run("PE", lambda h, i2=i2, pb=pb: h.matmul(
            pb[0:64, :], ql, khh[:, 512 * i2:512 * i2 + 512], start=True, stop=True),
            reads=["tokp", "kg", f"ps{2 + (i2 % 2)}"], writes=[f"ps{2 + (i2 % 2)}"])
        S.run("ACT", lambda h, i2=i2, pb=pb: h.activation(
            out=C.qsw[0:64, 512 * i2:512 * i2 + 512], in_=pb[0:64, :], func=AF.Copy),
            reads=[f"ps{2 + (i2 % 2)}", "qsw"], writes=["qsw"])
    S.run("DVE", lambda h: h.reduce_max(C.srow[0:64, 15:16], C.qsw[0:64, 0:4096], axis=AX.X),
          reads=["qsw", "srow"], writes=["srow"])
    S.run("DVE", lambda h: h.tensor_scalar(C.srow[0:64, 14:15], C.srow[0:64, 15:16], -1.0, None,
                                           ALU.mult, ALU.bypass),
          reads=["srow"], writes=["srow"])
    S.run("ACT", lambda h: h.activation(out=C.qsw[0:64, 0:4096], in_=C.qsw[0:64, 0:4096], func=AF.Exp,
                                        bias=C.srow[0:64, 14:15]),
          reads=["qsw", "srow"], writes=["qsw"])
    S.run("DVE", lambda h: h.tensor_reduce(C.srow[0:64, 15:16], C.qsw[0:64, 0:4096], axis=AX.X, op=ALU.add),
          reads=["qsw", "srow"], writes=["srow"])
    S.run("DVE", lambda h: h.reciprocal(C.srow[0:64, 14:15], C.srow[0:64, 15:16]),
          reads=["srow"], writes=["srow"])
    S.run("DVE", lambda h: h.tensor_scalar(C.qsw[0:64, 0:4096], C.qsw[0:64, 0:4096], C.srow[0:64, 14:15],
                                           None, ALU.mult, ALU.bypass),
          reads=["qsw", "srow"], writes=["qsw"])

    s2t = C.hdb[0:64, 0:64]
    S.run("PE", lambda h: h.matmul(C.ps[4][0:64, 0:64], ql, kl, start=True, stop=True),
          reads=["tokp", "ps4"], writes=["ps4"])
    S.run("ACT", lambda h: h.activation(out=s2t, in_=C.ps[4][0:64, 0:64], func=AF.Copy),
          reads=["ps4", "hdb0"], writes=["hdb0"])
    S.run("DVE", lambda h: h.reduce_max(C.srow[0:64, 15:16], s2t, axis=AX.X),
          reads=["hdb0", "srow"], writes=["srow"])
    S.run("DVE", lambda h: h.tensor_scalar(C.srow[0:64, 14:15], C.srow[0:64, 15:16], -1.0, None,
                                           ALU.mult, ALU.bypass),
          reads=["srow"], writes=["srow"])
    S.run("ACT", lambda h: h.activation(out=s2t, in_=s2t, func=AF.Exp, bias=C.srow[0:64, 14:15]),
          reads=["hdb0", "srow"], writes=["hdb0"])
    S.run("DVE", lambda h: h.tensor_reduce(C.srow[0:64, 15:16], s2t, axis=AX.X, op=ALU.add),
          reads=["hdb0", "srow"], writes=["srow"])
    S.run("DVE", lambda h: h.reciprocal(C.srow[0:64, 14:15], C.srow[0:64, 15:16]),
          reads=["srow"], writes=["srow"])
    S.run("DVE", lambda h: h.tensor_scalar(s2t, s2t, C.srow[0:64, 14:15], None, ALU.mult, ALU.bypass),
          reads=["hdb0", "srow"], writes=["hdb0"])

    s3T = C.scr if s == 0 else C.upool
    s3Tk = ["scrA", "scrB", "scrC", "scrD"] if s == 0 else ["upool"]
    for jc in range(32):
        pb = C.ps[2 + (jc % 2)]
        S.run("PE", lambda h, jc=jc, pb=pb: h.transpose(
            pb[:, 0:64], C.qsw[0:64, 128 * jc:128 * jc + 128], IDENT[0:64, 0:64]),
            reads=["qsw", "ident", f"ps{2 + (jc % 2)}"], writes=[f"ps{2 + (jc % 2)}"])
        S.run("ACT", lambda h, jc=jc, pb=pb: h.activation(
            out=s3T[:, 64 * jc:64 * jc + 64], in_=pb[:, 0:64], func=AF.Copy),
            reads=[f"ps{2 + (jc % 2)}"] + s3Tk, writes=s3Tk)

    def mm_s3v(h):
        last = None
        for jc in range(32):
            last = h.matmul(C.ps[5][0:64, 0:64], s3T[:, 64 * jc:64 * jc + 64],
                            C.fw[jc % 2][:, 128 * (jc // 2) + 64 * s:128 * (jc // 2) + 64 * s + 64],
                            start=(jc == 0), stop=(jc == 31))
        return last
    S.run("PE", mm_s3v, reads=s3Tk + ["fw0", "fw1", "ps5"], writes=["ps5"])
    s3vt = C.hdb[0:64, 64:128]
    S.run("ACT", lambda h: h.activation(out=s3vt, in_=C.ps[5][0:64, 0:64], func=AF.Copy),
          reads=["ps5", "hdb0"], writes=["hdb0"])

    _l1_pinv(nc, S, C, s2t, s3vt)

    def mm_r1b(h):
        last = None
        for c in range(8):
            last = h.matmul(C.ps[5][0:8, :], C.SEL8[0:64, 8 * c:8 * c + 8],
                            U1[:, 512 * c:512 * c + 512], start=(c == 0), stop=(c == 7))
        return last
    S.run("PE", mm_r1b, reads=U1k + ["sel8", "ps5"], writes=["ps5"])
    S.run("DVE", lambda h: h.reciprocal(C.hdb[0:8, 512:1024], C.ps[5][0:8, 0:512]),
          reads=["ps5", "hdb1"], writes=["hdb1"])

    wpt = C.hdb[0:64, 128:192]
    for c in range(8):
        S.run("PE", lambda h, c=c: h.matmul(C.ps[7][0:64, :], wpt, U1[:, 512 * c:512 * c + 512],
                                            start=True, stop=True),
              reads=["hdb0", "wp_h", "ps7"] + U1k, writes=["ps7"])
        S.run("PE", lambda h, c=c: h.matmul(C.ps[6][0:64, :], C.SEL8T[0:8, 64 * c:64 * c + 64],
                                            C.hdb[0:8, 512:1024], start=True, stop=True),
              reads=["hdb1", "sel8t", "ps6"], writes=["ps6"])
        S.run("ACT", lambda h, c=c: h.activation(out=C.vg[0:64, 512 * c:512 * c + 512],
                                                 in_=C.ps[6][0:64, :], func=AF.Copy),
              reads=["ps6", "vg"], writes=["vg"])
        S.run("DVE", lambda h, c=c: h.tensor_tensor(
            C.vg[0:64, 512 * c:512 * c + 512], C.vg[0:64, 512 * c:512 * c + 512],
            C.ps[7][0:64, :], ALU.mult),
            reads=["ps7", "vg"], writes=["vg"])
    S.dma("POOL", lambda h, head=head: h.dma_start(
        out=ap["att_sp"][64 * head:64 * head + 64, :], in_=C.vg[0:64, 0:4096]),
        reads=["vg"], writes=[f"att_{head}"], slot="atto")


def _l1_pinv(nc, S, C, s2t, s3vt):
    ONES = C.ONES
    IDENT = C.IDENT
    z = C.ysb[0:64, 0:64]
    zT = C.ysb[0:64, 64:128]
    xzs = C.ysb[0:64, 128:192]
    xzT = C.ysb[0:64, 192:256]
    tts = C.ysb[0:64, 256:320]
    s2T = C.ysb[0:64, 320:384]
    S.run("PE", lambda h: h.transpose(C.ps[4][0:64, 0:64], s2t, IDENT[0:64, 0:64]),
          reads=["hdb0", "ident", "ps4"], writes=["ps4"])
    S.run("ACT", lambda h: h.activation(out=s2T, in_=C.ps[4][0:64, 0:64], func=AF.Copy),
          reads=["ps4", "ysb"], writes=["ysb"])
    S.run("DVE", lambda h: h.tensor_reduce(C.srow[0:64, 15:16], s2t, axis=AX.X, op=ALU.add,
                                           apply_absolute_value=True),
          reads=["hdb0", "srow"], writes=["srow"])
    S.run("PE", lambda h: h.transpose(C.ps[4][0:1, 0:64], C.srow[0:64, 15:16], C.IDENT[0:64, 0:64]),
          reads=["srow", "ident", "ps4"], writes=["ps4"])
    S.run("DVE", lambda h: h.reduce_max(C.srow[0:1, 14:15], C.ps[4][0:1, 0:64], axis=AX.X),
          reads=["ps4", "srow"], writes=["srow"])
    S.run("PE", lambda h: h.matmul(C.ps[4][0:1, 0:64], ONES[0:64, 0:1], s2t, start=True, stop=True),
          reads=["hdb0", "ones", "ps4"], writes=["ps4"])
    S.run("DVE", lambda h: h.reduce_max(C.srow[0:1, 15:16], C.ps[4][0:1, 0:64], axis=AX.X,
                                        apply_absolute_value=True),
          reads=["ps4", "srow"], writes=["srow"])
    S.run("DVE", lambda h: h.tensor_tensor(C.srow[0:1, 13:14], C.srow[0:1, 14:15], C.srow[0:1, 15:16],
                                           ALU.mult),
          reads=["srow"], writes=["srow"])
    S.run("DVE", lambda h: h.reciprocal(C.srow[0:1, 12:13], C.srow[0:1, 13:14]),
          reads=["srow"], writes=["srow"])
    S.run("PE", lambda h: h.matmul(C.ps[4][0:64, 0:1], ONES[0:1, 0:64], C.srow[0:1, 12:13],
                                   start=True, stop=True),
          reads=["srow", "ones", "ps4"], writes=["ps4"])
    S.run("ACT", lambda h: h.activation(out=C.srow[0:64, 11:12], in_=C.ps[4][0:64, 0:1], func=AF.Copy),
          reads=["ps4", "srow"], writes=["srow"])
    S.run("DVE", lambda h: h.tensor_scalar(z, s2T, C.srow[0:64, 11:12], None, ALU.mult, ALU.bypass),
          reads=["ysb", "srow"], writes=["ysb"])
    for it in range(6):
        S.run("PE", lambda h: h.matmul(C.ps[4][0:64, 0:64], s2T, z, start=True, stop=True),
              reads=["ysb", "ps4"], writes=["ps4"])
        S.run("ACT", lambda h: h.activation(out=xzs, in_=C.ps[4][0:64, 0:64], func=AF.Copy),
              reads=["ps4", "ysb"], writes=["ysb"])
        S.run("PE", lambda h: h.transpose(C.ps[4][0:64, 0:64], xzs, IDENT[0:64, 0:64]),
              reads=["ident", "ps4", "ysb"], writes=["ps4"])
        S.run("ACT", lambda h: h.activation(out=xzT, in_=C.ps[4][0:64, 0:64], func=AF.Copy),
              reads=["ps4", "ysb"], writes=["ysb"])
        S.run("DVE", lambda h: h.scalar_tensor_tensor(tts, IDENT[0:64, 0:64], 7.0, xzs,
                                                      ALU.mult, ALU.subtract),
              reads=["ident", "ysb"], writes=["ysb"])
        S.run("PE", lambda h: h.matmul(C.ps[4][0:64, 0:64], xzT, tts, start=True, stop=True),
              reads=["ysb", "ps4"], writes=["ps4"])
        S.run("DVE", lambda h: h.scalar_tensor_tensor(tts, IDENT[0:64, 0:64], 15.0, C.ps[4][0:64, 0:64],
                                                      ALU.mult, ALU.subtract),
              reads=["ident", "ps4", "ysb"], writes=["ysb"])
        S.run("PE", lambda h: h.matmul(C.ps[4][0:64, 0:64], xzT, tts, start=True, stop=True),
              reads=["ysb", "ps4"], writes=["ps4"])
        S.run("DVE", lambda h: h.scalar_tensor_tensor(tts, IDENT[0:64, 0:64], 13.0, C.ps[4][0:64, 0:64],
                                                      ALU.mult, ALU.subtract),
              reads=["ident", "ps4", "ysb"], writes=["ysb"])
        S.run("PE", lambda h: h.transpose(C.ps[4][0:64, 0:64], z, IDENT[0:64, 0:64]),
              reads=["ident", "ps4", "ysb"], writes=["ps4"])
        S.run("ACT", lambda h: h.activation(out=zT, in_=C.ps[4][0:64, 0:64], func=AF.Copy),
              reads=["ps4", "ysb"], writes=["ysb"])
        S.run("PE", lambda h: h.matmul(C.ps[4][0:64, 0:64], zT, tts, start=True, stop=True),
              reads=["ysb", "ps4"], writes=["ps4"])
        S.run("ACT", lambda h: h.activation(out=z, in_=C.ps[4][0:64, 0:64], func=AF.Copy, scale=0.25),
              reads=["ps4", "ysb"], writes=["ysb"])
    S.run("PE", lambda h: h.transpose(C.ps[4][0:64, 0:64], z, IDENT[0:64, 0:64]),
          reads=["ident", "ps4", "ysb"], writes=["ps4"])
    S.run("ACT", lambda h: h.activation(out=zT, in_=C.ps[4][0:64, 0:64], func=AF.Copy),
          reads=["ps4", "ysb"], writes=["ysb"])
    S.run("PE", lambda h: h.matmul(C.ps[4][0:64, 0:64], zT, s3vt, start=True, stop=True),
          reads=["ysb", "hdb0", "ps4"], writes=["ps4"])
    S.run("ACT", lambda h: h.activation(out=C.hdb[0:64, 128:192], in_=C.ps[4][0:64, 0:64], func=AF.Copy),
          reads=["ps4", "hdb0"], writes=["hdb0", "wp_h"])


# version 04
# speedup vs baseline: 1.1148x; 1.1148x over previous
import numpy as np
import ml_dtypes
from contextlib import ExitStack

import concourse.bass as bass
import concourse.mybir as mybir
from concourse.bass_utils import run_bass_kernel_spmd
from concourse.replica_groups import maybe_share_collective_output_space
from concourse.bass_isa import ReduceOp

F32 = mybir.dt.float32
I32 = mybir.dt.int32
AF = mybir.ActivationFunctionType
ALU = mybir.AluOpType
AX = mybir.AxisListType

NCORES = 8
D = 1024
T = 4096
DC = D // 128
LSEG = 512
NSEG = T // LSEG
H = 16
DK = 64
HID = 3072
HC = HID // 128
RG8 = [[0, 1, 2, 3, 4, 5, 6, 7]]
C127 = float(np.float32(1.0 / 127.0))

ENGS = ("PE", "ACT", "DVE", "POOL", "SP")
_BA = {"PE": "tensor", "ACT": "scalar", "DVE": "vector", "POOL": "gpsimd", "SP": "sync"}

DEBUG = False
_DBG_SHAPES = {}
BUILD_STAGE = 2
ATT_SUB = 7
B_STAGE = 8


class Sched:

    def __init__(self, nc, block, ctx):
        self.nc, self.block, self.ctx = nc, block, ctx
        self.psem = {e: ctx.enter_context(nc.semaphore(f"P_{e}")) for e in ENGS}
        self.pcnt = {e: 0 for e in ENGS}
        self.drain_at = {e: 0 for e in ENGS}
        self.synced = {}
        self.dma_synced = {}
        self.writer = {}
        self.readers = {}
        self.dma_sem = {}

    def _slot(self, slot):
        if slot not in self.dma_sem:
            sem = self.ctx.enter_context(self.nc.semaphore(f"dma_{slot}"))
            self.dma_sem[slot] = (sem, [0])
        return self.dma_sem[slot]

    def _wait(self, eng, h, tok):
        if tok[0] == "E":
            _, peng, cnt = tok
            if peng == eng:
                if self.drain_at[eng] < cnt:
                    h.drain()
                    self.drain_at[eng] = self.pcnt[eng]
            else:
                key = (eng, peng)
                if self.synced.get(key, 0) < cnt:
                    h.wait_ge(self.psem[peng], cnt)
                    self.synced[key] = cnt
        else:
            _, slot, cnt16 = tok
            sem, _ = self._slot(slot)
            key = (eng, slot)
            if self.dma_synced.get(key, 0) < cnt16:
                h.wait_ge(sem, cnt16)
                self.dma_synced[key] = cnt16

    def _deps(self, reads, writes):
        toks = []
        for b in reads:
            w = self.writer.get(b)
            if w is not None:
                toks.append(w)
        for b in writes:
            w = self.writer.get(b)
            rd = self.readers.get(b, ())
            if w is not None and not (b.startswith("ps") and rd):
                toks.append(w)
            toks.extend(rd)
        best = {}
        for t in toks:
            k = t[:2]
            if k not in best or best[k][2] < t[2]:
                best[k] = t
        return list(best.values())

    def _commit(self, tok, reads, writes):
        for b in writes:
            self.writer[b] = tok
            self.readers[b] = []
        for b in reads:
            self.readers.setdefault(b, []).append(tok)

    def run(self, eng, emit, reads=(), writes=()):
        deps = self._deps(reads, writes)
        sem = self.psem[eng]

        def body(h):
            for t in deps:
                self._wait(eng, h, t)
            last = emit(h)
            assert last is not None, "emit() must return its last instruction"
            last.then_inc(sem, 1)

        getattr(self.block, _BA[eng])(body)
        self.pcnt[eng] += 1
        tok = ("E", eng, self.pcnt[eng])
        self._commit(tok, reads, writes)
        return tok

    def dma(self, eng, emit, reads=(), writes=(), slot=None, serialize=True):
        assert slot is not None
        deps = self._deps(reads, writes)
        sem, cnt = self._slot(slot)
        cnt[0] += 16
        target = cnt[0]
        if serialize and target > 16:
            deps = deps + [("D", slot, target - 16)]

        def body(h):
            for t in deps:
                self._wait(eng, h, t)
            emit(h).then_inc(sem, 16)

        getattr(self.block, _BA[eng])(body)
        tok = ("D", slot, target)
        self._commit(tok, reads, writes)
        return tok

    def wait_tokens(self, eng, toks):
        def body(h):
            for t in toks:
                self._wait(eng, h, t)
        getattr(self.block, _BA[eng])(body)



def _l0_mask():
    t = np.arange(512)
    gp, rem = t // 256, t % 256
    g2, r = rem // 32, rem % 32
    lh = 16 * r + 2 * g2 + gp
    return (lh[None, :] >= lh[:, None]).astype(np.float32)


def _prep_inputs(inputs):
    x = np.ascontiguousarray(np.asarray(inputs["L0_attn_x"], dtype=np.float32))
    ident = np.eye(128, dtype=np.float32)
    ones = np.ones((128, 128), dtype=np.float32)
    dup64 = np.zeros((64, 128), dtype=np.float32)
    dup64[np.arange(64), np.arange(64)] = 1.0
    dup64[np.arange(64), 64 + np.arange(64)] = 1.0
    swap = np.zeros((128, 128), dtype=np.float32)
    swap[np.arange(128), np.arange(128) ^ 64] = 1.0
    dupswap = np.zeros((128, 128), dtype=np.float32)
    dupswap[np.arange(128), np.arange(128)] = 1.0
    dupswap[np.arange(128), np.arange(128) ^ 64] = 1.0
    M0 = np.asarray(inputs["L0_attn_memory0"], np.float32).reshape(H, DK, DK)
    Mdup = np.zeros((128, H * 64), dtype=np.float32)
    for h in range(H):
        Mdup[0:64, 64 * h:64 * h + 64] = M0[h]
        Mdup[64:128, 64 * h:64 * h + 64] = M0[h]
    Z0 = np.asarray(inputs["L0_attn_z0"], np.float32).reshape(H, DK)
    Zdup = np.zeros((128, H), dtype=np.float32)
    for h in range(H):
        Zdup[0:64, h] = Z0[h]
        Zdup[64:128, h] = Z0[h]
    betas = np.asarray(inputs["L0_attn_betas"], np.float32).reshape(H, DK)
    sel8 = np.zeros((64, 64), dtype=np.float32)
    for c in range(8):
        sel8[:, 8 * c + c] = 1.0
    sel8t = np.zeros((8, 512), dtype=np.float32)
    for c in range(8):
        sel8t[c, 64 * c:64 * c + 64] = 1.0
    base = {
        "ident": ident, "ones": ones, "dup64": dup64, "swapm": swap, "dupswap": dupswap,
        "sel8": sel8, "sel8t": sel8t,
        "mask01": np.ascontiguousarray(_l0_mask().astype(ml_dtypes.bfloat16)),
        "betasc": np.ascontiguousarray(betas.T.copy()),
        "Mdup0": np.ascontiguousarray(Mdup),
        "Zdup0": np.ascontiguousarray(Zdup),
        "Wq0": np.ascontiguousarray(np.asarray(inputs["L0_attn_Wq"], np.float32)),
        "Wk0": np.ascontiguousarray(np.asarray(inputs["L0_attn_Wk"], np.float32)),
        "Wv0": np.ascontiguousarray(np.asarray(inputs["L0_attn_Wv"], np.float32)),
        "Wo0": np.ascontiguousarray(np.asarray(inputs["L0_attn_Wo"], np.float32)),
        "wgT0": np.ascontiguousarray(np.asarray(inputs["L0_ffn_w_gate"], np.int32).T),
        "wuT0": np.ascontiguousarray(np.asarray(inputs["L0_ffn_w_up"], np.int32).T),
        "wdT0": np.ascontiguousarray(np.asarray(inputs["L0_ffn_w_down"], np.int32).T),
        "Wq1": np.ascontiguousarray(np.asarray(inputs["L1_attn_Wq"], np.float32)),
        "Wk1": np.ascontiguousarray(np.asarray(inputs["L1_attn_Wk"], np.float32)),
        "Wv1": np.ascontiguousarray(np.asarray(inputs["L1_attn_Wv"], np.float32)),
        "Wo1": np.ascontiguousarray(np.asarray(inputs["L1_attn_Wo"], np.float32)),
        "bo1": np.ascontiguousarray(np.asarray(inputs["L1_attn_bo"], np.float32).reshape(1, D)),
        "bo1fm": np.ascontiguousarray(np.asarray(inputs["L1_attn_bo"], np.float32).reshape(DC, 128).T.copy()),
        "wg1": np.ascontiguousarray(np.asarray(inputs["L1_ffn_w_gate"], np.float32)),
        "wu1": np.ascontiguousarray(np.asarray(inputs["L1_ffn_w_up"], np.float32)),
        "wd1": np.ascontiguousarray(np.asarray(inputs["L1_ffn_w_down"], np.float32)),
    }
    maps = []
    for b in range(NCORES):
        m = dict(base)
        m["xT"] = np.ascontiguousarray(x[b].T)
        maps.append(m)
    return maps


class Ctx:
    pass


def _hbm(nc, C):
    C.ap = {}

    def din(name, shape, dtype=F32):
        C.ap[name] = nc.dram_tensor(name, list(shape), dtype, kind="ExternalInput").ap()

    def dsc(name, shape, dtype=F32):
        C.ap[name] = nc.dram_tensor(name, list(shape), dtype, kind="Internal").ap()

    din("xT", (D, T))
    for nm, sh in (("ident", (128, 128)), ("ones", (128, 128)), ("dup64", (64, 128)),
                   ("swapm", (128, 128)), ("dupswap", (128, 128)),
                   ("betasc", (64, H)), ("Mdup0", (128, H * 64)), ("Zdup0", (128, H)),
                   ("bo1", (1, D)), ("bo1fm", (128, DC)), ("sel8", (64, 64)),
                   ("sel8t", (8, 512))):
        din(nm, sh)
    din("mask01", (512, 512), mybir.dt.bfloat16)
    for w in ("Wq0", "Wk0", "Wv0", "Wo0", "Wq1", "Wk1", "Wv1", "Wo1"):
        din(w, (D, D))
    din("wgT0", (D, HID), I32)
    din("wuT0", (D, HID), I32)
    din("wdT0", (HID, D), I32)
    din("wg1", (HID, D))
    din("wu1", (HID, D))
    din("wd1", (D, HID))
    C.out_fm = nc.dram_tensor("out_fm", [D, T], F32, kind="ExternalOutput").ap()
    for nm, sh in (("wgT0_f", (D, HID)), ("wuT0_f", (D, HID)), ("wdT0_f", (HID, D)),
                   ("g_sp", (D, T)), ("x1_sp", (D, T)), ("xn1_sp", (D, T)), ("att_sp", (D, T)),
                   ("up_sp", (HID, T)), ("gate_sp", (HID, T)), ("t_sp", (HID, T)),
                   ("upq_sp", (HID, T)), ("u_sp", (HID, T)),
                   ("wgT1_f", (D, HID)), ("wuT1_f", (D, HID)), ("wdT1_f", (HID, D))):
        dsc(nm, sh)
    space = maybe_share_collective_output_space("AllReduce", RG8)
    for i in range(4):
        dsc(f"cc_in{i}", (1, 8))
        C.ap[f"cc_out{i}"] = nc.dram_tensor(f"cc_out{i}", [1, 8], F32, kind="Internal",
                                            addr_space=space).ap()
    C.dbg = {}
    if DEBUG:
        for name, sh in _DBG_SHAPES.items():
            C.dbg[name] = nc.dram_tensor("dbg_" + name, list(sh), F32, kind="ExternalOutput").ap()


def _dbg(S, C, name, src_ap, reads):
    if DEBUG and name in C.dbg:
        S.dma("POOL", lambda h: h.dma_start(out=C.dbg[name][:, :], in_=src_ap),
              reads=list(reads), writes=[], slot="dbgw")


def _acc(S, bank_ap, pairs, reads, writes):
    def emit(h):
        last = None
        n = len(pairs)
        for i, (lhsT, rhs) in enumerate(pairs):
            last = h.matmul(bank_ap, lhsT, rhs, start=(i == 0), stop=(i == n - 1))
        return last
    return S.run("PE", emit, reads=reads, writes=writes)


def build_program():
    nc = bass.Bass("TRN2", target_bir_lowering=False)
    C = Ctx()
    _hbm(nc, C)

    with ExitStack() as ctx:
        E = ctx.enter_context
        C.consts = E(nc.sbuf_tensor("consts", [128, 640], F32))
        C.selc = E(nc.sbuf_tensor("selc", [128, 64 + 512], F32))
        C.mask = E(nc.sbuf_tensor("mask", [128, 4 * 512], mybir.dt.bfloat16))
        C.gates = E(nc.sbuf_tensor("gates", [64, 3 * H], F32))
        C.Zd = E(nc.sbuf_tensor("Zd", [128, H], F32))
        C.Md = E(nc.sbuf_tensor("Md", [128, H * 64], F32))
        C.row = E(nc.sbuf_tensor("row", [1, 2048], F32))
        C.srow = E(nc.sbuf_tensor("srow", [128, 16], F32))
        C.cb = E(nc.sbuf_tensor("cb", [128, 8], F32))
        C.xtile = E(nc.sbuf_tensor("xtile", [128, DC * 512], F32))
        C.xn = E(nc.sbuf_tensor("xn", [128, DC * 512], F32))
        C.stg = E(nc.sbuf_tensor("stg", [128, DC * 512], F32))
        C.qg = E(nc.sbuf_tensor("qg", [128, H * 256], F32))
        C.kg = E(nc.sbuf_tensor("kg", [128, H * 256], F32))
        C.vg = E(nc.sbuf_tensor("vg", [128, H * 256], F32))
        C.qsw = E(nc.sbuf_tensor("qsw", [128, H * 256], F32))
        C.upool = E(nc.sbuf_tensor("upool", [128, 4 * 512], F32))
        C.tokp = E(nc.sbuf_tensor("tokp", [128, 8 * 128], F32))
        C.scr = E(nc.sbuf_tensor("scr", [128, 4 * 512], F32))
        C.pw = [E(nc.sbuf_tensor(f"pw{i}", [128, DC * 128], F32)) for i in range(2)]
        C.fw = [E(nc.sbuf_tensor(f"fw{i}", [128, HID], F32)) for i in range(2)]
        C.fwi = E(nc.sbuf_tensor("fwi", [128, 2048], I32))
        C.hdb = E(nc.sbuf_tensor("hdb", [128, 2 * 512], F32))
        C.ysb = E(nc.sbuf_tensor("ysb", [128, 2 * 512], F32))
        C.ps = [E(nc.psum_tensor(f"ps{i}", [128, 512], F32)) for i in range(8)]

        block = E(nc.Block())
        S = Sched(nc, block, ctx)

        C.IDENT = C.consts[:, 0:128]
        C.ONES = C.consts[:, 128:256]
        C.DUP64 = C.consts[0:64, 256:384]
        C.SWAP = C.consts[:, 384:512]
        C.DUPSW = C.consts[:, 512:640]
        C.SEL8 = C.selc[0:64, 0:64]
        C.SEL8T = C.selc[0:8, 64:576]
        C.scrA = C.scr[:, 0:512]
        C.scrB = C.scr[:, 512:1024]
        C.scrC = C.scr[:, 1024:1536]
        C.scrD = C.scr[:, 1536:2048]

        _init(nc, S, C)
        _l0_weight_convert(nc, S, C)
        _phase_a(nc, S, C)

        if B_STAGE >= 1:
            _phase_b(nc, S, C)
        else:
            for dc in range(DC):
                for tt in range(NSEG):
                    S.dma("POOL", lambda h, dc=dc, tt=tt: h.dma_start(
                        out=C.out_fm[128 * dc:128 * dc + 128, 512 * tt:512 * tt + 512],
                        in_=C.ap["x1_sp"][128 * dc:128 * dc + 128, 512 * tt:512 * tt + 512]),
                        reads=[f"x1sp_{dc}_{tt}"], writes=[], slot="outw", serialize=False)
        sem, cnt = S._slot("outw")
        S.wait_tokens("POOL", [("D", "outw", cnt[0])])

        if DEBUG:
            for nm in _DBG_SHAPES:
                S.dma("POOL", lambda h, nm=nm: h.dma_start(
                    out=C.dbg[nm][:, :],
                    in_=(C.srow[:, :] if nm == "srow" else C.ap[nm][:, :])),
                    reads=[], writes=[], slot="dbgw")
            sem2, cnt2 = S._slot("dbgw")
            S.wait_tokens("POOL", [("D", "dbgw", cnt2[0])])

    return nc


def _init(nc, S, C):
    ap = C.ap
    S.dma("SP", lambda h: h.dma_start(out=C.consts[:, 0:128], in_=ap["ident"][:, :]), writes=["ident"], slot="init")
    S.dma("SP", lambda h: h.dma_start(out=C.consts[:, 128:256], in_=ap["ones"][:, :]), writes=["ones"], slot="init")
    S.dma("SP", lambda h: h.dma_start(out=C.consts[0:64, 256:384], in_=ap["dup64"][:, :]), writes=["dup64"], slot="init")
    S.dma("SP", lambda h: h.dma_start(out=C.consts[:, 384:512], in_=ap["swapm"][:, :]), writes=["swapm"], slot="init")
    S.dma("SP", lambda h: h.dma_start(out=C.consts[:, 512:640], in_=ap["dupswap"][:, :]), writes=["dupsw"], slot="init")
    S.dma("SP", lambda h: h.dma_start(out=C.selc[0:64, 0:64], in_=ap["sel8"][:, :]), writes=["sel8"], slot="init")
    S.dma("SP", lambda h: h.dma_start(out=C.selc[0:8, 64:576], in_=ap["sel8t"][:, :]), writes=["sel8t"], slot="init")
    for jc in range(4):
        S.dma("SP", lambda h, jc=jc: h.dma_start(
            out=C.mask[:, 512 * jc:512 * jc + 512], in_=ap["mask01"][128 * jc:128 * jc + 128, :]),
            writes=["mask"], slot="init")
    S.dma("SP", lambda h: h.dma_start(out=C.gates[:, 0:H], in_=ap["betasc"][:, :]), writes=["betas"], slot="init")
    S.dma("SP", lambda h: h.dma_start(out=C.Zd[:, :], in_=ap["Zdup0"][:, :]), writes=["Zd"], slot="init")
    S.dma("SP", lambda h: h.dma_start(out=C.Md[:, :], in_=ap["Mdup0"][:, :]), writes=["Md"], slot="init")
    S.run("DVE", lambda h: h.memset(C.cb[:, 2:3], 1e-5), reads=[], writes=["cb_eps"])
    S.run("DVE", lambda h: h.memset(C.srow[0:1, 8:16], 0.0), reads=[], writes=["srow", "srow_init"])
    S.run("ACT", lambda h: h.activation(out=C.gates[:, H:2 * H], in_=C.gates[:, 0:H], func=AF.Sigmoid),
          reads=["betas"], writes=["gate"])
    S.run("DVE", lambda h: h.tensor_scalar(C.gates[:, 2 * H:3 * H], C.gates[:, H:2 * H], -1.0, 1.0, ALU.mult, ALU.add),
          reads=["gate"], writes=["gate1m"])


def _l0_weight_convert(nc, S, C):
    for src, dst, nchunk, width in (("wgT0", "wgT0_f", DC, HID),
                                    ("wuT0", "wuT0_f", DC, HID),
                                    ("wdT0", "wdT0_f", HC, D)):
        for c in range(nchunk):
            ib = c % 2
            for o in range(0, width, 2048):
                w2 = min(2048, width - o)
                S.dma("SP", lambda h, src=src, c=c, o=o, w2=w2: h.dma_start(
                    out=C.fwi[:, 0:w2], in_=C.ap[src][128 * c:128 * c + 128, o:o + w2]),
                    reads=["fwi"], writes=["fwi"], slot="cvt_i")
                S.run("DVE", lambda h, o=o, w2=w2, ib=ib: h.tensor_copy(
                    C.fw[ib][:, o:o + w2], C.fwi[:, 0:w2]),
                    reads=["fwi", f"fw{ib}"], writes=[f"fw{ib}"])
            S.dma("POOL", lambda h, dst=dst, c=c, width=width, ib=ib: h.dma_start(
                out=C.ap[dst][128 * c:128 * c + 128, :], in_=C.fw[ib][:, 0:width]),
                reads=[f"fw{ib}"], writes=[f"{dst}_c{c}"], slot=f"cvto{ib}")


XN = [f"xn_{i}" for i in range(DC)]
XT = [f"xt_{i}" for i in range(DC)]
STG = [f"stg_{i}" for i in range(DC)]


def _load_xtile(S, C, tt):
    for dc in range(DC):
        S.dma("SP", lambda h, tt=tt, dc=dc: h.dma_start(
            out=C.xtile[:, 512 * dc:512 * dc + 512],
            in_=C.ap["xT"][128 * dc:128 * dc + 128, 512 * tt:512 * tt + 512]),
            writes=[f"xt_{dc}"], slot="xin")


def _ln_tile(S, C):
    ONES = C.ONES
    _acc(S, C.ps[0][0:1, :],
         [(ONES[:, 0:1], C.xtile[:, 512 * dc:512 * dc + 512]) for dc in range(DC)],
         reads=XT + ["ones"], writes=["ps0"])
    S.run("ACT", lambda h: h.activation(out=C.xn[:, :], in_=C.xtile[:, :], func=AF.Square),
          reads=XT, writes=XN)
    _acc(S, C.ps[1][0:1, :],
         [(ONES[:, 0:1], C.xn[:, 512 * dc:512 * dc + 512]) for dc in range(DC)],
         reads=XN + ["ones"], writes=["ps1"])
    mrow = C.row[0:1, 0:512]
    rrow = C.row[0:1, 512:1024]
    S.run("DVE", lambda h: h.tensor_scalar(mrow, C.ps[0][0:1, :], 1.0 / 1024.0, None, ALU.mult, ALU.bypass),
          reads=["ps0"], writes=["rowM"])
    S.run("DVE", lambda h: h.tensor_scalar(rrow, C.ps[1][0:1, :], 1.0 / 1024.0, None, ALU.mult, ALU.bypass),
          reads=["ps1"], writes=["rowR"])
    S.run("DVE", lambda h: h.tensor_tensor(C.row[0:1, 1024:1536], mrow, mrow, ALU.mult),
          reads=["rowM"], writes=["rowC"])
    S.run("DVE", lambda h: h.tensor_tensor(rrow, rrow, C.row[0:1, 1024:1536], ALU.subtract),
          reads=["rowR", "rowC"], writes=["rowR"])
    S.run("ACT", lambda h: h.activation(out=rrow, in_=rrow, func=AF.Sqrt, bias=C.cb[0:1, 2:3]),
          reads=["rowR", "cb_eps"], writes=["rowR"])
    S.run("DVE", lambda h: h.reciprocal(rrow, rrow), reads=["rowR"], writes=["rowR"])
    S.run("PE", lambda h: h.matmul(C.ps[0][:, :], ONES[0:1, 0:128], mrow, start=True, stop=True),
          reads=["rowM", "ones", "ps0"], writes=["ps0"])
    S.run("PE", lambda h: h.matmul(C.ps[1][:, :], ONES[0:1, 0:128], rrow, start=True, stop=True),
          reads=["rowR", "ones", "ps1"], writes=["ps1"])
    S.run("DVE", lambda h: h.tensor_copy(C.scrA, C.ps[0][:, :]),
          reads=["ps0"], writes=["scrA"])
    S.run("DVE", lambda h: h.tensor_copy(C.scrB, C.ps[1][:, :]),
          reads=["ps1"], writes=["scrB"])
    for dc in range(DC):
        S.run("DVE", lambda h, dc=dc: h.tensor_tensor(
            C.xn[:, 512 * dc:512 * dc + 512], C.xtile[:, 512 * dc:512 * dc + 512], C.scrA, ALU.subtract),
            reads=[f"xt_{dc}", "scrA"], writes=[f"xn_{dc}"])
        S.run("DVE", lambda h, dc=dc: h.tensor_tensor(
            C.xn[:, 512 * dc:512 * dc + 512], C.xn[:, 512 * dc:512 * dc + 512], C.scrB, ALU.mult),
            reads=[f"xn_{dc}", "scrB"], writes=[f"xn_{dc}"])


def _wcol_load(S, C, mat, oc, ws, rkeys=()):
    dst = C.pw[ws][:].rearrange("p (dc c) -> p dc c", dc=DC, c=128)
    src = C.ap[mat][:, 128 * oc:128 * oc + 128].rearrange("(dc p) c -> p dc c", dc=DC, p=128)
    return S.dma("SP", lambda h: h.dma_start(out=dst, in_=src),
                 reads=list(rkeys), writes=[f"pw{ws}"], slot=f"pw{ws}")


def _phase_a(nc, S, C):
    if BUILD_STAGE < 1:
        return
    for tt in range(NSEG):
        _load_xtile(S, C, tt)
        _ln_tile(S, C)
        def emit_gateup(hc):
            S.dma("SP", lambda h, hc=hc: h.dma_start(
                out=C.pw[0][:].rearrange("p (dc c) -> p dc c", dc=DC, c=128),
                in_=C.ap["wgT0_f"][:, 128 * hc:128 * hc + 128].rearrange("(dc p) c -> p dc c", dc=DC, p=128)),
                reads=[f"wgT0_f_c{i}" for i in range(DC)], writes=["pw0"], slot="pw0")
            _acc(S, C.ps[6][:, :],
                 [(C.pw[0][:, 128 * dc:128 * dc + 128], C.xn[:, 512 * dc:512 * dc + 512]) for dc in range(DC)],
                 reads=["pw0"] + XN, writes=["ps6"])
            S.run("ACT", lambda h: h.activation(out=C.scrC, in_=C.ps[6][:, :], func=AF.Silu, scale=C127),
                  reads=["ps6", "scrC"], writes=["scrC"])
            S.dma("SP", lambda h, hc=hc: h.dma_start(
                out=C.pw[1][:].rearrange("p (dc c) -> p dc c", dc=DC, c=128),
                in_=C.ap["wuT0_f"][:, 128 * hc:128 * hc + 128].rearrange("(dc p) c -> p dc c", dc=DC, p=128)),
                reads=[f"wuT0_f_c{i}" for i in range(DC)], writes=["pw1"], slot="pw1")
            _acc(S, C.ps[7][:, :],
                 [(C.pw[1][:, 128 * dc:128 * dc + 128], C.xn[:, 512 * dc:512 * dc + 512]) for dc in range(DC)],
                 reads=["pw1"] + XN, writes=["ps7"])
            hsl = C.hdb[:, 512 * (hc % 2):512 * (hc % 2) + 512]
            S.run("DVE", lambda h, hsl=hsl: h.scalar_tensor_tensor(
                hsl, C.ps[7][:, :], C127, C.scrC, ALU.mult, ALU.mult),
                reads=["ps7", "scrC", f"hdb{hc % 2}"], writes=[f"hdb{hc % 2}"])
            S.dma("SP", lambda h, hc=hc: h.dma_start(
                out=C.fw[hc % 2][:, 0:D], in_=C.ap["wdT0_f"][128 * hc:128 * hc + 128, :]),
                reads=[f"wdT0_f_c{hc}", f"fw{hc % 2}"], writes=[f"fw{hc % 2}"], slot=f"fwd{hc % 2}")

        def emit_down(hc):
            hsl = C.hdb[:, 512 * (hc % 2):512 * (hc % 2) + 512]
            def mm_down(h, hc=hc, hsl=hsl):
                last = None
                for do in range(6):
                    last = h.matmul(C.ps[do][:, :], C.fw[hc % 2][:, 128 * do:128 * do + 128], hsl,
                                    start=(hc == 0), stop=(hc == HC - 1))
                return last
            S.run("PE", mm_down, reads=[f"fw{hc % 2}", f"hdb{hc % 2}"] + [f"ps{i}" for i in range(6)],
                  writes=[f"ps{i}" for i in range(6)])
            for j, do in enumerate((6, 7)):
                pb = C.ps[6 + j]
                S.run("PE", lambda h, do=do, pb=pb, hc=hc, hsl=hsl: h.matmul(
                    pb[:, :], C.fw[hc % 2][:, 128 * do:128 * do + 128], hsl, start=True, stop=True),
                    reads=[f"fw{hc % 2}", f"hdb{hc % 2}", f"ps{6 + j}"], writes=[f"ps{6 + j}"])
                ysl = C.ysb[:, 512 * j:512 * j + 512]
                if hc == 0:
                    S.run("DVE", lambda h, ysl=ysl, pb=pb: h.tensor_copy(ysl, pb[:, :]),
                          reads=[f"ps{6 + j}", "ysb"], writes=["ysb"])
                else:
                    S.run("DVE", lambda h, ysl=ysl, pb=pb: h.tensor_tensor(ysl, ysl, pb[:, :], ALU.add),
                          reads=[f"ps{6 + j}", "ysb"], writes=["ysb"])

        for hc in range(HC + 1):
            if hc < HC:
                emit_gateup(hc)
            if hc >= 1:
                emit_down(hc - 1)
        for do in range(6):
            S.run("ACT", lambda h, do=do: h.activation(
                out=C.stg[:, 512 * do:512 * do + 512], in_=C.ps[do][:, :], func=AF.Silu, scale=C127),
                reads=[f"ps{do}"], writes=[f"stg_{do}"])
        for j in range(2):
            S.run("ACT", lambda h, j=j: h.activation(
                out=C.stg[:, 512 * (6 + j):512 * (6 + j) + 512], in_=C.ysb[:, 512 * j:512 * j + 512],
                func=AF.Silu, scale=C127),
                reads=["ysb"], writes=[f"stg_{6 + j}"])
        for do in range(DC):
            S.dma("POOL", lambda h, do=do, tt=tt: h.dma_start(
                out=C.ap["g_sp"][128 * do:128 * do + 128, 512 * tt:512 * tt + 512],
                in_=C.stg[:, 512 * do:512 * do + 512]),
                reads=[f"stg_{do}"], writes=[f"gsp_{do}_{tt}"], slot="gsp")

    if BUILD_STAGE < 2:
        return
    for sg in range(NSEG):
        _load_xtile(S, C, sg)
        _ln_tile(S, C)

        for mat, dst, nm in (("Wq0", C.qg, "qg"), ("Wk0", C.kg, "kg"), ("Wv0", C.vg, "vg")):
            for oc in range(DC):
                ws = oc % 2
                _wcol_load(S, C, mat, oc, ws)
                _acc(S, C.ps[oc % 2][:, :],
                     [(C.pw[ws][:, 128 * dc:128 * dc + 128], C.xn[:, 512 * dc:512 * dc + 512]) for dc in range(DC)],
                     reads=[f"pw{ws}"] + XN, writes=[f"ps{oc % 2}"])
                S.run("DVE", lambda h, oc=oc: h.tensor_copy(C.stg[:, 512 * oc:512 * oc + 512], C.ps[oc % 2][:, :]),
                    reads=[f"ps{oc % 2}"], writes=[f"stg_{oc}"])
            def gat(h, dst=dst):
                last = None
                for gp in range(2):
                    src = C.stg[64 * gp:64 * gp + 64, :].rearrange(
                        "p (g2 hh rr) -> p hh g2 rr", g2=8, hh=16, rr=32)
                    dd = dst[64 * gp:64 * gp + 64, :].rearrange(
                        "p (hh g2 rr) -> p hh g2 rr", hh=16, g2=8, rr=32)
                    last = h.tensor_copy(dd, src)
                return last
            S.run("DVE", gat, reads=STG, writes=[nm])

        for c8 in range(8):
            S.run("PE", lambda h, c8=c8: h.matmul(
                C.ps[2 + (c8 % 2)][:, :], C.SWAP, C.qg[:, 512 * c8:512 * c8 + 512], start=True, stop=True),
                reads=["qg", "swapm", f"ps{2 + (c8 % 2)}"], writes=[f"ps{2 + (c8 % 2)}"])
            S.run("DVE", lambda h, c8=c8: h.tensor_copy(C.qsw[:, 512 * c8:512 * c8 + 512], C.ps[2 + (c8 % 2)][:, :]),
                reads=[f"ps{2 + (c8 % 2)}"], writes=["qsw"])

        if DEBUG:
            _dbg(S, C, f"qg_{sg}", C.qg[:, :], ["qg"])
            _dbg(S, C, f"kg_{sg}", C.kg[:, :], ["kg"])
            _dbg(S, C, f"vg_{sg}", C.vg[:, :], ["vg"])
            _dbg(S, C, f"xn_{sg}", C.xn[:, :], XN)

        if ATT_SUB >= 1:
            for head in range(H):
                _l0_head(nc, S, C, sg, head)

        if DEBUG:
            _dbg(S, C, f"sfm_{sg}", C.stg[:, :], [f"attw_{hh}" for hh in range(H)])

        if ATT_SUB < 7:
            continue
        attr = [f"attw_{hh}" for hh in range(H)]
        for oc in range(DC):
            ws = oc % 2
            _wcol_load(S, C, "Wo0", oc, ws)
            _acc(S, C.ps[oc % 2][:, :],
                 [(C.pw[ws][:, 128 * dc:128 * dc + 128], C.stg[:, 512 * dc:512 * dc + 512]) for dc in range(DC)],
                 reads=[f"pw{ws}"] + attr + STG, writes=[f"ps{oc % 2}"])
            S.dma("SP", lambda h, oc=oc, sg=sg: h.dma_start(
                out=C.scrC, in_=C.ap["g_sp"][128 * oc:128 * oc + 128, 512 * sg:512 * sg + 512]),
                reads=[f"gsp_{oc}_{sg}", "scrC"], writes=["scrC"], slot="gbk")
            S.run("DVE", lambda h, oc=oc: h.tensor_tensor(
                C.scrD, C.ps[oc % 2][:, :], C.scrC, ALU.mult),
                reads=[f"ps{oc % 2}", "scrC", "scrD"], writes=["scrD"])
            S.run("DVE", lambda h, oc=oc: h.tensor_tensor(
                C.scrD, C.scrD, C.xtile[:, 512 * oc:512 * oc + 512], ALU.add),
                reads=["scrD", f"xt_{oc}"], writes=["scrD"])
            S.dma("POOL", lambda h, oc=oc, sg=sg: h.dma_start(
                out=C.ap["x1_sp"][128 * oc:128 * oc + 128, 512 * sg:512 * sg + 512], in_=C.scrD),
                reads=["scrD"], writes=[f"x1sp_{oc}_{sg}"], slot="x1o")


def _l0_head(nc, S, C, sg, head):
    sub = ATT_SUB
    hof = 256 * head
    qh = C.qg[:, hof:hof + 256]
    qsh = C.qsw[:, hof:hof + 256]
    kh = C.kg[:, hof:hof + 256]
    vh = C.vg[:, hof:hof + 256]
    ONES = C.ONES
    IDENT = C.IDENT

    S.run("DVE", lambda h: h.tensor_tensor(C.scrA[:, 0:256], qh, qh, ALU.mult),
          reads=["qg", "scrA"], writes=["scrA"])
    def mm_qq(h):
        h.matmul(C.ps[4][0:1, 0:256], ONES[0:64, 0:1], C.scrA[0:64, 0:256], start=True, stop=True)
        return h.matmul(C.ps[5][0:1, 0:256], ONES[64:128, 64:65], C.scrA[64:128, 0:256],
                        start=True, stop=True)
    S.run("PE", mm_qq, reads=["scrA", "ones", "ps4", "ps5"], writes=["ps4", "ps5"])
    S.run("DVE", lambda h: h.reduce_max(C.srow[0:1, 0:1], C.ps[4][0:1, 0:256], axis=AX.X),
          reads=["ps4"], writes=["srow_qq"])
    S.run("DVE", lambda h: h.reduce_max(C.srow[0:1, 6:7], C.ps[5][0:1, 0:256], axis=AX.X),
          reads=["ps5"], writes=["srow_qq2"])
    S.run("DVE", lambda h: h.tensor_tensor(C.scrA[:, 0:256], kh, kh, ALU.mult),
          reads=["kg", "scrA", "ps4", "ps5"], writes=["scrA"])
    def mm_kk(h):
        h.matmul(C.ps[4][0:1, 0:256], ONES[0:64, 0:1], C.scrA[0:64, 0:256], start=True, stop=True)
        return h.matmul(C.ps[5][0:1, 0:256], ONES[64:128, 64:65], C.scrA[64:128, 0:256],
                        start=True, stop=True)
    S.run("PE", mm_kk, reads=["scrA", "ones", "ps4", "ps5", "srow_qq", "srow_qq2"], writes=["ps4", "ps5"])
    S.run("DVE", lambda h: h.reduce_max(C.srow[0:1, 1:2], C.ps[4][0:1, 0:256], axis=AX.X),
          reads=["ps4"], writes=["srow_kk"])
    S.run("DVE", lambda h: h.reduce_max(C.srow[0:1, 7:8], C.ps[5][0:1, 0:256], axis=AX.X),
          reads=["ps5"], writes=["srow_kk2"])
    S.run("DVE", lambda h: h.tensor_tensor(C.srow[0:1, 0:1], C.srow[0:1, 0:1], C.srow[0:1, 6:7], ALU.max),
          reads=["srow_qq", "srow_qq2"], writes=["srow_qq"])
    S.run("DVE", lambda h: h.tensor_tensor(C.srow[0:1, 1:2], C.srow[0:1, 1:2], C.srow[0:1, 7:8], ALU.max),
          reads=["srow_kk", "srow_kk2"], writes=["srow_kk"])
    S.run("DVE", lambda h: h.tensor_tensor(C.srow[0:1, 2:3], C.srow[0:1, 0:1], C.srow[0:1, 1:2], ALU.add),
          reads=["srow_qq", "srow_kk"], writes=["srow_c"])
    S.run("DVE", lambda h: h.tensor_scalar(C.srow[0:1, 4:5], C.srow[0:1, 2:3], -0.0625, None, ALU.mult, ALU.bypass),
          reads=["srow_c"], writes=["srow_cn"])
    S.run("PE", lambda h: h.matmul(C.ps[4][:, 0:1], ONES[0:1, 0:128], C.srow[0:1, 4:5], start=True, stop=True),
          reads=["srow_cn", "ones", "ps4", "srow_kk"], writes=["ps4"])
    S.run("DVE", lambda h: h.tensor_copy(C.cb[:, 0:1], C.ps[4][:, 0:1]),
          reads=["ps4"], writes=["cb"])

    if sub < 2:
        return
    for jc in range(4):
        gpj, blk = jc // 2, jc % 2
        kslice = kh[64 * gpj:64 * gpj + 64, 128 * blk:128 * blk + 128]
        def mm_st(h, kslice=kslice, gpj=gpj):
            q_same = (qh if gpj == 0 else qsh)[64 * gpj:64 * gpj + 64, :]
            q_other = (qsh if gpj == 0 else qh)[64 * gpj:64 * gpj + 64, :]
            h.matmul(C.ps[2][:, 0:256], kslice, q_same, start=True, stop=True)
            return h.matmul(C.ps[3][:, 0:256], kslice, q_other, start=True, stop=True)
        S.run("PE", mm_st, reads=["qg", "qsw", "kg", "ps2", "ps3"], writes=["ps2", "ps3"])
        S.run("ACT", lambda h, jc=jc: h.activation(
            out=C.upool[:, 512 * jc:512 * jc + 256], in_=C.ps[2][:, 0:256], func=AF.Exp,
            bias=C.cb[:, 0:1], scale=0.125),
            reads=["ps2", "cb"], writes=[f"U{jc}"])
        S.run("ACT", lambda h, jc=jc: h.activation(
            out=C.upool[:, 512 * jc + 256:512 * jc + 512], in_=C.ps[3][:, 0:256], func=AF.Exp,
            bias=C.cb[:, 0:1], scale=0.125),
            reads=["ps3", "cb"], writes=[f"U{jc}"])
        S.run("DVE", lambda h, jc=jc: h.tensor_tensor(
            C.upool[:, 512 * jc:512 * jc + 512], C.upool[:, 512 * jc:512 * jc + 512],
            C.mask[:, 512 * jc:512 * jc + 512], ALU.mult),
            reads=[f"U{jc}", "mask"], writes=[f"U{jc}"])
    Ur = [f"U{jc}" for jc in range(4)]

    if sub < 3:
        return
    for jc in range(4):
        gpj, blk = jc // 2, jc % 2
        S.run("PE", lambda h, gpj=gpj, blk=blk: h.transpose(
            C.ps[4][0:128, 0:64], vh[64 * gpj:64 * gpj + 64, 128 * blk:128 * blk + 128],
            IDENT[64 * gpj:64 * gpj + 64, 64 * gpj:64 * gpj + 64]),
            reads=["vg", "ident", "ps4", "cb"], writes=["ps4"])
        S.run("DVE", lambda h, jc=jc: h.tensor_copy(C.tokp[:, 128 * jc:128 * jc + 64], C.ps[4][0:128, 0:64]),
              reads=["ps4"], writes=[f"vtok{jc}"])
    S.run("DVE", lambda h: h.tensor_scalar(C.scrA[:, 0:256], kh, 0.0, None, ALU.min, ALU.bypass),
          reads=["kg", "scrA"], writes=["scrA"])
    S.run("ACT", lambda h: h.activation(out=C.scrA[:, 0:256], in_=C.scrA[:, 0:256], func=AF.Exp),
          reads=["scrA"], writes=["scrA"])
    S.run("DVE", lambda h: h.tensor_scalar(kh, kh, 0.0, None, ALU.max, ALU.bypass),
          reads=["kg"], writes=["kg"])
    S.run("DVE", lambda h: h.tensor_tensor(kh, kh, C.scrA[:, 0:256], ALU.add),
          reads=["kg", "scrA"], writes=["kg", "sk_h"])
    for jc in range(4):
        gpj, blk = jc // 2, jc % 2
        S.run("PE", lambda h, gpj=gpj, blk=blk: h.transpose(
            C.ps[4][0:128, 0:64], kh[64 * gpj:64 * gpj + 64, 128 * blk:128 * blk + 128],
            IDENT[64 * gpj:64 * gpj + 64, 64 * gpj:64 * gpj + 64]),
            reads=["sk_h", "ident", "ps4"], writes=["ps4"])
        S.run("DVE", lambda h, jc=jc: h.tensor_copy(C.tokp[:, 512 + 128 * jc:512 + 128 * jc + 64], C.ps[4][0:128, 0:64]),
              reads=["ps4"], writes=[f"sktok{jc}"])
    S.run("DVE", lambda h: h.tensor_scalar(C.scrA[:, 0:256], qh, 0.0, None, ALU.min, ALU.bypass),
          reads=["qg", "scrA"], writes=["scrA"])
    S.run("ACT", lambda h: h.activation(out=C.scrA[:, 0:256], in_=C.scrA[:, 0:256], func=AF.Exp),
          reads=["scrA"], writes=["scrA"])
    S.run("DVE", lambda h: h.tensor_scalar(qh, qh, 0.0, None, ALU.max, ALU.bypass),
          reads=["qg"], writes=["qg"])
    S.run("DVE", lambda h: h.tensor_tensor(qh, qh, C.scrA[:, 0:256], ALU.add),
          reads=["qg", "scrA"], writes=["qg", "sq_h"])

    if sub < 4:
        return
    _acc(S, C.ps[5][0:1, :],
         [(ONES[:, 0:1], C.upool[:, 512 * jc:512 * jc + 512]) for jc in range(4)],
         reads=Ur + ["ones", "ps5"], writes=["ps5"])
    _acc(S, C.ps[6][0:64, :],
         [(C.tokp[:, 128 * jc:128 * jc + 64], C.upool[:, 512 * jc:512 * jc + 512]) for jc in range(4)],
         reads=Ur + [f"vtok{jc}" for jc in range(4)] + ["ps6"], writes=["ps6"])
    def mm_num(h):
        h.matmul(C.ps[7][0:64, 0:256], C.Md[0:64, 64 * head:64 * head + 64], qh[0:64, :],
                 start=True, stop=True)
        return h.matmul(C.ps[3][0:64, 0:256], C.Md[64:128, 64 * head:64 * head + 64], qh[64:128, :],
                        start=True, stop=True)
    S.run("PE", mm_num, reads=["sq_h", "Md", "ps7", "ps3"], writes=["ps7", "ps3"])
    S.run("DVE", lambda h: h.reciprocal(C.row[0:1, 1024:1536], C.ps[5][0:1, :]),
          reads=["ps5"], writes=["rowC"])
    def mm_den(h):
        h.matmul(C.ps[5][0:1, 0:256], C.Zd[0:64, head:head + 1], qh[0:64, :], start=True, stop=True)
        return h.matmul(C.ps[4][0:1, 0:256], C.Zd[64:128, head:head + 1], qh[64:128, :],
                        start=True, stop=True)
    S.run("PE", mm_den, reads=["sq_h", "Zd", "ps5", "ps4"], writes=["ps5", "ps4"])
    S.run("DVE", lambda h: h.reciprocal(C.row[0:1, 1536:1792], C.ps[5][0:1, 0:256]),
          reads=["ps5"], writes=["rowD"])
    S.run("DVE", lambda h: h.reciprocal(C.row[0:1, 1792:2048], C.ps[4][0:1, 0:256]),
          reads=["ps4"], writes=["rowD"])
    S.run("PE", lambda h: h.matmul(C.ps[2][0:64, :], ONES[0:1, 0:64], C.row[0:1, 1024:1536],
                                   start=True, stop=True),
          reads=["rowC", "ones", "ps2"], writes=["ps2"])
    S.run("PE", lambda h: h.matmul(C.ps[5][0:64, :], ONES[0:1, 0:64], C.row[0:1, 1536:2048],
                                   start=True, stop=True),
          reads=["rowD", "ones", "ps5"], writes=["ps5"])
    S.run("DVE", lambda h: h.tensor_copy(C.scrB[0:64, :], C.ps[2][0:64, :]),
          reads=["ps2", "scrB"], writes=["scrB"])
    S.run("DVE", lambda h: h.tensor_copy(C.scrC[0:64, :], C.ps[5][0:64, :]),
          reads=["ps5", "scrC"], writes=["scrC"])
    if sub < 5:
        return
    gcol = C.gates[:, H + head:H + head + 1]
    g1col = C.gates[:, 2 * H + head:2 * H + head + 1]
    S.run("DVE", lambda h: h.tensor_tensor(C.scrD[0:64, 0:256], C.ps[7][0:64, 0:256],
                                           C.scrC[0:64, 0:256], ALU.mult),
          reads=["ps7", "scrC", "scrD"], writes=["scrD"])
    S.run("DVE", lambda h: h.tensor_tensor(C.scrD[0:64, 256:512], C.ps[3][0:64, 0:256],
                                           C.scrC[0:64, 256:512], ALU.mult),
          reads=["ps3", "scrC", "scrD"], writes=["scrD"])
    S.run("DVE", lambda h: h.tensor_tensor(C.scrA[0:64, 0:512], C.ps[6][0:64, :], C.scrB[0:64, :], ALU.mult),
          reads=["ps6", "scrB", "scrA"], writes=["scrA"])
    S.run("DVE", lambda h: h.tensor_scalar(C.scrA[0:64, 0:512], C.scrA[0:64, 0:512], g1col, None,
                                           ALU.mult, ALU.bypass),
          reads=["scrA", "gate1m"], writes=["scrA"])
    S.run("DVE", lambda h: h.scalar_tensor_tensor(C.scrD[0:64, :], C.scrD[0:64, :], gcol,
                                                  C.scrA[0:64, 0:512], ALU.mult, ALU.add),
          reads=["scrD", "scrA", "gate"], writes=["scrD"])
    S.run("PE", lambda h: h.matmul(C.ps[2][:, :], C.DUP64, C.scrD[0:64, :], start=True, stop=True),
          reads=["scrD", "dup64", "ps2"], writes=["ps2"])
    def scat(h):
        last = None
        for gp in range(2):
            src = C.ps[2][64 * gp:64 * gp + 64, 256 * gp:256 * gp + 256].rearrange(
                "p (g2 rr) -> p g2 rr", g2=8, rr=32)
            dd = C.stg[64 * gp:64 * gp + 64, :].rearrange(
                "p (g2 f) -> p g2 f", g2=8, f=512)[:, :, 32 * head:32 * head + 32]
            last = h.tensor_copy(dd, src)
        return last
    S.run("DVE", scat, reads=["ps2"] + (STG if head == 0 else []),
          writes=[f"attw_{head}"] + (STG if head == 0 else []))

    if sub < 6:
        return
    _acc(S, C.ps[3][0:64, 0:64],
         [(C.tokp[:, 512 + 128 * jc:512 + 128 * jc + 64], C.tokp[:, 128 * jc:128 * jc + 64]) for jc in range(4)],
         reads=[f"sktok{jc}" for jc in range(4)] + [f"vtok{jc}" for jc in range(4)] + ["ps3"],
         writes=["ps3"])
    S.run("DVE", lambda h: h.tensor_copy(C.scrA[0:64, 0:64], C.ps[3][0:64, 0:64]),
          reads=["ps3", "scrA"], writes=["scrA"])
    S.run("PE", lambda h: h.matmul(C.ps[3][:, 0:64], C.DUP64, C.scrA[0:64, 0:64], start=True, stop=True),
          reads=["scrA", "dup64", "ps3"], writes=["ps3"])
    S.run("DVE", lambda h: h.tensor_tensor(C.Md[:, 64 * head:64 * head + 64],
                                           C.Md[:, 64 * head:64 * head + 64],
                                           C.ps[3][:, 0:64], ALU.add),
          reads=["ps3", "Md", "ps7"], writes=["Md"])
    S.run("DVE", lambda h: h.tensor_reduce(C.srow[:, 5:6], kh, axis=AX.X, op=ALU.add),
          reads=["sk_h"], writes=["srow_dz"])
    S.run("PE", lambda h: h.matmul(C.ps[3][:, 64:65], C.DUPSW, C.srow[:, 5:6], start=True, stop=True),
          reads=["srow_dz", "dupsw", "ps3"], writes=["ps3"])
    S.run("DVE", lambda h: h.tensor_tensor(C.Zd[:, head:head + 1], C.Zd[:, head:head + 1],
                                           C.ps[3][:, 64:65], ALU.add),
          reads=["ps3", "Zd"], writes=["Zd"])


_nc_cache = None


def kernel(**inputs):
    global _nc_cache
    if _nc_cache is None:
        _nc_cache = build_program()
    in_maps = _prep_inputs(inputs)
    res = run_bass_kernel_spmd(_nc_cache, in_maps, core_ids=list(range(NCORES)))
    out = np.stack([np.ascontiguousarray(np.asarray(r["out_fm"]).T) for r in res.results])
    return out.astype(np.float32)



def _qdq_tile(S, C, dst_ap, src_ap, inv_col, sc_col, lo, hi, keyr, keyw, width):
    for o in range(0, width, 2048):
        w = min(2048, width - o)
        S.run("DVE", lambda h, o=o, w=w: h.tensor_scalar(
            C.fwi[:, 0:w], src_ap[:, o:o + w], inv_col, None, ALU.mult, ALU.bypass),
            reads=list(keyr) + ["fwi", "cb"], writes=["fwi"])
        S.run("DVE", lambda h, o=o, w=w: h.tensor_scalar(
            dst_ap[:, o:o + w], C.fwi[:, 0:w], float(lo), float(hi), ALU.max, ALU.min),
            reads=["fwi"] + list(keyw), writes=list(keyw))
    S.run("DVE", lambda h: h.tensor_scalar(dst_ap, dst_ap, sc_col, None, ALU.mult, ALU.bypass),
          reads=list(keyw) + ["cb"], writes=list(keyw))


def _absmax_track(S, C, src_ap, col, first, keyr):
    S.run("DVE", lambda h: h.reduce_max(C.srow[:, 15:16], src_ap, axis=AX.X, apply_absolute_value=True),
          reads=list(keyr) + ["srow"], writes=["srow"])
    if first:
        S.run("DVE", lambda h: h.tensor_copy(C.srow[:, col:col + 1], C.srow[:, 15:16]),
              reads=["srow"], writes=["srow"])
    else:
        S.run("DVE", lambda h: h.tensor_tensor(C.srow[:, col:col + 1], C.srow[:, col:col + 1],
                                               C.srow[:, 15:16], ALU.max),
              reads=["srow"], writes=["srow"])


def _finish_absmax(S, C, col, cell):
    S.run("PE", lambda h: h.transpose(C.ps[4][0:1, 0:128], C.srow[:, col:col + 1], C.IDENT[:, :]),
          reads=["srow", "ident", "ps4"], writes=["ps4"])
    S.run("DVE", lambda h: h.reduce_max(C.srow[0:1, 8 + cell:9 + cell], C.ps[4][0:1, 0:128], axis=AX.X),
          reads=["ps4", "srow"], writes=["srow"])


def _allreduce(S, C, idx):
    S.dma("POOL", lambda h: h.dma_start(out=C.ap[f"cc_in{idx}"][:, :], in_=C.srow[0:1, 8:16]),
          reads=["srow", f"ccin{idx}"], writes=[f"ccin{idx}"], slot="cc")
    tok = S.run("POOL", lambda h: h.collective_compute(
        "AllReduce", ALU.max, replica_groups=RG8,
        ins=[C.ap[f"cc_in{idx}"][:, :]], outs=[C.ap[f"cc_out{idx}"][:, :]]),
        reads=[f"ccin{idx}", f"ccout{idx}"], writes=[f"ccout{idx}"])
    def body(h):
        h.wait_ge(S.psem["POOL"], tok[2])
    getattr(S.block, "gpsimd")(body)
    S.dma("POOL", lambda h: h.dma_start(out=C.srow[0:1, 8:16], in_=C.ap[f"cc_out{idx}"][:, :]),
          reads=[f"ccout{idx}", "srow"], writes=["srow"], slot="cc")


def _scale_prep(S, C, cell, inv_col, sc_col):
    S.run("DVE", lambda h: h.tensor_scalar(C.srow[0:1, 14:15], C.srow[0:1, 8 + cell:9 + cell],
                                           float(np.float32(1.0 / 32767.0)), 1e-12, ALU.mult, ALU.max),
          reads=["srow"], writes=["srow"])
    S.run("DVE", lambda h: h.reciprocal(C.srow[0:1, 13:14], C.srow[0:1, 14:15]),
          reads=["srow"], writes=["srow"])
    S.run("PE", lambda h: h.matmul(C.ps[1][:, 0:2], C.ONES[0:1, 0:128], C.srow[0:1, 13:15],
                                   start=True, stop=True),
          reads=["srow", "ones", "ps1"], writes=["ps1"])
    S.run("DVE", lambda h: h.tensor_copy(C.cb[:, inv_col:inv_col + 1], C.ps[1][:, 0:1]),
          reads=["ps1", "cb"], writes=["cb"])
    S.run("DVE", lambda h: h.tensor_copy(C.cb[:, sc_col:sc_col + 1], C.ps[1][:, 1:2]),
          reads=["ps1", "cb"], writes=["cb"])


def _phase_b(nc, S, C):
    ap = C.ap
    ONES = C.ONES
    IDENT = C.IDENT

    for tt in range(NSEG):
        for dc in range(DC):
            S.dma("SP", lambda h, tt=tt, dc=dc: h.dma_start(
                out=C.xtile[:, 512 * dc:512 * dc + 512],
                in_=ap["x1_sp"][128 * dc:128 * dc + 128, 512 * tt:512 * tt + 512]),
                reads=[f"x1sp_{dc}_{tt}", f"xt_{dc}"], writes=[f"xt_{dc}"], slot="xin")
        _ln_tile(S, C)
        for dc in range(DC):
            S.dma("POOL", lambda h, tt=tt, dc=dc: h.dma_start(
                out=ap["xn1_sp"][128 * dc:128 * dc + 128, 512 * tt:512 * tt + 512],
                in_=C.xn[:, 512 * dc:512 * dc + 512]),
                reads=[f"xn_{dc}"], writes=[f"xn1_{dc}_{tt}"], slot="xn1o")
        if tt == 0:
            S.run("DVE", lambda h: h.memset(C.srow[0:1, 15:16], 0.0),
                  reads=[], writes=["srow", "srow_qq", "srow_kk", "srow_c", "srow_cn",
                                    "srow_dz", "srow_qq2", "srow_kk2"])
        _absmax_track(S, C, C.xn[:, :], 0, tt == 0, XN)
    _finish_absmax(S, C, 0, 0)
    _allreduce(S, C, 0)

    if B_STAGE < 2:
        return
    for src, dst, nchunk, width in (("wg1", "wgT1_f", HC, D), ("wu1", "wuT1_f", HC, D),
                                    ("wd1", "wdT1_f", DC, HID)):
        nblk = width // 32
        nj = width // 128
        for c in range(nchunk):
            S.dma("SP", lambda h, src=src, c=c, width=width: h.dma_start(
                out=C.fw[0][:, 0:width], in_=ap[src][128 * c:128 * c + 128, :]),
                reads=["fw0"], writes=["fw0"], slot="deq_i")
            S.run("DVE", lambda h, width=width, nblk=nblk: h.reduce_max(
                C.scr[:, 0:nblk],
                C.fw[0][:, 0:width].rearrange("p (b j) -> p b j", b=nblk, j=32),
                axis=AX.X, apply_absolute_value=True),
                reads=["fw0", "scrA", "scrB"], writes=["scrA"])
            S.run("DVE", lambda h, nblk=nblk: h.tensor_scalar(
                C.scr[:, 0:nblk], C.scr[:, 0:nblk], float(np.float32(1.0 / 7.0)), 1e-12, ALU.mult, ALU.max),
                reads=["scrA"], writes=["scrA"])
            S.run("DVE", lambda h, nblk=nblk: h.reciprocal(C.scr[:, 1024:1024 + nblk], C.scr[:, 0:nblk]),
                reads=["scrA", "scrC"], writes=["scrC"])
            for o in range(0, width, 2048):
                w2 = min(2048, width - o)
                nb2 = w2 // 32
                S.run("DVE", lambda h, o=o, w2=w2, nb2=nb2: h.tensor_tensor(
                    C.fwi[:, 0:w2].rearrange("p (b j) -> p b j", b=nb2, j=32),
                    C.fw[0][:, o:o + w2].rearrange("p (b j) -> p b j", b=nb2, j=32),
                    C.scr[:, 1024 + o // 32:1024 + o // 32 + nb2].rearrange(
                        "p (b o2) -> p b o2", b=nb2, o2=1).broadcast_to((128, nb2, 32)),
                    ALU.mult),
                    reads=["fw0", "scrC", "fwi"], writes=["fwi"])
                S.run("DVE", lambda h, o=o, w2=w2: h.tensor_scalar(
                    C.fw[1][:, o:o + w2], C.fwi[:, 0:w2], -8.0, 7.0, ALU.max, ALU.min),
                    reads=["fwi", "fw1"], writes=["fw1"])
            S.run("DVE", lambda h, width=width, nblk=nblk: h.tensor_tensor(
                C.fw[1][:, 0:width].rearrange("p (b j) -> p b j", b=nblk, j=32),
                C.fw[1][:, 0:width].rearrange("p (b j) -> p b j", b=nblk, j=32),
                C.scr[:, 0:nblk].rearrange("p (b o) -> p b o", b=nblk, o=1).broadcast_to((128, nblk, 32)),
                ALU.mult),
                reads=["fw1", "scrA"], writes=["fw1"])
            for j in range(nj):
                pb = C.ps[2 + (j % 2)]
                S.run("PE", lambda h, j=j, pb=pb: h.transpose(
                    pb[:, 0:128], C.fw[1][:, 128 * j:128 * j + 128], IDENT[:, :]),
                    reads=["fw1", "ident", f"ps{2 + (j % 2)}"], writes=[f"ps{2 + (j % 2)}"])
                S.run("DVE", lambda h, j=j, pb=pb: h.tensor_copy(C.qg[:, 128 * j:128 * j + 128], pb[:, 0:128]),
                    reads=[f"ps{2 + (j % 2)}", "qg"], writes=["qg"])
            S.dma("POOL", lambda h, dst=dst, c=c, nj=nj: h.dma_start(
                out=ap[dst][:, 128 * c:128 * c + 128].rearrange("(j p) c2 -> p j c2", j=nj, p=128),
                in_=C.qg[:, 0:128 * nj].rearrange("p (j c2) -> p j c2", j=nj, c2=128)),
                reads=["qg"], writes=[f"{dst}_c{c}"], slot="deq_o")

    if B_STAGE < 3:
        return
    _scale_prep(S, C, 0, 4, 3)
    for tt in range(NSEG):
        for dc in range(DC):
            S.dma("SP", lambda h, tt=tt, dc=dc: h.dma_start(
                out=C.stg[:, 512 * dc:512 * dc + 512],
                in_=ap["xn1_sp"][128 * dc:128 * dc + 128, 512 * tt:512 * tt + 512]),
                reads=[f"xn1_{dc}_{tt}", f"stg_{dc}"], writes=[f"stg_{dc}"], slot="xin")
        for dc in range(DC):
            _qdq_tile(S, C, C.xtile[:, 512 * dc:512 * dc + 512], C.stg[:, 512 * dc:512 * dc + 512],
                      C.cb[:, 4:5], C.cb[:, 3:4], -32768.0, 32767.0,
                      [f"stg_{dc}"], [f"xt_{dc}"], 512)
        for hc in range(HC):
            for mat, spill, bank, col in (("wuT1_f", "up_sp", 6 - 2 * (hc % 2), 1),
                                          ("wgT1_f", "gate_sp", 7 - 2 * (hc % 2), 2)):
                pwi = 0 if mat == "wuT1_f" else 1
                S.dma("SP", lambda h, mat=mat, hc=hc, pwi=pwi: h.dma_start(
                    out=C.pw[pwi][:].rearrange("p (dc c) -> p dc c", dc=DC, c=128),
                    in_=ap[mat][:, 128 * hc:128 * hc + 128].rearrange("(dc p) c -> p dc c", dc=DC, p=128)),
                    reads=[f"{mat}_c{i}" for i in range(HC)] + [f"pw{pwi}"], writes=[f"pw{pwi}"], slot=f"pw{pwi}")
                _acc(S, C.ps[bank][:, :],
                     [(C.pw[pwi][:, 128 * dc:128 * dc + 128], C.xtile[:, 512 * dc:512 * dc + 512]) for dc in range(DC)],
                     reads=[f"pw{pwi}"] + XT + [f"ps{bank}"], writes=[f"ps{bank}"])
                dstscr = C.scrC if mat == "wuT1_f" else C.scrD
                dk = "scrC" if mat == "wuT1_f" else "scrD"
                S.run("DVE", lambda h, bank=bank, dstscr=dstscr: h.tensor_copy(dstscr, C.ps[bank][:, :]),
                    reads=[f"ps{bank}", dk], writes=[dk])
                _absmax_track(S, C, dstscr, col, (tt == 0 and hc == 0), [dk])
                S.dma("POOL", lambda h, spill=spill, hc=hc, tt=tt, dstscr=dstscr: h.dma_start(
                    out=ap[spill][128 * hc:128 * hc + 128, 512 * tt:512 * tt + 512], in_=dstscr),
                    reads=[dk], writes=[f"{spill}_{hc}_{tt}"], slot="upo")
    _finish_absmax(S, C, 1, 1)
    _finish_absmax(S, C, 2, 2)
    _allreduce(S, C, 1)

    if B_STAGE < 4:
        return
    _scale_prep(S, C, 1, 4, 3)
    _scale_prep(S, C, 2, 6, 5)
    for hc in range(HC):
        S.dma("SP", lambda h, hc=hc: h.dma_start(
            out=C.stg[:, 0:4096], in_=ap["up_sp"][128 * hc:128 * hc + 128, :]),
            reads=[f"up_sp_{hc}_{t2}" for t2 in range(NSEG)] + STG, writes=STG, slot="xin")
        S.dma("SP", lambda h, hc=hc: h.dma_start(
            out=C.xtile[:, 0:4096], in_=ap["gate_sp"][128 * hc:128 * hc + 128, :]),
            reads=[f"gate_sp_{hc}_{t2}" for t2 in range(NSEG)] + XT, writes=XT, slot="xin2")
        _qdq_tile(S, C, C.stg[:, 0:4096], C.stg[:, 0:4096], C.cb[:, 4:5], C.cb[:, 3:4],
                  -32768.0, 32767.0, STG, STG, 4096)
        _qdq_tile(S, C, C.xtile[:, 0:4096], C.xtile[:, 0:4096], C.cb[:, 6:7], C.cb[:, 5:6],
                  -32768.0, 32767.0, XT, XT, 4096)
        S.run("ACT", lambda h: h.activation(out=C.qsw[:, 0:4096], in_=C.xtile[:, 0:4096], func=AF.Sigmoid),
              reads=XT + ["qsw"], writes=["qsw"])
        for o in range(0, 4096, 2048):
            S.run("DVE", lambda h, o=o: h.tensor_scalar(C.fwi[:, 0:2048], C.qsw[:, o:o + 2048], 65536.0, None,
                                                        ALU.mult, ALU.bypass),
                  reads=["qsw", "fwi"], writes=["fwi"])
            S.run("DVE", lambda h, o=o: h.tensor_scalar(C.qsw[:, o:o + 2048], C.fwi[:, 0:2048], 0.0, 65535.0,
                                                        ALU.max, ALU.min),
                  reads=["fwi", "qsw"], writes=["qsw"])
        S.run("DVE", lambda h: h.scalar_tensor_tensor(C.qsw[:, 0:4096], C.qsw[:, 0:4096],
                                                      float(np.float32(1.0 / 65536.0)),
                                                      C.xtile[:, 0:4096], ALU.mult, ALU.mult),
              reads=["qsw"] + XT, writes=["qsw"])
        _absmax_track(S, C, C.qsw[:, 0:4096], 3, hc == 0, ["qsw"])
        S.dma("POOL", lambda h, hc=hc: h.dma_start(out=ap["t_sp"][128 * hc:128 * hc + 128, :],
                                                   in_=C.qsw[:, 0:4096]),
              reads=["qsw"], writes=[f"t_sp_{hc}"], slot="upo")
        S.dma("POOL", lambda h, hc=hc: h.dma_start(out=ap["upq_sp"][128 * hc:128 * hc + 128, :],
                                                   in_=C.stg[:, 0:4096]),
              reads=STG, writes=[f"upq_sp_{hc}"], slot="upo2")
    _finish_absmax(S, C, 3, 3)
    _allreduce(S, C, 2)

    if B_STAGE < 5:
        return
    _scale_prep(S, C, 3, 4, 3)
    for hc in range(HC):
        S.dma("SP", lambda h, hc=hc: h.dma_start(out=C.stg[:, 0:4096],
                                                 in_=ap["t_sp"][128 * hc:128 * hc + 128, :]),
              reads=[f"t_sp_{hc}"] + STG, writes=STG, slot="xin")
        S.dma("SP", lambda h, hc=hc: h.dma_start(out=C.xtile[:, 0:4096],
                                                 in_=ap["upq_sp"][128 * hc:128 * hc + 128, :]),
              reads=[f"upq_sp_{hc}"] + XT, writes=XT, slot="xin2")
        _qdq_tile(S, C, C.stg[:, 0:4096], C.stg[:, 0:4096], C.cb[:, 4:5], C.cb[:, 3:4],
                  -32768.0, 32767.0, STG, STG, 4096)
        S.run("DVE", lambda h: h.tensor_tensor(C.stg[:, 0:4096], C.stg[:, 0:4096],
                                               C.xtile[:, 0:4096], ALU.mult),
              reads=STG + XT, writes=STG)
        _absmax_track(S, C, C.stg[:, 0:4096], 4, hc == 0, STG)
        S.dma("POOL", lambda h, hc=hc: h.dma_start(out=ap["u_sp"][128 * hc:128 * hc + 128, :],
                                                   in_=C.stg[:, 0:4096]),
              reads=STG, writes=[f"u_sp_{hc}"], slot="upo")
    _finish_absmax(S, C, 4, 4)
    _allreduce(S, C, 3)

    if B_STAGE < 6:
        return
    _scale_prep(S, C, 4, 4, 3)
    for hc in range(HC):
        S.dma("SP", lambda h, hc=hc: h.dma_start(out=C.stg[:, 0:4096],
                                                 in_=ap["u_sp"][128 * hc:128 * hc + 128, :]),
              reads=[f"u_sp_{hc}"] + STG, writes=STG, slot="xin")
        _qdq_tile(S, C, C.stg[:, 0:4096], C.stg[:, 0:4096], C.cb[:, 4:5], C.cb[:, 3:4],
                  -32768.0, 32767.0, STG, STG, 4096)
        S.dma("POOL", lambda h, hc=hc: h.dma_start(out=ap["upq_sp"][128 * hc:128 * hc + 128, :],
                                                   in_=C.stg[:, 0:4096]),
              reads=STG, writes=[f"oq_{hc}"], slot="upo")
    for tt in range(NSEG):
        for hc in range(HC):
            hs = hc % 2
            S.dma("SP", lambda h, hc=hc, tt=tt, hs=hs: h.dma_start(
                out=C.hdb[:, 512 * hs:512 * hs + 512],
                in_=ap["upq_sp"][128 * hc:128 * hc + 128, 512 * tt:512 * tt + 512]),
                reads=[f"oq_{hc}", f"hdb{hs}"], writes=[f"hdb{hs}"], slot=f"oqi{hs}")
            S.dma("SP", lambda h, hc=hc, hs=hs: h.dma_start(
                out=C.pw[hs][:, 0:D], in_=ap["wdT1_f"][128 * hc:128 * hc + 128, :]),
                reads=[f"wdT1_f_c{i}" for i in range(DC)] + [f"pw{hs}"], writes=[f"pw{hs}"], slot=f"pw{hs}")
            def mmd(h, hc=hc, hs=hs):
                last = None
                for do in range(DC):
                    last = h.matmul(C.ps[do][:, :], C.pw[hs][:, 128 * do:128 * do + 128],
                                    C.hdb[:, 512 * hs:512 * hs + 512],
                                    start=(hc == 0), stop=(hc == HC - 1))
                return last
            S.run("PE", mmd, reads=[f"pw{hs}", f"hdb{hs}"] + [f"ps{i}" for i in range(DC)],
                  writes=[f"ps{i}" for i in range(DC)])
        for do in range(DC):
            S.run("ACT", lambda h, do=do: h.activation(
                out=C.stg[:, 512 * do:512 * do + 512], in_=C.ps[do][:, :], func=AF.Silu),
                reads=[f"ps{do}", f"stg_{do}"], writes=[f"stg_{do}"])
            S.dma("POOL", lambda h, do=do, tt=tt: h.dma_start(
                out=ap["g_sp"][128 * do:128 * do + 128, 512 * tt:512 * tt + 512],
                in_=C.stg[:, 512 * do:512 * do + 512]),
                reads=[f"stg_{do}"], writes=[f"g1_{do}_{tt}"], slot="gsp")

    if B_STAGE < 7:
        return
    for pair in range(8):
        _l1_pair(nc, S, C, pair)

    if B_STAGE < 8:
        return
    S.dma("SP", lambda h: h.dma_start(out=C.srow[:, 0:DC], in_=ap["bo1fm"][:, :]),
          reads=["srow"], writes=["srow", "srow_bo"], slot="xin2")
    for tt in range(NSEG):
        for dc in range(DC):
            S.dma("SP", lambda h, tt=tt, dc=dc: h.dma_start(
                out=C.stg[:, 512 * dc:512 * dc + 512],
                in_=ap["att_sp"][128 * dc:128 * dc + 128, 512 * tt:512 * tt + 512]),
                reads=[f"att_{2 * dc}", f"att_{2 * dc + 1}", f"stg_{dc}"],
                writes=[f"stg_{dc}"], slot="xin")
        for oc in range(DC):
            ws = oc % 2
            _wcol_load(S, C, "Wo1", oc, ws)
            _acc(S, C.ps[2 + ws][:, :],
                 [(C.pw[ws][:, 128 * dc:128 * dc + 128], C.stg[:, 512 * dc:512 * dc + 512]) for dc in range(DC)],
                 reads=[f"pw{ws}"] + STG + [f"ps{2 + ws}"], writes=[f"ps{2 + ws}"])
            S.dma("SP", lambda h, oc=oc, tt=tt: h.dma_start(
                out=C.scrC, in_=ap["g_sp"][128 * oc:128 * oc + 128, 512 * tt:512 * tt + 512]),
                reads=[f"g1_{oc}_{tt}", "scrC"], writes=["scrC"], slot="gbk")
            S.dma("SP", lambda h, oc=oc, tt=tt: h.dma_start(
                out=C.scrD, in_=ap["x1_sp"][128 * oc:128 * oc + 128, 512 * tt:512 * tt + 512]),
                reads=[f"x1sp_{oc}_{tt}", "scrD"], writes=["scrD"], slot="gbk2")
            S.run("DVE", lambda h, oc=oc, ws=ws: h.tensor_scalar(
                C.scrB, C.ps[2 + ws][:, :], C.srow[:, oc:oc + 1], None, ALU.add, ALU.bypass),
                reads=[f"ps{2 + ws}", "srow_bo", "scrB"], writes=["scrB"])
            S.run("DVE", lambda h: h.tensor_tensor(C.scrB, C.scrB, C.scrC, ALU.mult),
                  reads=["scrB", "scrC"], writes=["scrB"])
            S.run("DVE", lambda h: h.tensor_tensor(C.scrB, C.scrB, C.scrD, ALU.add),
                  reads=["scrB", "scrD"], writes=["scrB"])
            S.dma("POOL", lambda h, oc=oc, tt=tt: h.dma_start(
                out=C.out_fm[128 * oc:128 * oc + 128, 512 * tt:512 * tt + 512], in_=C.scrB),
                reads=["scrB"], writes=[f"out_{oc}_{tt}"], slot="outw")


def _l1_pair(nc, S, C, pair):
    ap = C.ap
    ONES = C.ONES
    IDENT = C.IDENT

    for tt in range(NSEG):
        for dc in range(DC):
            S.dma("SP", lambda h, tt=tt, dc=dc: h.dma_start(
                out=C.stg[:, 512 * dc:512 * dc + 512],
                in_=ap["xn1_sp"][128 * dc:128 * dc + 128, 512 * tt:512 * tt + 512]),
                reads=[f"xn1_{dc}_{tt}", f"stg_{dc}"], writes=[f"stg_{dc}"], slot="xin")
        for mat, dst, nm, pwi, qscale in (("Wq1", C.qg, "qg", 0, 0.125),
                                          ("Wk1", C.kg, "kg", 1, 1.0)):
            if tt == 0:
                S.dma("SP", lambda h, mat=mat, pair=pair, pwi=pwi: h.dma_start(
                    out=C.pw[pwi][:].rearrange("p (dc c) -> p dc c", dc=DC, c=128),
                    in_=ap[mat][:, 128 * pair:128 * pair + 128].rearrange("(dc p) c -> p dc c", dc=DC, p=128)),
                    reads=[f"pw{pwi}"], writes=[f"pw{pwi}"], slot=f"pw{pwi}")
            _acc(S, C.ps[2 + pwi][:, :],
                 [(C.pw[pwi][:, 128 * dc:128 * dc + 128], C.stg[:, 512 * dc:512 * dc + 512]) for dc in range(DC)],
                 reads=[f"pw{pwi}"] + STG + [f"ps{2 + pwi}"], writes=[f"ps{2 + pwi}"])
            S.run("DVE", lambda h, tt=tt, dst=dst, pwi=pwi, nm=nm, qscale=qscale: h.tensor_scalar(
                dst[:, 512 * tt:512 * tt + 512], C.ps[2 + pwi][:, :], float(qscale), None, ALU.mult, ALU.bypass),
                reads=[f"ps{2 + pwi}", nm], writes=[nm])
        if tt == 0:
            S.dma("SP", lambda h, pair=pair: h.dma_start(
                out=C.fw[0][:, 0:D].rearrange("p (dc c) -> p dc c", dc=DC, c=128),
                in_=ap["Wv1"][:, 128 * pair:128 * pair + 128].rearrange("(dc p) c -> p dc c", dc=DC, p=128)),
                reads=["fw0"], writes=["fw0"], slot="pw0")
        _acc(S, C.ps[6][:, :],
             [(C.fw[0][:, 128 * dc:128 * dc + 128], C.stg[:, 512 * dc:512 * dc + 512]) for dc in range(DC)],
             reads=["fw0"] + STG + ["ps6"], writes=["ps6"])
        S.run("DVE", lambda h, tt=tt: h.tensor_copy(C.vg[:, 512 * tt:512 * tt + 512], C.ps[6][:, :]),
            reads=["ps6", "vg"], writes=["vg"])

    S.run("DVE", lambda h: h.tensor_reduce(
        C.tokp[:, 0:64], C.qg[:, :].rearrange("p (m j) -> p m j", m=64, j=64), axis=AX.X, op=ALU.add),
        reads=["qg", "tokp"],
        writes=["tokp"] + [f"vtok{j}" for j in range(4)] + [f"sktok{j}" for j in range(4)])
    S.run("DVE", lambda h: h.tensor_reduce(
        C.tokp[:, 64:128], C.kg[:, :].rearrange("p (m j) -> p m j", m=64, j=64), axis=AX.X, op=ALU.add),
        reads=["kg", "tokp"], writes=["tokp"])
    S.run("DVE", lambda h: h.tensor_scalar(C.tokp[:, 0:128], C.tokp[:, 0:128],
                                           float(np.float32(1.0 / 64.0)), None, ALU.mult, ALU.bypass),
          reads=["tokp"], writes=["tokp"])

    for jc in range(32):
        pb = C.ps[2 + (jc % 2)]
        S.run("PE", lambda h, jc=jc, pb=pb: h.transpose(
            pb[:, 0:128], C.vg[:, 128 * jc:128 * jc + 128], IDENT[:, :]),
            reads=["vg", "ident", f"ps{2 + (jc % 2)}"], writes=[f"ps{2 + (jc % 2)}"])
        S.run("DVE", lambda h, jc=jc, pb=pb: h.tensor_copy(C.fw[jc % 2][:, 128 * (jc // 2):128 * (jc // 2) + 128], pb[:, 0:128]),
            reads=[f"ps{2 + (jc % 2)}", f"fw{jc % 2}"], writes=[f"fw{jc % 2}"])

    for s in range(2):
        _l1_head(nc, S, C, pair, s, 2 * pair + s)


def _l1_head(nc, S, C, pair, s, head):
    ap = C.ap
    ONES = C.ONES
    IDENT = C.IDENT
    ql = C.tokp[64 * s:64 * s + 64, 0:64]
    kl = C.tokp[64 * s:64 * s + 64, 64:128]
    qh = C.qg[64 * s:64 * s + 64, :]
    khh = C.kg[64 * s:64 * s + 64, :]
    U1 = (C.xtile if s == 0 else C.xn)[0:64, :]
    U1k = (XT if s == 0 else XN)

    S.run("DVE", lambda h: h.tensor_tensor(C.qsw[64 * s:64 * s + 64, :], qh, qh, ALU.mult),
          reads=["qg", "qsw"], writes=["qsw"])
    for i2 in range(8):
        S.run("PE", lambda h, i2=i2: h.matmul(C.ps[4][0:1, :], ONES[64 * s:64 * s + 64, 0:1],
                                              C.qsw[64 * s:64 * s + 64, 512 * i2:512 * i2 + 512],
                                              start=True, stop=True),
              reads=["qsw", "ones", "ps4"], writes=["ps4"])
        S.run("DVE", lambda h: h.reduce_max(C.srow[0:1, 15:16], C.ps[4][0:1, 0:512], axis=AX.X),
              reads=["ps4", "srow"], writes=["srow"])
        if i2 == 0:
            S.run("DVE", lambda h: h.tensor_copy(C.srow[0:1, 0:1], C.srow[0:1, 15:16]),
                  reads=["srow"], writes=["srow"])
        else:
            S.run("DVE", lambda h: h.tensor_tensor(C.srow[0:1, 0:1], C.srow[0:1, 0:1],
                                                   C.srow[0:1, 15:16], ALU.max),
                  reads=["srow"], writes=["srow"])
    S.run("DVE", lambda h: h.tensor_tensor(C.qsw[64 * s:64 * s + 64, 0:64], kl, kl, ALU.mult),
          reads=["tokp", "qsw"], writes=["qsw"])
    S.run("PE", lambda h: h.matmul(C.ps[4][0:1, 0:64], ONES[64 * s:64 * s + 64, 0:1],
                                   C.qsw[64 * s:64 * s + 64, 0:64], start=True, stop=True),
          reads=["qsw", "ones", "ps4"], writes=["ps4"])
    S.run("DVE", lambda h: h.reduce_max(C.srow[0:1, 1:2], C.ps[4][0:1, 0:64], axis=AX.X),
          reads=["ps4", "srow"], writes=["srow"])
    S.run("DVE", lambda h: h.tensor_tensor(C.srow[0:1, 2:3], C.srow[0:1, 0:1], C.srow[0:1, 1:2], ALU.add),
          reads=["srow"], writes=["srow"])
    S.run("DVE", lambda h: h.tensor_scalar(C.srow[0:1, 4:5], C.srow[0:1, 2:3], -0.5, None, ALU.mult, ALU.bypass),
          reads=["srow"], writes=["srow"])
    S.run("PE", lambda h: h.matmul(C.ps[4][:, 0:1], ONES[0:1, 0:128], C.srow[0:1, 4:5], start=True, stop=True),
          reads=["srow", "ones", "ps4"], writes=["ps4"])
    S.run("DVE", lambda h: h.tensor_copy(C.cb[:, 0:1], C.ps[4][:, 0:1]),
          reads=["ps4", "cb"], writes=["cb"])

    for i2 in range(8):
        pb = C.ps[2 + (i2 % 2)]
        S.run("PE", lambda h, i2=i2, pb=pb: h.matmul(
            pb[0:64, :], kl, qh[:, 512 * i2:512 * i2 + 512], start=True, stop=True),
            reads=["tokp", "qg", f"ps{2 + (i2 % 2)}"], writes=[f"ps{2 + (i2 % 2)}"])
        S.run("ACT", lambda h, i2=i2, pb=pb: h.activation(
            out=U1[:, 512 * i2:512 * i2 + 512], in_=pb[0:64, :], func=AF.Exp, bias=C.cb[0:64, 0:1]),
            reads=[f"ps{2 + (i2 % 2)}", "cb"] + U1k, writes=U1k)

    for i2 in range(8):
        pb = C.ps[2 + (i2 % 2)]
        S.run("PE", lambda h, i2=i2, pb=pb: h.matmul(
            pb[0:64, :], ql, khh[:, 512 * i2:512 * i2 + 512], start=True, stop=True),
            reads=["tokp", "kg", f"ps{2 + (i2 % 2)}"], writes=[f"ps{2 + (i2 % 2)}"])
        S.run("DVE", lambda h, i2=i2, pb=pb: h.tensor_copy(C.qsw[0:64, 512 * i2:512 * i2 + 512], pb[0:64, :]),
            reads=[f"ps{2 + (i2 % 2)}", "qsw"], writes=["qsw"])
    S.run("DVE", lambda h: h.reduce_max(C.srow[0:64, 15:16], C.qsw[0:64, 0:4096], axis=AX.X),
          reads=["qsw", "srow"], writes=["srow"])
    S.run("DVE", lambda h: h.tensor_scalar(C.srow[0:64, 14:15], C.srow[0:64, 15:16], -1.0, None,
                                           ALU.mult, ALU.bypass),
          reads=["srow"], writes=["srow"])
    S.run("ACT", lambda h: h.activation(out=C.qsw[0:64, 0:4096], in_=C.qsw[0:64, 0:4096], func=AF.Exp,
                                        bias=C.srow[0:64, 14:15]),
          reads=["qsw", "srow"], writes=["qsw"])
    S.run("DVE", lambda h: h.tensor_reduce(C.srow[0:64, 15:16], C.qsw[0:64, 0:4096], axis=AX.X, op=ALU.add),
          reads=["qsw", "srow"], writes=["srow"])
    S.run("DVE", lambda h: h.reciprocal(C.srow[0:64, 14:15], C.srow[0:64, 15:16]),
          reads=["srow"], writes=["srow"])
    S.run("DVE", lambda h: h.tensor_scalar(C.qsw[0:64, 0:4096], C.qsw[0:64, 0:4096], C.srow[0:64, 14:15],
                                           None, ALU.mult, ALU.bypass),
          reads=["qsw", "srow"], writes=["qsw"])

    s2t = C.hdb[0:64, 0:64]
    S.run("PE", lambda h: h.matmul(C.ps[4][0:64, 0:64], ql, kl, start=True, stop=True),
          reads=["tokp", "ps4"], writes=["ps4"])
    S.run("DVE", lambda h: h.tensor_copy(s2t, C.ps[4][0:64, 0:64]),
          reads=["ps4", "hdb0"], writes=["hdb0"])
    S.run("DVE", lambda h: h.reduce_max(C.srow[0:64, 15:16], s2t, axis=AX.X),
          reads=["hdb0", "srow"], writes=["srow"])
    S.run("DVE", lambda h: h.tensor_scalar(C.srow[0:64, 14:15], C.srow[0:64, 15:16], -1.0, None,
                                           ALU.mult, ALU.bypass),
          reads=["srow"], writes=["srow"])
    S.run("ACT", lambda h: h.activation(out=s2t, in_=s2t, func=AF.Exp, bias=C.srow[0:64, 14:15]),
          reads=["hdb0", "srow"], writes=["hdb0"])
    S.run("DVE", lambda h: h.tensor_reduce(C.srow[0:64, 15:16], s2t, axis=AX.X, op=ALU.add),
          reads=["hdb0", "srow"], writes=["srow"])
    S.run("DVE", lambda h: h.reciprocal(C.srow[0:64, 14:15], C.srow[0:64, 15:16]),
          reads=["srow"], writes=["srow"])
    S.run("DVE", lambda h: h.tensor_scalar(s2t, s2t, C.srow[0:64, 14:15], None, ALU.mult, ALU.bypass),
          reads=["hdb0", "srow"], writes=["hdb0"])

    s3T = C.scr if s == 0 else C.upool
    s3Tk = ["scrA", "scrB", "scrC", "scrD"] if s == 0 else ["upool"]
    for jc in range(32):
        pb = C.ps[2 + (jc % 2)]
        S.run("PE", lambda h, jc=jc, pb=pb: h.transpose(
            pb[:, 0:64], C.qsw[0:64, 128 * jc:128 * jc + 128], IDENT[0:64, 0:64]),
            reads=["qsw", "ident", f"ps{2 + (jc % 2)}"], writes=[f"ps{2 + (jc % 2)}"])
        S.run("DVE", lambda h, jc=jc, pb=pb: h.tensor_copy(s3T[:, 64 * jc:64 * jc + 64], pb[:, 0:64]),
            reads=[f"ps{2 + (jc % 2)}"] + s3Tk, writes=s3Tk)

    def mm_s3v(h):
        last = None
        for jc in range(32):
            last = h.matmul(C.ps[5][0:64, 0:64], s3T[:, 64 * jc:64 * jc + 64],
                            C.fw[jc % 2][:, 128 * (jc // 2) + 64 * s:128 * (jc // 2) + 64 * s + 64],
                            start=(jc == 0), stop=(jc == 31))
        return last
    S.run("PE", mm_s3v, reads=s3Tk + ["fw0", "fw1", "ps5"], writes=["ps5"])
    s3vt = C.hdb[0:64, 64:128]
    S.run("DVE", lambda h: h.tensor_copy(s3vt, C.ps[5][0:64, 0:64]),
          reads=["ps5", "hdb0"], writes=["hdb0"])

    _l1_pinv(nc, S, C, s2t, s3vt)

    def mm_r1b(h):
        last = None
        for c in range(8):
            last = h.matmul(C.ps[5][0:8, :], C.SEL8[0:64, 8 * c:8 * c + 8],
                            U1[:, 512 * c:512 * c + 512], start=(c == 0), stop=(c == 7))
        return last
    S.run("PE", mm_r1b, reads=U1k + ["sel8", "ps5"], writes=["ps5"])
    S.run("DVE", lambda h: h.reciprocal(C.hdb[0:8, 512:1024], C.ps[5][0:8, 0:512]),
          reads=["ps5", "hdb1"], writes=["hdb1"])

    wpt = C.hdb[0:64, 128:192]
    for c in range(8):
        S.run("PE", lambda h, c=c: h.matmul(C.ps[7][0:64, :], wpt, U1[:, 512 * c:512 * c + 512],
                                            start=True, stop=True),
              reads=["hdb0", "wp_h", "ps7"] + U1k, writes=["ps7"])
        S.run("PE", lambda h, c=c: h.matmul(C.ps[6][0:64, :], C.SEL8T[0:8, 64 * c:64 * c + 64],
                                            C.hdb[0:8, 512:1024], start=True, stop=True),
              reads=["hdb1", "sel8t", "ps6"], writes=["ps6"])
        S.run("DVE", lambda h, c=c: h.tensor_copy(C.vg[0:64, 512 * c:512 * c + 512], C.ps[6][0:64, :]),
              reads=["ps6", "vg"], writes=["vg"])
        S.run("DVE", lambda h, c=c: h.tensor_tensor(
            C.vg[0:64, 512 * c:512 * c + 512], C.vg[0:64, 512 * c:512 * c + 512],
            C.ps[7][0:64, :], ALU.mult),
            reads=["ps7", "vg"], writes=["vg"])
    S.dma("POOL", lambda h, head=head: h.dma_start(
        out=ap["att_sp"][64 * head:64 * head + 64, :], in_=C.vg[0:64, 0:4096]),
        reads=["vg"], writes=[f"att_{head}"], slot="atto")


def _l1_pinv(nc, S, C, s2t, s3vt):
    ONES = C.ONES
    IDENT = C.IDENT
    z = C.ysb[0:64, 0:64]
    zT = C.ysb[0:64, 64:128]
    xzs = C.ysb[0:64, 128:192]
    xzT = C.ysb[0:64, 192:256]
    tts = C.ysb[0:64, 256:320]
    s2T = C.ysb[0:64, 320:384]
    S.run("PE", lambda h: h.transpose(C.ps[4][0:64, 0:64], s2t, IDENT[0:64, 0:64]),
          reads=["hdb0", "ident", "ps4"], writes=["ps4"])
    S.run("DVE", lambda h: h.tensor_copy(s2T, C.ps[4][0:64, 0:64]),
          reads=["ps4", "ysb"], writes=["ysb"])
    S.run("DVE", lambda h: h.tensor_reduce(C.srow[0:64, 15:16], s2t, axis=AX.X, op=ALU.add,
                                           apply_absolute_value=True),
          reads=["hdb0", "srow"], writes=["srow"])
    S.run("PE", lambda h: h.transpose(C.ps[4][0:1, 0:64], C.srow[0:64, 15:16], C.IDENT[0:64, 0:64]),
          reads=["srow", "ident", "ps4"], writes=["ps4"])
    S.run("DVE", lambda h: h.reduce_max(C.srow[0:1, 14:15], C.ps[4][0:1, 0:64], axis=AX.X),
          reads=["ps4", "srow"], writes=["srow"])
    S.run("PE", lambda h: h.matmul(C.ps[4][0:1, 0:64], ONES[0:64, 0:1], s2t, start=True, stop=True),
          reads=["hdb0", "ones", "ps4"], writes=["ps4"])
    S.run("DVE", lambda h: h.reduce_max(C.srow[0:1, 15:16], C.ps[4][0:1, 0:64], axis=AX.X,
                                        apply_absolute_value=True),
          reads=["ps4", "srow"], writes=["srow"])
    S.run("DVE", lambda h: h.tensor_tensor(C.srow[0:1, 13:14], C.srow[0:1, 14:15], C.srow[0:1, 15:16],
                                           ALU.mult),
          reads=["srow"], writes=["srow"])
    S.run("DVE", lambda h: h.reciprocal(C.srow[0:1, 12:13], C.srow[0:1, 13:14]),
          reads=["srow"], writes=["srow"])
    S.run("PE", lambda h: h.matmul(C.ps[4][0:64, 0:1], ONES[0:1, 0:64], C.srow[0:1, 12:13],
                                   start=True, stop=True),
          reads=["srow", "ones", "ps4"], writes=["ps4"])
    S.run("DVE", lambda h: h.tensor_copy(C.srow[0:64, 11:12], C.ps[4][0:64, 0:1]),
          reads=["ps4", "srow"], writes=["srow"])
    S.run("DVE", lambda h: h.tensor_scalar(z, s2T, C.srow[0:64, 11:12], None, ALU.mult, ALU.bypass),
          reads=["ysb", "srow"], writes=["ysb"])
    for it in range(6):
        S.run("PE", lambda h: h.matmul(C.ps[4][0:64, 0:64], s2T, z, start=True, stop=True),
              reads=["ysb", "ps4"], writes=["ps4"])
        S.run("DVE", lambda h: h.tensor_copy(xzs, C.ps[4][0:64, 0:64]),
              reads=["ps4", "ysb"], writes=["ysb"])
        S.run("PE", lambda h: h.transpose(C.ps[4][0:64, 0:64], xzs, IDENT[0:64, 0:64]),
              reads=["ident", "ps4", "ysb"], writes=["ps4"])
        S.run("DVE", lambda h: h.tensor_copy(xzT, C.ps[4][0:64, 0:64]),
              reads=["ps4", "ysb"], writes=["ysb"])
        S.run("DVE", lambda h: h.scalar_tensor_tensor(tts, IDENT[0:64, 0:64], 7.0, xzs,
                                                      ALU.mult, ALU.subtract),
              reads=["ident", "ysb"], writes=["ysb"])
        S.run("PE", lambda h: h.matmul(C.ps[4][0:64, 0:64], xzT, tts, start=True, stop=True),
              reads=["ysb", "ps4"], writes=["ps4"])
        S.run("DVE", lambda h: h.scalar_tensor_tensor(tts, IDENT[0:64, 0:64], 15.0, C.ps[4][0:64, 0:64],
                                                      ALU.mult, ALU.subtract),
              reads=["ident", "ps4", "ysb"], writes=["ysb"])
        S.run("PE", lambda h: h.matmul(C.ps[4][0:64, 0:64], xzT, tts, start=True, stop=True),
              reads=["ysb", "ps4"], writes=["ps4"])
        S.run("DVE", lambda h: h.scalar_tensor_tensor(tts, IDENT[0:64, 0:64], 13.0, C.ps[4][0:64, 0:64],
                                                      ALU.mult, ALU.subtract),
              reads=["ident", "ps4", "ysb"], writes=["ysb"])
        S.run("PE", lambda h: h.transpose(C.ps[4][0:64, 0:64], z, IDENT[0:64, 0:64]),
              reads=["ident", "ps4", "ysb"], writes=["ps4"])
        S.run("DVE", lambda h: h.tensor_copy(zT, C.ps[4][0:64, 0:64]),
              reads=["ps4", "ysb"], writes=["ysb"])
        S.run("PE", lambda h: h.matmul(C.ps[4][0:64, 0:64], zT, tts, start=True, stop=True),
              reads=["ysb", "ps4"], writes=["ps4"])
        S.run("DVE", lambda h: h.tensor_scalar(z, C.ps[4][0:64, 0:64], 0.25, None, ALU.mult, ALU.bypass),
              reads=["ps4", "ysb"], writes=["ysb"])
    S.run("PE", lambda h: h.transpose(C.ps[4][0:64, 0:64], z, IDENT[0:64, 0:64]),
          reads=["ident", "ps4", "ysb"], writes=["ps4"])
    S.run("DVE", lambda h: h.tensor_copy(zT, C.ps[4][0:64, 0:64]),
          reads=["ps4", "ysb"], writes=["ysb"])
    S.run("PE", lambda h: h.matmul(C.ps[4][0:64, 0:64], zT, s3vt, start=True, stop=True),
          reads=["ysb", "hdb0", "ps4"], writes=["ps4"])
    S.run("DVE", lambda h: h.tensor_copy(C.hdb[0:64, 128:192], C.ps[4][0:64, 0:64]),
          reads=["ps4", "hdb0"], writes=["hdb0", "wp_h"])


# version 05 (final)
# speedup vs baseline: 1.2096x; 1.0850x over previous
import numpy as np
import ml_dtypes
from contextlib import ExitStack

import concourse.bass as bass
import concourse.mybir as mybir
from concourse.bass_utils import run_bass_kernel_spmd
from concourse.replica_groups import maybe_share_collective_output_space
from concourse.bass_isa import ReduceOp

F32 = mybir.dt.float32
I32 = mybir.dt.int32
AF = mybir.ActivationFunctionType
ALU = mybir.AluOpType
AX = mybir.AxisListType

NCORES = 8
D = 1024
T = 4096
DC = D // 128
LSEG = 512
NSEG = T // LSEG
H = 16
DK = 64
HID = 3072
HC = HID // 128
RG8 = [[0, 1, 2, 3, 4, 5, 6, 7]]
C127 = float(np.float32(1.0 / 127.0))

ENGS = ("PE", "ACT", "DVE", "POOL", "SP")
_BA = {"PE": "tensor", "ACT": "scalar", "DVE": "vector", "POOL": "gpsimd", "SP": "sync"}

DEBUG = False
_DBG_SHAPES = {}
BUILD_STAGE = 2
ATT_SUB = 7
B_STAGE = 8


class Sched:

    def __init__(self, nc, block, ctx):
        self.nc, self.block, self.ctx = nc, block, ctx
        self.psem = {e: ctx.enter_context(nc.semaphore(f"P_{e}")) for e in ENGS}
        self.pcnt = {e: 0 for e in ENGS}
        self.drain_at = {e: 0 for e in ENGS}
        self.synced = {}
        self.dma_synced = {}
        self.writer = {}
        self.readers = {}
        self.dma_sem = {}

    def _slot(self, slot):
        if slot not in self.dma_sem:
            sem = self.ctx.enter_context(self.nc.semaphore(f"dma_{slot}"))
            self.dma_sem[slot] = (sem, [0])
        return self.dma_sem[slot]

    def _wait(self, eng, h, tok):
        if tok[0] == "E":
            _, peng, cnt = tok
            if peng == eng:
                if eng != "PE" and self.drain_at[eng] < cnt:
                    h.drain()
                    self.drain_at[eng] = self.pcnt[eng]
            else:
                key = (eng, peng)
                if self.synced.get(key, 0) < cnt:
                    h.wait_ge(self.psem[peng], cnt)
                    self.synced[key] = cnt
        else:
            _, slot, cnt16 = tok
            sem, _ = self._slot(slot)
            key = (eng, slot)
            if self.dma_synced.get(key, 0) < cnt16:
                h.wait_ge(sem, cnt16)
                self.dma_synced[key] = cnt16

    def _deps(self, reads, writes):
        toks = []
        for b in reads:
            w = self.writer.get(b)
            if w is not None:
                toks.append(w)
        for b in writes:
            w = self.writer.get(b)
            rd = self.readers.get(b, ())
            if w is not None and not (b.startswith("ps") and rd):
                toks.append(w)
            toks.extend(rd)
        best = {}
        for t in toks:
            k = t[:2]
            if k not in best or best[k][2] < t[2]:
                best[k] = t
        return list(best.values())

    def _commit(self, tok, reads, writes):
        for b in writes:
            self.writer[b] = tok
            self.readers[b] = []
        for b in reads:
            self.readers.setdefault(b, []).append(tok)

    def run(self, eng, emit, reads=(), writes=()):
        deps = self._deps(reads, writes)
        sem = self.psem[eng]

        def body(h):
            for t in deps:
                self._wait(eng, h, t)
            last = emit(h)
            assert last is not None, "emit() must return its last instruction"
            last.then_inc(sem, 1)

        getattr(self.block, _BA[eng])(body)
        self.pcnt[eng] += 1
        tok = ("E", eng, self.pcnt[eng])
        self._commit(tok, reads, writes)
        return tok

    def dma(self, eng, emit, reads=(), writes=(), slot=None, serialize=True):
        assert slot is not None
        deps = self._deps(reads, writes)
        sem, cnt = self._slot(slot)
        cnt[0] += 16
        target = cnt[0]
        if serialize and target > 16:
            deps = deps + [("D", slot, target - 16)]

        def body(h):
            for t in deps:
                self._wait(eng, h, t)
            emit(h).then_inc(sem, 16)

        getattr(self.block, _BA[eng])(body)
        tok = ("D", slot, target)
        self._commit(tok, reads, writes)
        return tok

    def wait_tokens(self, eng, toks):
        def body(h):
            for t in toks:
                self._wait(eng, h, t)
        getattr(self.block, _BA[eng])(body)



def _l0_mask():
    t = np.arange(512)
    gp, rem = t // 256, t % 256
    g2, r = rem // 32, rem % 32
    lh = 16 * r + 2 * g2 + gp
    return (lh[None, :] >= lh[:, None]).astype(np.float32)


def _prep_inputs(inputs):
    x = np.ascontiguousarray(np.asarray(inputs["L0_attn_x"], dtype=np.float32))
    ident = np.eye(128, dtype=np.float32)
    ones = np.ones((128, 128), dtype=np.float32)
    dup64 = np.zeros((64, 128), dtype=np.float32)
    dup64[np.arange(64), np.arange(64)] = 1.0
    dup64[np.arange(64), 64 + np.arange(64)] = 1.0
    swap = np.zeros((128, 128), dtype=np.float32)
    swap[np.arange(128), np.arange(128) ^ 64] = 1.0
    dupswap = np.zeros((128, 128), dtype=np.float32)
    dupswap[np.arange(128), np.arange(128)] = 1.0
    dupswap[np.arange(128), np.arange(128) ^ 64] = 1.0
    M0 = np.asarray(inputs["L0_attn_memory0"], np.float32).reshape(H, DK, DK)
    Mdup = np.zeros((128, H * 64), dtype=np.float32)
    for h in range(H):
        Mdup[0:64, 64 * h:64 * h + 64] = M0[h]
        Mdup[64:128, 64 * h:64 * h + 64] = M0[h]
    Z0 = np.asarray(inputs["L0_attn_z0"], np.float32).reshape(H, DK)
    Zdup = np.zeros((128, H), dtype=np.float32)
    for h in range(H):
        Zdup[0:64, h] = Z0[h]
        Zdup[64:128, h] = Z0[h]
    betas = np.asarray(inputs["L0_attn_betas"], np.float32).reshape(H, DK)
    sel8 = np.zeros((64, 64), dtype=np.float32)
    for c in range(8):
        sel8[:, 8 * c + c] = 1.0
    sel8t = np.zeros((8, 512), dtype=np.float32)
    for c in range(8):
        sel8t[c, 64 * c:64 * c + 64] = 1.0
    base = {
        "ident": ident, "ones": ones, "dup64": dup64, "swapm": swap, "dupswap": dupswap,
        "sel8": sel8, "sel8t": sel8t,
        "mask01": np.ascontiguousarray(_l0_mask().astype(ml_dtypes.bfloat16)),
        "betasc": np.ascontiguousarray(betas.T.copy()),
        "Mdup0": np.ascontiguousarray(Mdup),
        "Zdup0": np.ascontiguousarray(Zdup),
        "Wq0": np.ascontiguousarray(np.asarray(inputs["L0_attn_Wq"], np.float32)),
        "Wk0": np.ascontiguousarray(np.asarray(inputs["L0_attn_Wk"], np.float32)),
        "Wv0": np.ascontiguousarray(np.asarray(inputs["L0_attn_Wv"], np.float32)),
        "Wo0": np.ascontiguousarray(np.asarray(inputs["L0_attn_Wo"], np.float32)),
        "wgT0": np.ascontiguousarray(np.asarray(inputs["L0_ffn_w_gate"], np.int32).T),
        "wuT0": np.ascontiguousarray(np.asarray(inputs["L0_ffn_w_up"], np.int32).T),
        "wdT0": np.ascontiguousarray(np.asarray(inputs["L0_ffn_w_down"], np.int32).T),
        "Wq1": np.ascontiguousarray(np.asarray(inputs["L1_attn_Wq"], np.float32)),
        "Wk1": np.ascontiguousarray(np.asarray(inputs["L1_attn_Wk"], np.float32)),
        "Wv1": np.ascontiguousarray(np.asarray(inputs["L1_attn_Wv"], np.float32)),
        "Wo1": np.ascontiguousarray(np.asarray(inputs["L1_attn_Wo"], np.float32)),
        "bo1": np.ascontiguousarray(np.asarray(inputs["L1_attn_bo"], np.float32).reshape(1, D)),
        "bo1fm": np.ascontiguousarray(np.asarray(inputs["L1_attn_bo"], np.float32).reshape(DC, 128).T.copy()),
        "wg1": np.ascontiguousarray(np.asarray(inputs["L1_ffn_w_gate"], np.float32)),
        "wu1": np.ascontiguousarray(np.asarray(inputs["L1_ffn_w_up"], np.float32)),
        "wd1": np.ascontiguousarray(np.asarray(inputs["L1_ffn_w_down"], np.float32)),
    }
    maps = []
    for b in range(NCORES):
        m = dict(base)
        m["xT"] = np.ascontiguousarray(x[b].T)
        maps.append(m)
    return maps


class Ctx:
    pass


def _hbm(nc, C):
    C.ap = {}

    def din(name, shape, dtype=F32):
        C.ap[name] = nc.dram_tensor(name, list(shape), dtype, kind="ExternalInput").ap()

    def dsc(name, shape, dtype=F32):
        C.ap[name] = nc.dram_tensor(name, list(shape), dtype, kind="Internal").ap()

    din("xT", (D, T))
    for nm, sh in (("ident", (128, 128)), ("ones", (128, 128)), ("dup64", (64, 128)),
                   ("swapm", (128, 128)), ("dupswap", (128, 128)),
                   ("betasc", (64, H)), ("Mdup0", (128, H * 64)), ("Zdup0", (128, H)),
                   ("bo1", (1, D)), ("bo1fm", (128, DC)), ("sel8", (64, 64)),
                   ("sel8t", (8, 512))):
        din(nm, sh)
    din("mask01", (512, 512), mybir.dt.bfloat16)
    for w in ("Wq0", "Wk0", "Wv0", "Wo0", "Wq1", "Wk1", "Wv1", "Wo1"):
        din(w, (D, D))
    din("wgT0", (D, HID), I32)
    din("wuT0", (D, HID), I32)
    din("wdT0", (HID, D), I32)
    din("wg1", (HID, D))
    din("wu1", (HID, D))
    din("wd1", (D, HID))
    C.out_fm = nc.dram_tensor("out_fm", [D, T], F32, kind="ExternalOutput").ap()
    for nm, sh in (("wgT0_b", (D, HID)), ("wuT0_b", (D, HID)), ("wdT0_b", (HID, D)),
                   ("Wq0_h", (D, D)), ("Wq0_l", (D, D)), ("Wk0_h", (D, D)), ("Wk0_l", (D, D)),
                   ("Wv0_h", (D, D)), ("Wv0_l", (D, D)),
                   ("Wq1_h", (D, D)), ("Wq1_l", (D, D)), ("Wk1_h", (D, D)), ("Wk1_l", (D, D)),
                   ("Wv1_h", (D, D)), ("Wv1_l", (D, D)),
                   ("wgT1_h", (D, HID)), ("wgT1_l", (D, HID)),
                   ("wuT1_h", (D, HID)), ("wuT1_l", (D, HID)),
                   ("wdT1_h", (HID, D)), ("wdT1_l", (HID, D)),
                   ("oqh_sp", (HID, T)), ("oql_sp", (HID, T)),
                   ("xn1h_sp", (D, T)), ("xn1l_sp", (D, T))):
        dsc(nm, sh, mybir.dt.bfloat16)
    for nm, sh in (("wgT0_f", (D, HID)), ("wuT0_f", (D, HID)), ("wdT0_f", (HID, D)),
                   ("g_sp", (D, T)), ("x1_sp", (D, T)), ("xn1_sp", (D, T)), ("att_sp", (D, T)),
                   ("up_sp", (HID, T)), ("gate_sp", (HID, T)), ("t_sp", (HID, T)),
                   ("upq_sp", (HID, T)), ("u_sp", (HID, T)),
                   ("wgT1_f", (D, HID)), ("wuT1_f", (D, HID)), ("wdT1_f", (HID, D))):
        dsc(nm, sh)
    space = maybe_share_collective_output_space("AllReduce", RG8)
    for i in range(4):
        dsc(f"cc_in{i}", (1, 8))
        C.ap[f"cc_out{i}"] = nc.dram_tensor(f"cc_out{i}", [1, 8], F32, kind="Internal",
                                            addr_space=space).ap()
    C.dbg = {}
    if DEBUG:
        for name, sh in _DBG_SHAPES.items():
            C.dbg[name] = nc.dram_tensor("dbg_" + name, list(sh), F32, kind="ExternalOutput").ap()


def _dbg(S, C, name, src_ap, reads):
    if DEBUG and name in C.dbg:
        S.dma("POOL", lambda h: h.dma_start(out=C.dbg[name][:, :], in_=src_ap),
              reads=list(reads), writes=[], slot="dbgw")


def _acc(S, bank_ap, pairs, reads, writes):
    def emit(h):
        last = None
        n = len(pairs)
        for i, (lhsT, rhs) in enumerate(pairs):
            last = h.matmul(bank_ap, lhsT, rhs, start=(i == 0), stop=(i == n - 1))
        return last
    return S.run("PE", emit, reads=reads, writes=writes)


def build_program():
    nc = bass.Bass("TRN2", target_bir_lowering=False)
    C = Ctx()
    _hbm(nc, C)

    with ExitStack() as ctx:
        E = ctx.enter_context
        C.consts = E(nc.sbuf_tensor("consts", [128, 640], F32))
        C.selc = E(nc.sbuf_tensor("selc", [128, 64 + 512], F32))
        C.mask = E(nc.sbuf_tensor("mask", [128, 4 * 512], mybir.dt.bfloat16))
        C.gates = E(nc.sbuf_tensor("gates", [64, 3 * H], F32))
        C.Zd = E(nc.sbuf_tensor("Zd", [128, H], F32))
        C.Md = E(nc.sbuf_tensor("Md", [128, H * 64], F32))
        C.row = E(nc.sbuf_tensor("row", [1, 2048], F32))
        C.srow = E(nc.sbuf_tensor("srow", [128, 16], F32))
        C.cb = E(nc.sbuf_tensor("cb", [128, 8], F32))
        C.xtile = E(nc.sbuf_tensor("xtile", [128, DC * 512], F32))
        C.xn = E(nc.sbuf_tensor("xn", [128, 2 * DC * 512], mybir.dt.bfloat16))
        C.stg = E(nc.sbuf_tensor("stg", [128, DC * 512], F32))
        C.qg = E(nc.sbuf_tensor("qg", [128, H * 256], F32))
        C.kg = E(nc.sbuf_tensor("kg", [128, H * 256], F32))
        C.vg = E(nc.sbuf_tensor("vg", [128, H * 256], F32))
        C.qsw = E(nc.sbuf_tensor("qsw", [128, H * 256], F32))
        C.upool = E(nc.sbuf_tensor("upool", [128, 4 * 512], F32))
        C.tokp = E(nc.sbuf_tensor("tokp", [128, 8 * 128], F32))
        C.scr = E(nc.sbuf_tensor("scr", [128, 4 * 512], F32))
        C.pw = [E(nc.sbuf_tensor(f"pw{i}", [128, 2 * DC * 128], mybir.dt.bfloat16)) for i in range(2)]
        C.fw = [E(nc.sbuf_tensor(f"fw{i}", [128, HID], F32)) for i in range(2)]
        C.fwb = E(nc.sbuf_tensor("fwb", [128, 2048], mybir.dt.bfloat16))
        C.fwi = E(nc.sbuf_tensor("fwi", [128, 2048], I32))
        C.hdb = E(nc.sbuf_tensor("hdb", [128, 2 * 2 * 512], mybir.dt.bfloat16))
        C.ysb = E(nc.sbuf_tensor("ysb", [128, 2 * 512], F32))
        C.ps = [E(nc.psum_tensor(f"ps{i}", [128, 512], F32)) for i in range(8)]

        block = E(nc.Block())
        S = Sched(nc, block, ctx)

        C.IDENT = C.consts[:, 0:128]
        C.ONES = C.consts[:, 128:256]
        C.DUP64 = C.consts[0:64, 256:384]
        C.SWAP = C.consts[:, 384:512]
        C.DUPSW = C.consts[:, 512:640]
        C.SEL8 = C.selc[0:64, 0:64]
        C.SEL8T = C.selc[0:8, 64:576]
        C.scrA = C.scr[:, 0:512]
        C.scrB = C.scr[:, 512:1024]
        C.scrC = C.scr[:, 1024:1536]
        C.scrD = C.scr[:, 1536:2048]

        _init(nc, S, C)
        _l0_weight_convert(nc, S, C)
        _split_weights(nc, S, C)
        _phase_a(nc, S, C)

        if B_STAGE >= 1:
            _phase_b(nc, S, C)
        else:
            for dc in range(DC):
                for tt in range(NSEG):
                    S.dma("POOL", lambda h, dc=dc, tt=tt: h.dma_start(
                        out=C.out_fm[128 * dc:128 * dc + 128, 512 * tt:512 * tt + 512],
                        in_=C.ap["x1_sp"][128 * dc:128 * dc + 128, 512 * tt:512 * tt + 512]),
                        reads=[f"x1sp_{dc}_{tt}"], writes=[], slot="outw", serialize=False)
        sem, cnt = S._slot("outw")
        S.wait_tokens("POOL", [("D", "outw", cnt[0])])

        if DEBUG:
            for nm in _DBG_SHAPES:
                S.dma("POOL", lambda h, nm=nm: h.dma_start(
                    out=C.dbg[nm][:, :],
                    in_=(C.srow[:, :] if nm == "srow" else C.ap[nm][:, :])),
                    reads=[], writes=[], slot="dbgw")
            sem2, cnt2 = S._slot("dbgw")
            S.wait_tokens("POOL", [("D", "dbgw", cnt2[0])])

    return nc


def _init(nc, S, C):
    ap = C.ap
    S.dma("SP", lambda h: h.dma_start(out=C.consts[:, 0:128], in_=ap["ident"][:, :]), writes=["ident"], slot="init")
    S.dma("SP", lambda h: h.dma_start(out=C.consts[:, 128:256], in_=ap["ones"][:, :]), writes=["ones"], slot="init")
    S.dma("SP", lambda h: h.dma_start(out=C.consts[0:64, 256:384], in_=ap["dup64"][:, :]), writes=["dup64"], slot="init")
    S.dma("SP", lambda h: h.dma_start(out=C.consts[:, 384:512], in_=ap["swapm"][:, :]), writes=["swapm"], slot="init")
    S.dma("SP", lambda h: h.dma_start(out=C.consts[:, 512:640], in_=ap["dupswap"][:, :]), writes=["dupsw"], slot="init")
    S.dma("SP", lambda h: h.dma_start(out=C.selc[0:64, 0:64], in_=ap["sel8"][:, :]), writes=["sel8"], slot="init")
    S.dma("SP", lambda h: h.dma_start(out=C.selc[0:8, 64:576], in_=ap["sel8t"][:, :]), writes=["sel8t"], slot="init")
    for jc in range(4):
        S.dma("SP", lambda h, jc=jc: h.dma_start(
            out=C.mask[:, 512 * jc:512 * jc + 512], in_=ap["mask01"][128 * jc:128 * jc + 128, :]),
            writes=["mask"], slot="init")
    S.dma("SP", lambda h: h.dma_start(out=C.gates[:, 0:H], in_=ap["betasc"][:, :]), writes=["betas"], slot="init")
    S.dma("SP", lambda h: h.dma_start(out=C.Zd[:, :], in_=ap["Zdup0"][:, :]), writes=["Zd"], slot="init")
    S.dma("SP", lambda h: h.dma_start(out=C.Md[:, :], in_=ap["Mdup0"][:, :]), writes=["Md"], slot="init")
    S.run("DVE", lambda h: h.memset(C.cb[:, 2:3], 1e-5), reads=[], writes=["cb_eps"])
    S.run("DVE", lambda h: h.memset(C.srow[0:1, 8:16], 0.0), reads=[], writes=["srow", "srow_init"])
    S.run("ACT", lambda h: h.activation(out=C.gates[:, H:2 * H], in_=C.gates[:, 0:H], func=AF.Sigmoid),
          reads=["betas"], writes=["gate"])
    S.run("DVE", lambda h: h.tensor_scalar(C.gates[:, 2 * H:3 * H], C.gates[:, H:2 * H], -1.0, 1.0, ALU.mult, ALU.add),
          reads=["gate"], writes=["gate1m"])


def _l0_weight_convert(nc, S, C):
    for srcn, dst, nchunk, width in (("wgT0", "wgT0_b", DC, HID),
                                     ("wuT0", "wuT0_b", DC, HID),
                                     ("wdT0", "wdT0_b", HC, D)):
        for c in range(nchunk):
            for o in range(0, width, 2048):
                w2 = min(2048, width - o)
                S.dma("SP", lambda h, srcn=srcn, c=c, o=o, w2=w2: h.dma_start(
                    out=C.fwi[:, 0:w2], in_=C.ap[srcn][128 * c:128 * c + 128, o:o + w2]),
                    reads=["fwi"], writes=["fwi"], slot="cvt_i")
                S.run("DVE", lambda h, o=o, w2=w2: h.tensor_copy(
                    C.xn[:, o:o + w2], C.fwi[:, 0:w2]),
                    reads=["fwi"] + XN, writes=XN)
            S.dma("POOL", lambda h, dst=dst, c=c, width=width: h.dma_start(
                out=C.ap[dst][128 * c:128 * c + 128, :],
                in_=C.xn[:, 0:width]),
                reads=XN, writes=[f"{dst}_c{c}"], slot="cvto")


def _split_weights(nc, S, C):
    for mat in ("Wq0", "Wk0", "Wv0", "Wq1", "Wk1", "Wv1"):
        for c in range(DC):
            S.dma("SP", lambda h, mat=mat, c=c: h.dma_start(
                out=C.fw[0][:, 0:D], in_=C.ap[mat][128 * c:128 * c + 128, :]),
                reads=["fw0"], writes=["fw0"], slot="spl_i")
            S.run("DVE", lambda h: h.tensor_copy(C.xn[:, 0:D], C.fw[0][:, 0:D]),
                  reads=["fw0"] + XN, writes=XN)
            S.run("DVE", lambda h: h.tensor_tensor(C.xn[:, D:2 * D], C.fw[0][:, 0:D],
                                                   C.xn[:, 0:D], ALU.subtract),
                  reads=["fw0"] + XN, writes=XN)
            S.dma("POOL", lambda h, mat=mat, c=c: h.dma_start(
                out=C.ap[mat + "_h"][128 * c:128 * c + 128, :], in_=C.xn[:, 0:D]),
                reads=XN, writes=[f"{mat}_h_c{c}"], slot="spl_oh")
            S.dma("POOL", lambda h, mat=mat, c=c: h.dma_start(
                out=C.ap[mat + "_l"][128 * c:128 * c + 128, :], in_=C.xn[:, D:2 * D]),
                reads=XN, writes=[f"{mat}_l_c{c}"], slot="spl_ol")


XN = [f"xn_{i}" for i in range(DC)]
XT = [f"xt_{i}" for i in range(DC)]
STG = [f"stg_{i}" for i in range(DC)]


def _load_xtile(S, C, tt):
    for dc in range(DC):
        S.dma("SP", lambda h, tt=tt, dc=dc: h.dma_start(
            out=C.xtile[:, 512 * dc:512 * dc + 512],
            in_=C.ap["xT"][128 * dc:128 * dc + 128, 512 * tt:512 * tt + 512]),
            writes=[f"xt_{dc}"], slot="xin")


def _ln_tile(S, C, chunk_sink=None):
    ONES = C.ONES
    _acc(S, C.ps[0][0:1, :],
         [(ONES[:, 0:1], C.xtile[:, 512 * dc:512 * dc + 512]) for dc in range(DC)],
         reads=XT + ["ones", "ps0"], writes=["ps0"])
    S.run("ACT", lambda h: h.activation(out=C.stg[:, :], in_=C.xtile[:, :], func=AF.Square),
          reads=XT + STG, writes=STG)
    _acc(S, C.ps[1][0:1, :],
         [(ONES[:, 0:1], C.stg[:, 512 * dc:512 * dc + 512]) for dc in range(DC)],
         reads=STG + ["ones", "ps1"], writes=["ps1"])
    mrow = C.row[0:1, 0:512]
    rrow = C.row[0:1, 512:1024]
    S.run("DVE", lambda h: h.tensor_scalar(mrow, C.ps[0][0:1, :], 1.0 / 1024.0, None, ALU.mult, ALU.bypass),
          reads=["ps0"], writes=["rowM"])
    S.run("DVE", lambda h: h.tensor_scalar(rrow, C.ps[1][0:1, :], 1.0 / 1024.0, None, ALU.mult, ALU.bypass),
          reads=["ps1"], writes=["rowR"])
    S.run("DVE", lambda h: h.tensor_tensor(C.row[0:1, 1024:1536], mrow, mrow, ALU.mult),
          reads=["rowM"], writes=["rowC"])
    S.run("DVE", lambda h: h.tensor_tensor(rrow, rrow, C.row[0:1, 1024:1536], ALU.subtract),
          reads=["rowR", "rowC"], writes=["rowR"])
    S.run("ACT", lambda h: h.activation(out=rrow, in_=rrow, func=AF.Sqrt, bias=C.cb[0:1, 2:3]),
          reads=["rowR", "cb_eps"], writes=["rowR"])
    S.run("DVE", lambda h: h.reciprocal(rrow, rrow), reads=["rowR"], writes=["rowR"])
    S.run("PE", lambda h: h.matmul(C.ps[0][:, :], ONES[0:1, 0:128], mrow, start=True, stop=True),
          reads=["rowM", "ones", "ps0"], writes=["ps0"])
    S.run("PE", lambda h: h.matmul(C.ps[1][:, :], ONES[0:1, 0:128], rrow, start=True, stop=True),
          reads=["rowR", "ones", "ps1"], writes=["ps1"])
    S.run("DVE", lambda h: h.tensor_copy(C.scrA, C.ps[0][:, :]),
          reads=["ps0", "scrA"], writes=["scrA"])
    S.run("DVE", lambda h: h.tensor_copy(C.scrB, C.ps[1][:, :]),
          reads=["ps1", "scrB"], writes=["scrB"])
    for dc in range(DC):
        S.run("DVE", lambda h, dc=dc: h.tensor_tensor(
            C.scrC, C.xtile[:, 512 * dc:512 * dc + 512], C.scrA, ALU.subtract),
            reads=[f"xt_{dc}", "scrA", "scrC"], writes=["scrC"])
        S.run("DVE", lambda h, dc=dc: h.tensor_tensor(C.scrC, C.scrC, C.scrB, ALU.mult),
            reads=["scrC", "scrB"], writes=["scrC"])
        S.run("DVE", lambda h, dc=dc: h.tensor_copy(C.xn[:, 512 * dc:512 * dc + 512], C.scrC),
            reads=["scrC"], writes=[f"xn_{dc}"])
        S.run("DVE", lambda h, dc=dc: h.tensor_tensor(
            C.xn[:, DC * 512 + 512 * dc:DC * 512 + 512 * dc + 512], C.scrC,
            C.xn[:, 512 * dc:512 * dc + 512], ALU.subtract),
            reads=["scrC", f"xn_{dc}"], writes=[f"xn_{dc}"])
        if chunk_sink is not None:
            chunk_sink(dc)


def _wcol_load_pair(S, C, mat, oc, ws, rkeys=()):
    dsth = C.pw[ws][:, 0:DC * 128].rearrange("p (dc c) -> p dc c", dc=DC, c=128)
    dstl = C.pw[ws][:, DC * 128:2 * DC * 128].rearrange("p (dc c) -> p dc c", dc=DC, c=128)
    srch = C.ap[mat + "_h"][:, 128 * oc:128 * oc + 128].rearrange("(dc p) c -> p dc c", dc=DC, p=128)
    srcl = C.ap[mat + "_l"][:, 128 * oc:128 * oc + 128].rearrange("(dc p) c -> p dc c", dc=DC, p=128)
    S.dma("SP", lambda h: h.dma_start(out=dsth, in_=srch),
          reads=list(rkeys), writes=[f"pw{ws}"], slot=f"pw{ws}")
    S.dma("SP", lambda h: h.dma_start(out=dstl, in_=srcl),
          reads=list(rkeys), writes=[f"pw{ws}"], slot=f"pw{ws}b")


def _wcol_load_single(S, C, mat, oc, ws, rkeys=()):
    dst = C.pw[ws][:, 0:DC * 128].rearrange("p (dc c) -> p dc c", dc=DC, c=128)
    srch = C.ap[mat][:, 128 * oc:128 * oc + 128].rearrange("(dc p) c -> p dc c", dc=DC, p=128)
    return S.dma("SP", lambda h: h.dma_start(out=dst, in_=srch),
                 reads=list(rkeys), writes=[f"pw{ws}"], slot=f"pw{ws}")


def _acc3(S, bank_ap, ws, C, reads, writes, bankkey=None):
    xh = C.xn[:, 0:DC * 512]
    xl = C.xn[:, DC * 512:2 * DC * 512]
    def emit(h):
        last = None
        n = 0
        for wofs, xref in ((0, xh), (0, xl), (DC * 128, xh)):
            for dc in range(DC):
                n += 1
                last = h.matmul(bank_ap, C.pw[ws][:, wofs + 128 * dc:wofs + 128 * dc + 128],
                                xref[:, 512 * dc:512 * dc + 512],
                                start=(n == 1), stop=(n == 3 * DC))
        return last
    return S.run("PE", emit, reads=reads, writes=writes)


def _acc2(S, bank_ap, ws, C, reads, writes):
    xh = C.xn[:, 0:DC * 512]
    xl = C.xn[:, DC * 512:2 * DC * 512]
    def emit(h):
        last = None
        n = 0
        for xref in (xh, xl):
            for dc in range(DC):
                n += 1
                last = h.matmul(bank_ap, C.pw[0][:, 128 * dc:128 * dc + 128],
                                xref[:, 512 * dc:512 * dc + 512],
                                start=(n == 1), stop=(n == 2 * DC))
        return last
    return S.run("PE", emit, reads=reads, writes=writes)


def _wcol_load(S, C, mat, oc, ws, rkeys=()):
    dst = C.fw[ws][:, 0:DC * 128].rearrange("p (dc c) -> p dc c", dc=DC, c=128)
    src = C.ap[mat][:, 128 * oc:128 * oc + 128].rearrange("(dc p) c -> p dc c", dc=DC, p=128)
    return S.dma("SP", lambda h: h.dma_start(out=dst, in_=src),
                 reads=list(rkeys), writes=[f"fw{ws}"], slot=f"fwv{ws}")


def _phase_a(nc, S, C):
    if BUILD_STAGE < 1:
        return
    for tt in range(NSEG):
        _load_xtile(S, C, tt)
        _ln_tile(S, C)
        def emit_gateup(hc):
            S.dma("SP", lambda h, hc=hc: h.dma_start(
                out=C.pw[0][:, 0:DC * 128].rearrange("p (dc c) -> p dc c", dc=DC, c=128),
                in_=C.ap["wgT0_b"][:, 128 * hc:128 * hc + 128].rearrange("(dc p) c -> p dc c", dc=DC, p=128)),
                reads=[f"wgT0_b_c{i}" for i in range(DC)], writes=["pw0"], slot="pw0")
            _acc2(S, C.ps[6][:, :], 0, C, reads=["pw0"] + XN + ["ps6"], writes=["ps6"])
            S.run("ACT", lambda h: h.activation(out=C.scrC, in_=C.ps[6][:, :], func=AF.Silu, scale=C127),
                  reads=["ps6", "scrC"], writes=["scrC"])
            S.dma("SP", lambda h, hc=hc: h.dma_start(
                out=C.pw[1][:, 0:DC * 128].rearrange("p (dc c) -> p dc c", dc=DC, c=128),
                in_=C.ap["wuT0_b"][:, 128 * hc:128 * hc + 128].rearrange("(dc p) c -> p dc c", dc=DC, p=128)),
                reads=[f"wuT0_b_c{i}" for i in range(DC)], writes=["pw1"], slot="pw1")
            def mm_up(h):
                last = None
                n = 0
                for xo in (0, DC * 512):
                    for dc in range(DC):
                        n += 1
                        last = h.matmul(C.ps[7][:, :], C.pw[1][:, 128 * dc:128 * dc + 128],
                                        C.xn[:, xo + 512 * dc:xo + 512 * dc + 512],
                                        start=(n == 1), stop=(n == 2 * DC))
                return last
            S.run("PE", mm_up, reads=["pw1"] + XN + ["ps7"], writes=["ps7"])
            S.run("DVE", lambda h: h.scalar_tensor_tensor(
                C.scrD, C.ps[7][:, :], C127, C.scrC, ALU.mult, ALU.mult),
                reads=["ps7", "scrC", "scrD"], writes=["scrD"])
            hh = C.hdb[:, 1024 * (hc % 2):1024 * (hc % 2) + 512]
            hl = C.hdb[:, 1024 * (hc % 2) + 512:1024 * (hc % 2) + 1024]
            S.run("DVE", lambda h, hh=hh: h.tensor_copy(hh, C.scrD),
                  reads=["scrD", f"hdb{hc % 2}"], writes=[f"hdb{hc % 2}"])
            S.run("DVE", lambda h, hh=hh, hl=hl: h.tensor_tensor(hl, C.scrD, hh, ALU.subtract),
                  reads=["scrD", f"hdb{hc % 2}"], writes=[f"hdb{hc % 2}"])
            S.dma("SP", lambda h, hc=hc: h.dma_start(
                out=C.fwb[:, 1024 * (hc % 2):1024 * (hc % 2) + 1024],
                in_=C.ap["wdT0_b"][128 * hc:128 * hc + 128, :]),
                reads=[f"wdT0_b_c{hc}", f"fwb{hc % 2}"], writes=[f"fwb{hc % 2}"], slot=f"fwd{hc % 2}")

        def emit_down(hc):
            hh = C.hdb[:, 1024 * (hc % 2):1024 * (hc % 2) + 512]
            hl = C.hdb[:, 1024 * (hc % 2) + 512:1024 * (hc % 2) + 1024]
            wrow = C.fwb[:, 1024 * (hc % 2):1024 * (hc % 2) + 1024]
            def mm_down(h, hc=hc):
                last = None
                for do in range(6):
                    last = h.matmul(C.ps[do][:, :], wrow[:, 128 * do:128 * do + 128], hh,
                                    start=(hc == 0), stop=False)
                    last = h.matmul(C.ps[do][:, :], wrow[:, 128 * do:128 * do + 128], hl,
                                    start=False, stop=(hc == HC - 1))
                return last
            S.run("PE", mm_down, reads=[f"fwb{hc % 2}", f"hdb{hc % 2}"] + [f"ps{i}" for i in range(6)],
                  writes=[f"ps{i}" for i in range(6)])
            for j, do in enumerate((6, 7)):
                pb = C.ps[6 + j]
                def mm_part(h, do=do, pb=pb):
                    h.matmul(pb[:, :], wrow[:, 128 * do:128 * do + 128], hh, start=True, stop=False)
                    return h.matmul(pb[:, :], wrow[:, 128 * do:128 * do + 128], hl, start=False, stop=True)
                S.run("PE", mm_part,
                      reads=[f"fwb{hc % 2}", f"hdb{hc % 2}", f"ps{6 + j}"], writes=[f"ps{6 + j}"])
                ysl = C.ysb[:, 512 * j:512 * j + 512]
                if hc == 0:
                    S.run("DVE", lambda h, ysl=ysl, pb=pb: h.tensor_copy(ysl, pb[:, :]),
                          reads=[f"ps{6 + j}", "ysb"], writes=["ysb"])
                else:
                    S.run("DVE", lambda h, ysl=ysl, pb=pb: h.tensor_tensor(ysl, ysl, pb[:, :], ALU.add),
                          reads=[f"ps{6 + j}", "ysb"], writes=["ysb"])

        for hc in range(HC + 1):
            if hc < HC:
                emit_gateup(hc)
            if hc >= 1:
                emit_down(hc - 1)
        for do in range(6):
            S.run("ACT", lambda h, do=do: h.activation(
                out=C.stg[:, 512 * do:512 * do + 512], in_=C.ps[do][:, :], func=AF.Silu, scale=C127),
                reads=[f"ps{do}"], writes=[f"stg_{do}"])
        for j in range(2):
            S.run("ACT", lambda h, j=j: h.activation(
                out=C.stg[:, 512 * (6 + j):512 * (6 + j) + 512], in_=C.ysb[:, 512 * j:512 * j + 512],
                func=AF.Silu, scale=C127),
                reads=["ysb"], writes=[f"stg_{6 + j}"])
        for do in range(DC):
            S.dma("POOL", lambda h, do=do, tt=tt: h.dma_start(
                out=C.ap["g_sp"][128 * do:128 * do + 128, 512 * tt:512 * tt + 512],
                in_=C.stg[:, 512 * do:512 * do + 512]),
                reads=[f"stg_{do}"], writes=[f"gsp_{do}_{tt}"], slot="gsp")

    if BUILD_STAGE < 2:
        return
    for sg in range(NSEG):
        _load_xtile(S, C, sg)
        _ln_tile(S, C)

        for mat, dst, nm in (("Wq0", C.qg, "qg"), ("Wk0", C.kg, "kg"), ("Wv0", C.vg, "vg")):
            for oc in range(DC):
                ws = oc % 2
                _wcol_load_pair(S, C, mat, oc, ws)
                _acc3(S, C.ps[oc % 2][:, :], ws, C,
                      reads=[f"pw{ws}"] + XN + [f"ps{oc % 2}"], writes=[f"ps{oc % 2}"])
                S.run("DVE", lambda h, oc=oc: h.tensor_copy(C.stg[:, 512 * oc:512 * oc + 512], C.ps[oc % 2][:, :]),
                    reads=[f"ps{oc % 2}"], writes=[f"stg_{oc}"])
            def gat(h, dst=dst):
                last = None
                for gp in range(2):
                    src = C.stg[64 * gp:64 * gp + 64, :].rearrange(
                        "p (g2 hh rr) -> p hh g2 rr", g2=8, hh=16, rr=32)
                    dd = dst[64 * gp:64 * gp + 64, :].rearrange(
                        "p (hh g2 rr) -> p hh g2 rr", hh=16, g2=8, rr=32)
                    last = h.tensor_copy(dd, src)
                return last
            S.run("DVE", gat, reads=STG, writes=[nm])

        for c8 in range(8):
            S.run("PE", lambda h, c8=c8: h.matmul(
                C.ps[2 + (c8 % 2)][:, :], C.SWAP, C.qg[:, 512 * c8:512 * c8 + 512], start=True, stop=True),
                reads=["qg", "swapm", f"ps{2 + (c8 % 2)}"], writes=[f"ps{2 + (c8 % 2)}"])
            S.run("DVE", lambda h, c8=c8: h.tensor_copy(C.qsw[:, 512 * c8:512 * c8 + 512], C.ps[2 + (c8 % 2)][:, :]),
                reads=[f"ps{2 + (c8 % 2)}"], writes=["qsw"])

        if DEBUG:
            _dbg(S, C, f"qg_{sg}", C.qg[:, :], ["qg"])
            _dbg(S, C, f"kg_{sg}", C.kg[:, :], ["kg"])
            _dbg(S, C, f"vg_{sg}", C.vg[:, :], ["vg"])
            _dbg(S, C, f"xn_{sg}", C.xn[:, :], XN)

        if ATT_SUB >= 1:
            for head in range(H):
                _l0_head(nc, S, C, sg, head)

        if DEBUG:
            _dbg(S, C, f"sfm_{sg}", C.stg[:, :], [f"attw_{hh}" for hh in range(H)])

        if ATT_SUB < 7:
            continue
        attr = [f"attw_{hh}" for hh in range(H)]
        for oc in range(DC):
            ws = oc % 2
            _wcol_load(S, C, "Wo0", oc, ws)
            _acc(S, C.ps[oc % 2][:, :],
                 [(C.fw[ws][:, 128 * dc:128 * dc + 128], C.stg[:, 512 * dc:512 * dc + 512]) for dc in range(DC)],
                 reads=[f"fw{ws}"] + attr + STG + [f"ps{oc % 2}"], writes=[f"ps{oc % 2}"])
            S.dma("SP", lambda h, oc=oc, sg=sg: h.dma_start(
                out=C.scrC, in_=C.ap["g_sp"][128 * oc:128 * oc + 128, 512 * sg:512 * sg + 512]),
                reads=[f"gsp_{oc}_{sg}", "scrC"], writes=["scrC"], slot="gbk")
            S.run("DVE", lambda h, oc=oc: h.tensor_tensor(
                C.scrD, C.ps[oc % 2][:, :], C.scrC, ALU.mult),
                reads=[f"ps{oc % 2}", "scrC", "scrD"], writes=["scrD"])
            S.run("DVE", lambda h, oc=oc: h.tensor_tensor(
                C.scrD, C.scrD, C.xtile[:, 512 * oc:512 * oc + 512], ALU.add),
                reads=["scrD", f"xt_{oc}"], writes=["scrD"])
            S.dma("POOL", lambda h, oc=oc, sg=sg: h.dma_start(
                out=C.ap["x1_sp"][128 * oc:128 * oc + 128, 512 * sg:512 * sg + 512], in_=C.scrD),
                reads=["scrD"], writes=[f"x1sp_{oc}_{sg}"], slot="x1o")


def _l0_head(nc, S, C, sg, head):
    sub = ATT_SUB
    hof = 256 * head
    qh = C.qg[:, hof:hof + 256]
    qsh = C.qsw[:, hof:hof + 256]
    kh = C.kg[:, hof:hof + 256]
    vh = C.vg[:, hof:hof + 256]
    ONES = C.ONES
    IDENT = C.IDENT

    S.run("DVE", lambda h: h.tensor_tensor(C.scrA[:, 0:256], qh, qh, ALU.mult),
          reads=["qg", "scrA"], writes=["scrA"])
    def mm_qq(h):
        h.matmul(C.ps[4][0:1, 0:256], ONES[0:64, 0:1], C.scrA[0:64, 0:256], start=True, stop=True)
        return h.matmul(C.ps[5][0:1, 0:256], ONES[64:128, 64:65], C.scrA[64:128, 0:256],
                        start=True, stop=True)
    S.run("PE", mm_qq, reads=["scrA", "ones", "ps4", "ps5"], writes=["ps4", "ps5"])
    S.run("DVE", lambda h: h.reduce_max(C.srow[0:1, 0:1], C.ps[4][0:1, 0:256], axis=AX.X),
          reads=["ps4"], writes=["srow_qq"])
    S.run("DVE", lambda h: h.reduce_max(C.srow[0:1, 6:7], C.ps[5][0:1, 0:256], axis=AX.X),
          reads=["ps5"], writes=["srow_qq2"])
    S.run("DVE", lambda h: h.tensor_tensor(C.scrA[:, 0:256], kh, kh, ALU.mult),
          reads=["kg", "scrA", "ps4", "ps5"], writes=["scrA"])
    def mm_kk(h):
        h.matmul(C.ps[4][0:1, 0:256], ONES[0:64, 0:1], C.scrA[0:64, 0:256], start=True, stop=True)
        return h.matmul(C.ps[5][0:1, 0:256], ONES[64:128, 64:65], C.scrA[64:128, 0:256],
                        start=True, stop=True)
    S.run("PE", mm_kk, reads=["scrA", "ones", "ps4", "ps5", "srow_qq", "srow_qq2"], writes=["ps4", "ps5"])
    S.run("DVE", lambda h: h.reduce_max(C.srow[0:1, 1:2], C.ps[4][0:1, 0:256], axis=AX.X),
          reads=["ps4"], writes=["srow_kk"])
    S.run("DVE", lambda h: h.reduce_max(C.srow[0:1, 7:8], C.ps[5][0:1, 0:256], axis=AX.X),
          reads=["ps5"], writes=["srow_kk2"])
    S.run("DVE", lambda h: h.tensor_tensor(C.srow[0:1, 0:1], C.srow[0:1, 0:1], C.srow[0:1, 6:7], ALU.max),
          reads=["srow_qq", "srow_qq2"], writes=["srow_qq"])
    S.run("DVE", lambda h: h.tensor_tensor(C.srow[0:1, 1:2], C.srow[0:1, 1:2], C.srow[0:1, 7:8], ALU.max),
          reads=["srow_kk", "srow_kk2"], writes=["srow_kk"])
    S.run("DVE", lambda h: h.tensor_tensor(C.srow[0:1, 2:3], C.srow[0:1, 0:1], C.srow[0:1, 1:2], ALU.add),
          reads=["srow_qq", "srow_kk"], writes=["srow_c"])
    S.run("DVE", lambda h: h.tensor_scalar(C.srow[0:1, 4:5], C.srow[0:1, 2:3], -0.0625, None, ALU.mult, ALU.bypass),
          reads=["srow_c"], writes=["srow_cn"])
    S.run("PE", lambda h: h.matmul(C.ps[4][:, 0:1], ONES[0:1, 0:128], C.srow[0:1, 4:5], start=True, stop=True),
          reads=["srow_cn", "ones", "ps4", "srow_kk"], writes=["ps4"])
    S.run("DVE", lambda h: h.tensor_copy(C.cb[:, 0:1], C.ps[4][:, 0:1]),
          reads=["ps4"], writes=["cb"])

    if sub < 2:
        return
    for jc in range(4):
        gpj, blk = jc // 2, jc % 2
        kslice = kh[64 * gpj:64 * gpj + 64, 128 * blk:128 * blk + 128]
        def mm_st(h, kslice=kslice, gpj=gpj):
            q_same = (qh if gpj == 0 else qsh)[64 * gpj:64 * gpj + 64, :]
            q_other = (qsh if gpj == 0 else qh)[64 * gpj:64 * gpj + 64, :]
            h.matmul(C.ps[2][:, 0:256], kslice, q_same, start=True, stop=True)
            return h.matmul(C.ps[3][:, 0:256], kslice, q_other, start=True, stop=True)
        S.run("PE", mm_st, reads=["qg", "qsw", "kg", "ps2", "ps3"], writes=["ps2", "ps3"])
        S.run("ACT", lambda h, jc=jc: h.activation(
            out=C.upool[:, 512 * jc:512 * jc + 256], in_=C.ps[2][:, 0:256], func=AF.Exp,
            bias=C.cb[:, 0:1], scale=0.125),
            reads=["ps2", "cb"], writes=[f"U{jc}"])
        S.run("ACT", lambda h, jc=jc: h.activation(
            out=C.upool[:, 512 * jc + 256:512 * jc + 512], in_=C.ps[3][:, 0:256], func=AF.Exp,
            bias=C.cb[:, 0:1], scale=0.125),
            reads=["ps3", "cb"], writes=[f"U{jc}"])
        S.run("DVE", lambda h, jc=jc: h.tensor_tensor(
            C.upool[:, 512 * jc:512 * jc + 512], C.upool[:, 512 * jc:512 * jc + 512],
            C.mask[:, 512 * jc:512 * jc + 512], ALU.mult),
            reads=[f"U{jc}", "mask"], writes=[f"U{jc}"])
    Ur = [f"U{jc}" for jc in range(4)]

    if sub < 3:
        return
    for jc in range(4):
        gpj, blk = jc // 2, jc % 2
        S.run("PE", lambda h, gpj=gpj, blk=blk: h.transpose(
            C.ps[4][0:128, 0:64], vh[64 * gpj:64 * gpj + 64, 128 * blk:128 * blk + 128],
            IDENT[64 * gpj:64 * gpj + 64, 64 * gpj:64 * gpj + 64]),
            reads=["vg", "ident", "ps4", "cb"], writes=["ps4"])
        S.run("DVE", lambda h, jc=jc: h.tensor_copy(C.tokp[:, 128 * jc:128 * jc + 64], C.ps[4][0:128, 0:64]),
              reads=["ps4"], writes=[f"vtok{jc}"])
    S.run("DVE", lambda h: h.tensor_scalar(C.scrA[:, 0:256], kh, 0.0, None, ALU.min, ALU.bypass),
          reads=["kg", "scrA"], writes=["scrA"])
    S.run("ACT", lambda h: h.activation(out=C.scrA[:, 0:256], in_=C.scrA[:, 0:256], func=AF.Exp),
          reads=["scrA"], writes=["scrA"])
    S.run("DVE", lambda h: h.tensor_scalar(kh, kh, 0.0, None, ALU.max, ALU.bypass),
          reads=["kg"], writes=["kg"])
    S.run("DVE", lambda h: h.tensor_tensor(kh, kh, C.scrA[:, 0:256], ALU.add),
          reads=["kg", "scrA"], writes=["kg", "sk_h"])
    for jc in range(4):
        gpj, blk = jc // 2, jc % 2
        S.run("PE", lambda h, gpj=gpj, blk=blk: h.transpose(
            C.ps[4][0:128, 0:64], kh[64 * gpj:64 * gpj + 64, 128 * blk:128 * blk + 128],
            IDENT[64 * gpj:64 * gpj + 64, 64 * gpj:64 * gpj + 64]),
            reads=["sk_h", "ident", "ps4"], writes=["ps4"])
        S.run("DVE", lambda h, jc=jc: h.tensor_copy(C.tokp[:, 512 + 128 * jc:512 + 128 * jc + 64], C.ps[4][0:128, 0:64]),
              reads=["ps4"], writes=[f"sktok{jc}"])
    S.run("DVE", lambda h: h.tensor_scalar(C.scrA[:, 0:256], qh, 0.0, None, ALU.min, ALU.bypass),
          reads=["qg", "scrA"], writes=["scrA"])
    S.run("ACT", lambda h: h.activation(out=C.scrA[:, 0:256], in_=C.scrA[:, 0:256], func=AF.Exp),
          reads=["scrA"], writes=["scrA"])
    S.run("DVE", lambda h: h.tensor_scalar(qh, qh, 0.0, None, ALU.max, ALU.bypass),
          reads=["qg"], writes=["qg"])
    S.run("DVE", lambda h: h.tensor_tensor(qh, qh, C.scrA[:, 0:256], ALU.add),
          reads=["qg", "scrA"], writes=["qg", "sq_h"])

    if sub < 4:
        return
    _acc(S, C.ps[5][0:1, :],
         [(ONES[:, 0:1], C.upool[:, 512 * jc:512 * jc + 512]) for jc in range(4)],
         reads=Ur + ["ones", "ps5"], writes=["ps5"])
    _acc(S, C.ps[6][0:64, :],
         [(C.tokp[:, 128 * jc:128 * jc + 64], C.upool[:, 512 * jc:512 * jc + 512]) for jc in range(4)],
         reads=Ur + [f"vtok{jc}" for jc in range(4)] + ["ps6"], writes=["ps6"])
    def mm_num(h):
        h.matmul(C.ps[7][0:64, 0:256], C.Md[0:64, 64 * head:64 * head + 64], qh[0:64, :],
                 start=True, stop=True)
        return h.matmul(C.ps[3][0:64, 0:256], C.Md[64:128, 64 * head:64 * head + 64], qh[64:128, :],
                        start=True, stop=True)
    S.run("PE", mm_num, reads=["sq_h", "Md", "ps7", "ps3"], writes=["ps7", "ps3"])
    S.run("DVE", lambda h: h.reciprocal(C.row[0:1, 1024:1536], C.ps[5][0:1, :]),
          reads=["ps5"], writes=["rowC"])
    def mm_den(h):
        h.matmul(C.ps[5][0:1, 0:256], C.Zd[0:64, head:head + 1], qh[0:64, :], start=True, stop=True)
        return h.matmul(C.ps[4][0:1, 0:256], C.Zd[64:128, head:head + 1], qh[64:128, :],
                        start=True, stop=True)
    S.run("PE", mm_den, reads=["sq_h", "Zd", "ps5", "ps4"], writes=["ps5", "ps4"])
    S.run("DVE", lambda h: h.reciprocal(C.row[0:1, 1536:1792], C.ps[5][0:1, 0:256]),
          reads=["ps5"], writes=["rowD"])
    S.run("DVE", lambda h: h.reciprocal(C.row[0:1, 1792:2048], C.ps[4][0:1, 0:256]),
          reads=["ps4"], writes=["rowD"])
    S.run("PE", lambda h: h.matmul(C.ps[2][0:64, :], ONES[0:1, 0:64], C.row[0:1, 1024:1536],
                                   start=True, stop=True),
          reads=["rowC", "ones", "ps2"], writes=["ps2"])
    S.run("PE", lambda h: h.matmul(C.ps[5][0:64, :], ONES[0:1, 0:64], C.row[0:1, 1536:2048],
                                   start=True, stop=True),
          reads=["rowD", "ones", "ps5"], writes=["ps5"])
    S.run("DVE", lambda h: h.tensor_copy(C.scrB[0:64, :], C.ps[2][0:64, :]),
          reads=["ps2", "scrB"], writes=["scrB"])
    S.run("DVE", lambda h: h.tensor_copy(C.scrC[0:64, :], C.ps[5][0:64, :]),
          reads=["ps5", "scrC"], writes=["scrC"])
    if sub < 5:
        return
    gcol = C.gates[:, H + head:H + head + 1]
    g1col = C.gates[:, 2 * H + head:2 * H + head + 1]
    S.run("DVE", lambda h: h.tensor_tensor(C.scrD[0:64, 0:256], C.ps[7][0:64, 0:256],
                                           C.scrC[0:64, 0:256], ALU.mult),
          reads=["ps7", "scrC", "scrD"], writes=["scrD"])
    S.run("DVE", lambda h: h.tensor_tensor(C.scrD[0:64, 256:512], C.ps[3][0:64, 0:256],
                                           C.scrC[0:64, 256:512], ALU.mult),
          reads=["ps3", "scrC", "scrD"], writes=["scrD"])
    S.run("DVE", lambda h: h.tensor_tensor(C.scrA[0:64, 0:512], C.ps[6][0:64, :], C.scrB[0:64, :], ALU.mult),
          reads=["ps6", "scrB", "scrA"], writes=["scrA"])
    S.run("DVE", lambda h: h.tensor_scalar(C.scrA[0:64, 0:512], C.scrA[0:64, 0:512], g1col, None,
                                           ALU.mult, ALU.bypass),
          reads=["scrA", "gate1m"], writes=["scrA"])
    S.run("DVE", lambda h: h.scalar_tensor_tensor(C.scrD[0:64, :], C.scrD[0:64, :], gcol,
                                                  C.scrA[0:64, 0:512], ALU.mult, ALU.add),
          reads=["scrD", "scrA", "gate"], writes=["scrD"])
    S.run("PE", lambda h: h.matmul(C.ps[2][:, :], C.DUP64, C.scrD[0:64, :], start=True, stop=True),
          reads=["scrD", "dup64", "ps2"], writes=["ps2"])
    def scat(h):
        last = None
        for gp in range(2):
            src = C.ps[2][64 * gp:64 * gp + 64, 256 * gp:256 * gp + 256].rearrange(
                "p (g2 rr) -> p g2 rr", g2=8, rr=32)
            dd = C.stg[64 * gp:64 * gp + 64, :].rearrange(
                "p (g2 f) -> p g2 f", g2=8, f=512)[:, :, 32 * head:32 * head + 32]
            last = h.tensor_copy(dd, src)
        return last
    S.run("DVE", scat, reads=["ps2"] + (STG if head == 0 else []),
          writes=[f"attw_{head}"] + (STG if head == 0 else []))

    if sub < 6:
        return
    _acc(S, C.ps[3][0:64, 0:64],
         [(C.tokp[:, 512 + 128 * jc:512 + 128 * jc + 64], C.tokp[:, 128 * jc:128 * jc + 64]) for jc in range(4)],
         reads=[f"sktok{jc}" for jc in range(4)] + [f"vtok{jc}" for jc in range(4)] + ["ps3"],
         writes=["ps3"])
    S.run("DVE", lambda h: h.tensor_copy(C.scrA[0:64, 0:64], C.ps[3][0:64, 0:64]),
          reads=["ps3", "scrA"], writes=["scrA"])
    S.run("PE", lambda h: h.matmul(C.ps[3][:, 0:64], C.DUP64, C.scrA[0:64, 0:64], start=True, stop=True),
          reads=["scrA", "dup64", "ps3"], writes=["ps3"])
    S.run("DVE", lambda h: h.tensor_tensor(C.Md[:, 64 * head:64 * head + 64],
                                           C.Md[:, 64 * head:64 * head + 64],
                                           C.ps[3][:, 0:64], ALU.add),
          reads=["ps3", "Md", "ps7"], writes=["Md"])
    S.run("DVE", lambda h: h.tensor_reduce(C.srow[:, 5:6], kh, axis=AX.X, op=ALU.add),
          reads=["sk_h"], writes=["srow_dz"])
    S.run("PE", lambda h: h.matmul(C.ps[3][:, 64:65], C.DUPSW, C.srow[:, 5:6], start=True, stop=True),
          reads=["srow_dz", "dupsw", "ps3"], writes=["ps3"])
    S.run("DVE", lambda h: h.tensor_tensor(C.Zd[:, head:head + 1], C.Zd[:, head:head + 1],
                                           C.ps[3][:, 64:65], ALU.add),
          reads=["ps3", "Zd"], writes=["Zd"])


_nc_cache = None


def kernel(**inputs):
    global _nc_cache
    if _nc_cache is None:
        _nc_cache = build_program()
    in_maps = _prep_inputs(inputs)
    res = run_bass_kernel_spmd(_nc_cache, in_maps, core_ids=list(range(NCORES)))
    out = np.stack([np.ascontiguousarray(np.asarray(r["out_fm"]).T) for r in res.results])
    return out.astype(np.float32)



def _qdq_tile(S, C, dst_ap, src_ap, inv_col, sc_col, lo, hi, keyr, keyw, width):
    for o in range(0, width, 2048):
        w = min(2048, width - o)
        S.run("DVE", lambda h, o=o, w=w: h.tensor_scalar(
            C.fwi[:, 0:w], src_ap[:, o:o + w], inv_col, None, ALU.mult, ALU.bypass),
            reads=list(keyr) + ["fwi", "cb"], writes=["fwi"])
        S.run("DVE", lambda h, o=o, w=w: h.tensor_scalar(
            dst_ap[:, o:o + w], C.fwi[:, 0:w], float(lo), float(hi), ALU.max, ALU.min),
            reads=["fwi"] + list(keyw), writes=list(keyw))
    S.run("DVE", lambda h: h.tensor_scalar(dst_ap, dst_ap, sc_col, None, ALU.mult, ALU.bypass),
          reads=list(keyw) + ["cb"], writes=list(keyw))


def _absmax_track(S, C, src_ap, col, first, keyr):
    S.run("DVE", lambda h: h.reduce_max(C.srow[:, 15:16], src_ap, axis=AX.X, apply_absolute_value=True),
          reads=list(keyr) + ["srow"], writes=["srow"])
    if first:
        S.run("DVE", lambda h: h.tensor_copy(C.srow[:, col:col + 1], C.srow[:, 15:16]),
              reads=["srow"], writes=["srow"])
    else:
        S.run("DVE", lambda h: h.tensor_tensor(C.srow[:, col:col + 1], C.srow[:, col:col + 1],
                                               C.srow[:, 15:16], ALU.max),
              reads=["srow"], writes=["srow"])


def _finish_absmax(S, C, col, cell):
    S.run("PE", lambda h: h.transpose(C.ps[4][0:1, 0:128], C.srow[:, col:col + 1], C.IDENT[:, :]),
          reads=["srow", "ident", "ps4"], writes=["ps4"])
    S.run("DVE", lambda h: h.reduce_max(C.srow[0:1, 8 + cell:9 + cell], C.ps[4][0:1, 0:128], axis=AX.X),
          reads=["ps4", "srow"], writes=["srow"])


def _allreduce(S, C, idx):
    S.dma("POOL", lambda h: h.dma_start(out=C.ap[f"cc_in{idx}"][:, :], in_=C.srow[0:1, 8:16]),
          reads=["srow", f"ccin{idx}"], writes=[f"ccin{idx}"], slot="cc")
    tok = S.run("POOL", lambda h: h.collective_compute(
        "AllReduce", ALU.max, replica_groups=RG8,
        ins=[C.ap[f"cc_in{idx}"][:, :]], outs=[C.ap[f"cc_out{idx}"][:, :]]),
        reads=[f"ccin{idx}", f"ccout{idx}"], writes=[f"ccout{idx}"])
    def body(h):
        h.wait_ge(S.psem["POOL"], tok[2])
    getattr(S.block, "gpsimd")(body)
    S.dma("POOL", lambda h: h.dma_start(out=C.srow[0:1, 8:16], in_=C.ap[f"cc_out{idx}"][:, :]),
          reads=[f"ccout{idx}", "srow"], writes=["srow"], slot="cc")


def _scale_prep(S, C, cell, inv_col, sc_col):
    S.run("DVE", lambda h: h.tensor_scalar(C.srow[0:1, 14:15], C.srow[0:1, 8 + cell:9 + cell],
                                           float(np.float32(1.0 / 32767.0)), 1e-12, ALU.mult, ALU.max),
          reads=["srow"], writes=["srow"])
    S.run("DVE", lambda h: h.reciprocal(C.srow[0:1, 13:14], C.srow[0:1, 14:15]),
          reads=["srow"], writes=["srow"])
    S.run("PE", lambda h: h.matmul(C.ps[1][:, 0:2], C.ONES[0:1, 0:128], C.srow[0:1, 13:15],
                                   start=True, stop=True),
          reads=["srow", "ones", "ps1"], writes=["ps1"])
    S.run("DVE", lambda h: h.tensor_copy(C.cb[:, inv_col:inv_col + 1], C.ps[1][:, 0:1]),
          reads=["ps1", "cb"], writes=["cb"])
    S.run("DVE", lambda h: h.tensor_copy(C.cb[:, sc_col:sc_col + 1], C.ps[1][:, 1:2]),
          reads=["ps1", "cb"], writes=["cb"])


def _phase_b(nc, S, C):
    ap = C.ap
    ONES = C.ONES
    IDENT = C.IDENT

    for tt in range(NSEG):
        for dc in range(DC):
            S.dma("SP", lambda h, tt=tt, dc=dc: h.dma_start(
                out=C.xtile[:, 512 * dc:512 * dc + 512],
                in_=ap["x1_sp"][128 * dc:128 * dc + 128, 512 * tt:512 * tt + 512]),
                reads=[f"x1sp_{dc}_{tt}", f"xt_{dc}"], writes=[f"xt_{dc}"], slot="xin")
        if tt == 0:
            S.run("DVE", lambda h: h.memset(C.srow[0:1, 15:16], 0.0),
                  reads=[], writes=["srow", "srow_qq", "srow_kk", "srow_c", "srow_cn",
                                    "srow_dz", "srow_qq2", "srow_kk2"])

        def sink(dc, tt=tt):
            S.dma("POOL", lambda h, tt=tt, dc=dc: h.dma_start(
                out=ap["xn1_sp"][128 * dc:128 * dc + 128, 512 * tt:512 * tt + 512],
                in_=C.scrC),
                reads=["scrC"], writes=[f"xn1_{dc}_{tt}"], slot="xn1o")
            _absmax_track(S, C, C.scrC, 0, (tt == 0 and dc == 0), ["scrC"])
        _ln_tile(S, C, chunk_sink=sink)
        for dc in range(DC):
            S.dma("POOL", lambda h, tt=tt, dc=dc: h.dma_start(
                out=ap["xn1h_sp"][128 * dc:128 * dc + 128, 512 * tt:512 * tt + 512],
                in_=C.xn[:, 512 * dc:512 * dc + 512]),
                reads=[f"xn_{dc}"], writes=[f"xn1p_{dc}_{tt}"], slot="xn1oh")
            S.dma("POOL", lambda h, tt=tt, dc=dc: h.dma_start(
                out=ap["xn1l_sp"][128 * dc:128 * dc + 128, 512 * tt:512 * tt + 512],
                in_=C.xn[:, DC * 512 + 512 * dc:DC * 512 + 512 * dc + 512]),
                reads=[f"xn_{dc}"], writes=[f"xn1p_{dc}_{tt}"], slot="xn1ol")
    _finish_absmax(S, C, 0, 0)
    _allreduce(S, C, 0)

    if B_STAGE < 2:
        return
    for src, dst, nchunk, width in (("wg1", "wgT1", HC, D), ("wu1", "wuT1", HC, D),
                                    ("wd1", "wdT1", DC, HID)):
        nblk = width // 32
        nj = width // 128
        for c in range(nchunk):
            S.dma("SP", lambda h, src=src, c=c, width=width: h.dma_start(
                out=C.fw[0][:, 0:width], in_=ap[src][128 * c:128 * c + 128, :]),
                reads=["fw0"], writes=["fw0"], slot="deq_i")
            S.run("DVE", lambda h, width=width, nblk=nblk: h.reduce_max(
                C.scr[:, 0:nblk],
                C.fw[0][:, 0:width].rearrange("p (b j) -> p b j", b=nblk, j=32),
                axis=AX.X, apply_absolute_value=True),
                reads=["fw0", "scrA", "scrB"], writes=["scrA"])
            S.run("DVE", lambda h, nblk=nblk: h.tensor_scalar(
                C.scr[:, 0:nblk], C.scr[:, 0:nblk], float(np.float32(1.0 / 7.0)), 1e-12, ALU.mult, ALU.max),
                reads=["scrA"], writes=["scrA"])
            S.run("DVE", lambda h, nblk=nblk: h.reciprocal(C.scr[:, 1024:1024 + nblk], C.scr[:, 0:nblk]),
                reads=["scrA", "scrC"], writes=["scrC"])
            for o in range(0, width, 2048):
                w2 = min(2048, width - o)
                nb2 = w2 // 32
                S.run("DVE", lambda h, o=o, w2=w2, nb2=nb2: h.tensor_tensor(
                    C.fwi[:, 0:w2].rearrange("p (b j) -> p b j", b=nb2, j=32),
                    C.fw[0][:, o:o + w2].rearrange("p (b j) -> p b j", b=nb2, j=32),
                    C.scr[:, 1024 + o // 32:1024 + o // 32 + nb2].rearrange(
                        "p (b o2) -> p b o2", b=nb2, o2=1).broadcast_to((128, nb2, 32)),
                    ALU.mult),
                    reads=["fw0", "scrC", "fwi"], writes=["fwi"])
                S.run("DVE", lambda h, o=o, w2=w2: h.tensor_scalar(
                    C.fw[1][:, o:o + w2], C.fwi[:, 0:w2], -8.0, 7.0, ALU.max, ALU.min),
                    reads=["fwi", "fw1"], writes=["fw1"])
            S.run("DVE", lambda h, width=width, nblk=nblk: h.tensor_tensor(
                C.fw[1][:, 0:width].rearrange("p (b j) -> p b j", b=nblk, j=32),
                C.fw[1][:, 0:width].rearrange("p (b j) -> p b j", b=nblk, j=32),
                C.scr[:, 0:nblk].rearrange("p (b o) -> p b o", b=nblk, o=1).broadcast_to((128, nblk, 32)),
                ALU.mult),
                reads=["fw1", "scrA"], writes=["fw1"])
            for j in range(nj):
                pb = C.ps[2 + (j % 2)]
                S.run("PE", lambda h, j=j, pb=pb: h.transpose(
                    pb[:, 0:128], C.fw[1][:, 128 * j:128 * j + 128], IDENT[:, :]),
                    reads=["fw1", "ident", f"ps{2 + (j % 2)}"], writes=[f"ps{2 + (j % 2)}"])
                S.run("DVE", lambda h, j=j, pb=pb: h.tensor_copy(C.qg[:, 128 * j:128 * j + 128], pb[:, 0:128]),
                    reads=[f"ps{2 + (j % 2)}", "qg"], writes=["qg"])
            S.run("DVE", lambda h, nj=nj: h.tensor_copy(C.xn[:, 0:128 * nj], C.qg[:, 0:128 * nj]),
                  reads=["qg"] + XN, writes=XN)
            S.run("DVE", lambda h, nj=nj: h.tensor_tensor(
                C.xn[:, DC * 512:DC * 512 + 128 * nj], C.qg[:, 0:128 * nj], C.xn[:, 0:128 * nj],
                ALU.subtract),
                reads=["qg"] + XN, writes=XN)
            S.dma("POOL", lambda h, dst=dst, c=c, nj=nj: h.dma_start(
                out=ap[dst + "_h"][:, 128 * c:128 * c + 128].rearrange("(j p) c2 -> p j c2", j=nj, p=128),
                in_=C.xn[:, 0:128 * nj].rearrange("p (j c2) -> p j c2", j=nj, c2=128)),
                reads=XN, writes=[f"{dst}_h_c{c}"], slot="deq_o")
            S.dma("POOL", lambda h, dst=dst, c=c, nj=nj: h.dma_start(
                out=ap[dst + "_l"][:, 128 * c:128 * c + 128].rearrange("(j p) c2 -> p j c2", j=nj, p=128),
                in_=C.xn[:, DC * 512:DC * 512 + 128 * nj].rearrange("p (j c2) -> p j c2", j=nj, c2=128)),
                reads=XN, writes=[f"{dst}_l_c{c}"], slot="deq_o2")

    if B_STAGE < 3:
        return
    _scale_prep(S, C, 0, 4, 3)
    for tt in range(NSEG):
        for dc in range(DC):
            S.dma("SP", lambda h, tt=tt, dc=dc: h.dma_start(
                out=C.stg[:, 512 * dc:512 * dc + 512],
                in_=ap["xn1_sp"][128 * dc:128 * dc + 128, 512 * tt:512 * tt + 512]),
                reads=[f"xn1_{dc}_{tt}", f"stg_{dc}"], writes=[f"stg_{dc}"], slot="xin")
        for dc in range(DC):
            _qdq_tile(S, C, C.xtile[:, 512 * dc:512 * dc + 512], C.stg[:, 512 * dc:512 * dc + 512],
                      C.cb[:, 4:5], C.cb[:, 3:4], -32768.0, 32767.0,
                      [f"stg_{dc}"], [f"xt_{dc}"], 512)
            S.run("DVE", lambda h, dc=dc: h.tensor_copy(
                C.xn[:, 512 * dc:512 * dc + 512], C.xtile[:, 512 * dc:512 * dc + 512]),
                reads=[f"xt_{dc}", f"xn_{dc}"], writes=[f"xn_{dc}"])
            S.run("DVE", lambda h, dc=dc: h.tensor_tensor(
                C.xn[:, DC * 512 + 512 * dc:DC * 512 + 512 * dc + 512],
                C.xtile[:, 512 * dc:512 * dc + 512], C.xn[:, 512 * dc:512 * dc + 512], ALU.subtract),
                reads=[f"xt_{dc}", f"xn_{dc}"], writes=[f"xn_{dc}"])
        for hc in range(HC):
            for mat, spill, bank, col in (("wuT1", "up_sp", 6 - 2 * (hc % 2), 1),
                                          ("wgT1", "gate_sp", 7 - 2 * (hc % 2), 2)):
                pwi = 0 if mat == "wuT1" else 1
                S.dma("SP", lambda h, mat=mat, hc=hc, pwi=pwi: h.dma_start(
                    out=C.pw[pwi][:, 0:DC * 128].rearrange("p (dc c) -> p dc c", dc=DC, c=128),
                    in_=ap[mat + "_h"][:, 128 * hc:128 * hc + 128].rearrange("(dc p) c -> p dc c", dc=DC, p=128)),
                    reads=[f"{mat}_h_c{i}" for i in range(HC)] + [f"pw{pwi}"], writes=[f"pw{pwi}"], slot=f"pw{pwi}")
                S.dma("SP", lambda h, mat=mat, hc=hc, pwi=pwi: h.dma_start(
                    out=C.pw[pwi][:, DC * 128:2 * DC * 128].rearrange("p (dc c) -> p dc c", dc=DC, c=128),
                    in_=ap[mat + "_l"][:, 128 * hc:128 * hc + 128].rearrange("(dc p) c -> p dc c", dc=DC, p=128)),
                    reads=[f"{mat}_l_c{i}" for i in range(HC)] + [f"pw{pwi}"], writes=[f"pw{pwi}"], slot=f"pw{pwi}b")
                _acc3(S, C.ps[bank][:, :], pwi, C,
                      reads=[f"pw{pwi}"] + XN + [f"ps{bank}"], writes=[f"ps{bank}"])
                dstscr = C.scrC if mat == "wuT1" else C.scrD
                dk = "scrC" if mat == "wuT1" else "scrD"
                S.run("DVE", lambda h, bank=bank, dstscr=dstscr: h.tensor_copy(dstscr, C.ps[bank][:, :]),
                    reads=[f"ps{bank}", dk], writes=[dk])
                _absmax_track(S, C, dstscr, col, (tt == 0 and hc == 0), [dk])
                S.dma("POOL", lambda h, spill=spill, hc=hc, tt=tt, dstscr=dstscr: h.dma_start(
                    out=ap[spill][128 * hc:128 * hc + 128, 512 * tt:512 * tt + 512], in_=dstscr),
                    reads=[dk], writes=[f"{spill}_{hc}_{tt}"], slot="upo")
    _finish_absmax(S, C, 1, 1)
    _finish_absmax(S, C, 2, 2)
    _allreduce(S, C, 1)

    if B_STAGE < 4:
        return
    _scale_prep(S, C, 1, 4, 3)
    _scale_prep(S, C, 2, 6, 5)
    for hc in range(HC):
        S.dma("SP", lambda h, hc=hc: h.dma_start(
            out=C.stg[:, 0:4096], in_=ap["up_sp"][128 * hc:128 * hc + 128, :]),
            reads=[f"up_sp_{hc}_{t2}" for t2 in range(NSEG)] + STG, writes=STG, slot="xin")
        S.dma("SP", lambda h, hc=hc: h.dma_start(
            out=C.xtile[:, 0:4096], in_=ap["gate_sp"][128 * hc:128 * hc + 128, :]),
            reads=[f"gate_sp_{hc}_{t2}" for t2 in range(NSEG)] + XT, writes=XT, slot="xin2")
        _qdq_tile(S, C, C.stg[:, 0:4096], C.stg[:, 0:4096], C.cb[:, 4:5], C.cb[:, 3:4],
                  -32768.0, 32767.0, STG, STG, 4096)
        _qdq_tile(S, C, C.xtile[:, 0:4096], C.xtile[:, 0:4096], C.cb[:, 6:7], C.cb[:, 5:6],
                  -32768.0, 32767.0, XT, XT, 4096)
        S.run("ACT", lambda h: h.activation(out=C.qsw[:, 0:4096], in_=C.xtile[:, 0:4096], func=AF.Sigmoid),
              reads=XT + ["qsw"], writes=["qsw"])
        for o in range(0, 4096, 2048):
            S.run("DVE", lambda h, o=o: h.tensor_scalar(C.fwi[:, 0:2048], C.qsw[:, o:o + 2048], 65536.0, None,
                                                        ALU.mult, ALU.bypass),
                  reads=["qsw", "fwi"], writes=["fwi"])
            S.run("DVE", lambda h, o=o: h.tensor_scalar(C.qsw[:, o:o + 2048], C.fwi[:, 0:2048], 0.0, 65535.0,
                                                        ALU.max, ALU.min),
                  reads=["fwi", "qsw"], writes=["qsw"])
        S.run("DVE", lambda h: h.scalar_tensor_tensor(C.qsw[:, 0:4096], C.qsw[:, 0:4096],
                                                      float(np.float32(1.0 / 65536.0)),
                                                      C.xtile[:, 0:4096], ALU.mult, ALU.mult),
              reads=["qsw"] + XT, writes=["qsw"])
        _absmax_track(S, C, C.qsw[:, 0:4096], 3, hc == 0, ["qsw"])
        S.dma("POOL", lambda h, hc=hc: h.dma_start(out=ap["t_sp"][128 * hc:128 * hc + 128, :],
                                                   in_=C.qsw[:, 0:4096]),
              reads=["qsw"], writes=[f"t_sp_{hc}"], slot="upo")
        S.dma("POOL", lambda h, hc=hc: h.dma_start(out=ap["upq_sp"][128 * hc:128 * hc + 128, :],
                                                   in_=C.stg[:, 0:4096]),
              reads=STG, writes=[f"upq_sp_{hc}"], slot="upo2")
    _finish_absmax(S, C, 3, 3)
    _allreduce(S, C, 2)

    if B_STAGE < 5:
        return
    _scale_prep(S, C, 3, 4, 3)
    for hc in range(HC):
        S.dma("SP", lambda h, hc=hc: h.dma_start(out=C.stg[:, 0:4096],
                                                 in_=ap["t_sp"][128 * hc:128 * hc + 128, :]),
              reads=[f"t_sp_{hc}"] + STG, writes=STG, slot="xin")
        S.dma("SP", lambda h, hc=hc: h.dma_start(out=C.xtile[:, 0:4096],
                                                 in_=ap["upq_sp"][128 * hc:128 * hc + 128, :]),
              reads=[f"upq_sp_{hc}"] + XT, writes=XT, slot="xin2")
        _qdq_tile(S, C, C.stg[:, 0:4096], C.stg[:, 0:4096], C.cb[:, 4:5], C.cb[:, 3:4],
                  -32768.0, 32767.0, STG, STG, 4096)
        S.run("DVE", lambda h: h.tensor_tensor(C.stg[:, 0:4096], C.stg[:, 0:4096],
                                               C.xtile[:, 0:4096], ALU.mult),
              reads=STG + XT, writes=STG)
        _absmax_track(S, C, C.stg[:, 0:4096], 4, hc == 0, STG)
        S.dma("POOL", lambda h, hc=hc: h.dma_start(out=ap["u_sp"][128 * hc:128 * hc + 128, :],
                                                   in_=C.stg[:, 0:4096]),
              reads=STG, writes=[f"u_sp_{hc}"], slot="upo")
    _finish_absmax(S, C, 4, 4)
    _allreduce(S, C, 3)

    if B_STAGE < 6:
        return
    _scale_prep(S, C, 4, 4, 3)
    for hc in range(HC):
        S.dma("SP", lambda h, hc=hc: h.dma_start(out=C.stg[:, 0:4096],
                                                 in_=ap["u_sp"][128 * hc:128 * hc + 128, :]),
              reads=[f"u_sp_{hc}"] + STG, writes=STG, slot="xin")
        _qdq_tile(S, C, C.stg[:, 0:4096], C.stg[:, 0:4096], C.cb[:, 4:5], C.cb[:, 3:4],
                  -32768.0, 32767.0, STG, STG, 4096)
        S.run("DVE", lambda h: h.tensor_copy(C.xn[:, 0:4096], C.stg[:, 0:4096]),
              reads=STG + XN, writes=XN)
        S.run("DVE", lambda h: h.tensor_tensor(C.xn[:, 4096:8192], C.stg[:, 0:4096],
                                               C.xn[:, 0:4096], ALU.subtract),
              reads=STG + XN, writes=XN)
        S.dma("POOL", lambda h, hc=hc: h.dma_start(out=ap["oqh_sp"][128 * hc:128 * hc + 128, :],
                                                   in_=C.xn[:, 0:4096]),
              reads=XN, writes=[f"oq_{hc}"], slot="upo")
        S.dma("POOL", lambda h, hc=hc: h.dma_start(out=ap["oql_sp"][128 * hc:128 * hc + 128, :],
                                                   in_=C.xn[:, 4096:8192]),
              reads=XN, writes=[f"oq_{hc}"], slot="upo2")
    for tt in range(NSEG):
        for hc in range(HC):
            hs = hc % 2
            S.dma("SP", lambda h, hc=hc, tt=tt, hs=hs: h.dma_start(
                out=C.hdb[:, 1024 * hs:1024 * hs + 512],
                in_=ap["oqh_sp"][128 * hc:128 * hc + 128, 512 * tt:512 * tt + 512]),
                reads=[f"oq_{hc}", f"hdb{hs}"], writes=[f"hdb{hs}"], slot=f"oqi{hs}")
            S.dma("SP", lambda h, hc=hc, tt=tt, hs=hs: h.dma_start(
                out=C.hdb[:, 1024 * hs + 512:1024 * hs + 1024],
                in_=ap["oql_sp"][128 * hc:128 * hc + 128, 512 * tt:512 * tt + 512]),
                reads=[f"oq_{hc}", f"hdb{hs}"], writes=[f"hdb{hs}"], slot=f"oqi{hs}b")
            S.dma("SP", lambda h, hc=hc, hs=hs: h.dma_start(
                out=C.pw[hs][:, 0:D], in_=ap["wdT1_h"][128 * hc:128 * hc + 128, :]),
                reads=[f"wdT1_h_c{i}" for i in range(DC)] + [f"pw{hs}"], writes=[f"pw{hs}"], slot=f"pw{hs}")
            S.dma("SP", lambda h, hc=hc, hs=hs: h.dma_start(
                out=C.pw[hs][:, D:2 * D], in_=ap["wdT1_l"][128 * hc:128 * hc + 128, :]),
                reads=[f"wdT1_l_c{i}" for i in range(DC)] + [f"pw{hs}"], writes=[f"pw{hs}"], slot=f"pw{hs}b")
            def mmd(h, hc=hc, hs=hs):
                last = None
                oh = C.hdb[:, 1024 * hs:1024 * hs + 512]
                ol = C.hdb[:, 1024 * hs + 512:1024 * hs + 1024]
                for do in range(DC):
                    last = h.matmul(C.ps[do][:, :], C.pw[hs][:, 128 * do:128 * do + 128], oh,
                                    start=(hc == 0), stop=False)
                    last = h.matmul(C.ps[do][:, :], C.pw[hs][:, 128 * do:128 * do + 128], ol,
                                    start=False, stop=False)
                    last = h.matmul(C.ps[do][:, :], C.pw[hs][:, D + 128 * do:D + 128 * do + 128], oh,
                                    start=False, stop=(hc == HC - 1))
                return last
            S.run("PE", mmd, reads=[f"pw{hs}", f"hdb{hs}"] + [f"ps{i}" for i in range(DC)],
                  writes=[f"ps{i}" for i in range(DC)])
        for do in range(DC):
            S.run("ACT", lambda h, do=do: h.activation(
                out=C.stg[:, 512 * do:512 * do + 512], in_=C.ps[do][:, :], func=AF.Silu),
                reads=[f"ps{do}", f"stg_{do}"], writes=[f"stg_{do}"])
            S.dma("POOL", lambda h, do=do, tt=tt: h.dma_start(
                out=ap["g_sp"][128 * do:128 * do + 128, 512 * tt:512 * tt + 512],
                in_=C.stg[:, 512 * do:512 * do + 512]),
                reads=[f"stg_{do}"], writes=[f"g1_{do}_{tt}"], slot="gsp")

    if B_STAGE < 7:
        return
    for pair in range(8):
        _l1_pair(nc, S, C, pair)

    if B_STAGE < 8:
        return
    S.dma("SP", lambda h: h.dma_start(out=C.srow[:, 0:DC], in_=ap["bo1fm"][:, :]),
          reads=["srow"], writes=["srow", "srow_bo"], slot="xin2")
    for tt in range(NSEG):
        for dc in range(DC):
            S.dma("SP", lambda h, tt=tt, dc=dc: h.dma_start(
                out=C.stg[:, 512 * dc:512 * dc + 512],
                in_=ap["att_sp"][128 * dc:128 * dc + 128, 512 * tt:512 * tt + 512]),
                reads=[f"att_{2 * dc}", f"att_{2 * dc + 1}", f"stg_{dc}"],
                writes=[f"stg_{dc}"], slot="xin")
        for oc in range(DC):
            ws = oc % 2
            _wcol_load(S, C, "Wo1", oc, ws)
            _acc(S, C.ps[2 + ws][:, :],
                 [(C.fw[ws][:, 128 * dc:128 * dc + 128], C.stg[:, 512 * dc:512 * dc + 512]) for dc in range(DC)],
                 reads=[f"fw{ws}"] + STG + [f"ps{2 + ws}"], writes=[f"ps{2 + ws}"])
            S.dma("SP", lambda h, oc=oc, tt=tt: h.dma_start(
                out=C.scrC, in_=ap["g_sp"][128 * oc:128 * oc + 128, 512 * tt:512 * tt + 512]),
                reads=[f"g1_{oc}_{tt}", "scrC"], writes=["scrC"], slot="gbk")
            S.dma("SP", lambda h, oc=oc, tt=tt: h.dma_start(
                out=C.scrD, in_=ap["x1_sp"][128 * oc:128 * oc + 128, 512 * tt:512 * tt + 512]),
                reads=[f"x1sp_{oc}_{tt}", "scrD"], writes=["scrD"], slot="gbk2")
            S.run("DVE", lambda h, oc=oc, ws=ws: h.tensor_scalar(
                C.scrB, C.ps[2 + ws][:, :], C.srow[:, oc:oc + 1], None, ALU.add, ALU.bypass),
                reads=[f"ps{2 + ws}", "srow_bo", "scrB"], writes=["scrB"])
            S.run("DVE", lambda h: h.tensor_tensor(C.scrB, C.scrB, C.scrC, ALU.mult),
                  reads=["scrB", "scrC"], writes=["scrB"])
            S.run("DVE", lambda h: h.tensor_tensor(C.scrB, C.scrB, C.scrD, ALU.add),
                  reads=["scrB", "scrD"], writes=["scrB"])
            S.dma("POOL", lambda h, oc=oc, tt=tt: h.dma_start(
                out=C.out_fm[128 * oc:128 * oc + 128, 512 * tt:512 * tt + 512], in_=C.scrB),
                reads=["scrB"], writes=[f"out_{oc}_{tt}"], slot="outw")


def _l1_pair(nc, S, C, pair):
    ap = C.ap
    ONES = C.ONES
    IDENT = C.IDENT

    for tt in range(NSEG):
        for dc in range(DC):
            S.dma("SP", lambda h, tt=tt, dc=dc: h.dma_start(
                out=C.xn[:, 512 * dc:512 * dc + 512],
                in_=ap["xn1h_sp"][128 * dc:128 * dc + 128, 512 * tt:512 * tt + 512]),
                reads=[f"xn1p_{dc}_{tt}", f"xn_{dc}"], writes=[f"xn_{dc}"], slot="xin")
            S.dma("SP", lambda h, tt=tt, dc=dc: h.dma_start(
                out=C.xn[:, DC * 512 + 512 * dc:DC * 512 + 512 * dc + 512],
                in_=ap["xn1l_sp"][128 * dc:128 * dc + 128, 512 * tt:512 * tt + 512]),
                reads=[f"xn1p_{dc}_{tt}", f"xn_{dc}"], writes=[f"xn_{dc}"], slot="xin2")
        for mat, dst, nm, pwi, qscale in (("Wq1", C.qg, "qg", 0, 0.125),
                                          ("Wk1", C.kg, "kg", 1, 1.0)):
            if tt == 0:
                _wcol_load_pair(S, C, mat, pair, pwi)
            _acc3(S, C.ps[2 + pwi][:, :], pwi, C,
                  reads=[f"pw{pwi}"] + XN + [f"ps{2 + pwi}"], writes=[f"ps{2 + pwi}"])
            S.run("DVE", lambda h, tt=tt, dst=dst, pwi=pwi, nm=nm, qscale=qscale: h.tensor_scalar(
                dst[:, 512 * tt:512 * tt + 512], C.ps[2 + pwi][:, :], float(qscale), None, ALU.mult, ALU.bypass),
                reads=[f"ps{2 + pwi}", nm], writes=[nm])
        if tt == 0:
            S.dma("SP", lambda h, pair=pair: h.dma_start(
                out=C.fwb[:, 0:D].rearrange("p (dc c) -> p dc c", dc=DC, c=128),
                in_=ap["Wv1_h"][:, 128 * pair:128 * pair + 128].rearrange("(dc p) c -> p dc c", dc=DC, p=128)),
                reads=["fwb0", "fwb1"], writes=["fwb0"], slot="fwbh")
            S.dma("SP", lambda h, pair=pair: h.dma_start(
                out=C.fwb[:, D:2 * D].rearrange("p (dc c) -> p dc c", dc=DC, c=128),
                in_=ap["Wv1_l"][:, 128 * pair:128 * pair + 128].rearrange("(dc p) c -> p dc c", dc=DC, p=128)),
                reads=["fwb1", "fwb0"], writes=["fwb1"], slot="fwbl")
        def mm_v(h):
            last = None
            n = 0
            for wofs, xo in ((0, 0), (0, DC * 512), (D, 0)):
                for dc in range(DC):
                    n += 1
                    last = h.matmul(C.ps[6][:, :], C.fwb[:, wofs + 128 * dc:wofs + 128 * dc + 128],
                                    C.xn[:, xo + 512 * dc:xo + 512 * dc + 512],
                                    start=(n == 1), stop=(n == 3 * DC))
            return last
        S.run("PE", mm_v, reads=["fwb0", "fwb1"] + XN + ["ps6"], writes=["ps6"])
        S.run("DVE", lambda h, tt=tt: h.tensor_copy(C.vg[:, 512 * tt:512 * tt + 512], C.ps[6][:, :]),
            reads=["ps6", "vg"], writes=["vg"])

    S.run("DVE", lambda h: h.tensor_reduce(
        C.tokp[:, 0:64], C.qg[:, :].rearrange("p (m j) -> p m j", m=64, j=64), axis=AX.X, op=ALU.add),
        reads=["qg", "tokp"],
        writes=["tokp"] + [f"vtok{j}" for j in range(4)] + [f"sktok{j}" for j in range(4)])
    S.run("DVE", lambda h: h.tensor_reduce(
        C.tokp[:, 64:128], C.kg[:, :].rearrange("p (m j) -> p m j", m=64, j=64), axis=AX.X, op=ALU.add),
        reads=["kg", "tokp"], writes=["tokp"])
    S.run("DVE", lambda h: h.tensor_scalar(C.tokp[:, 0:128], C.tokp[:, 0:128],
                                           float(np.float32(1.0 / 64.0)), None, ALU.mult, ALU.bypass),
          reads=["tokp"], writes=["tokp"])

    for jc in range(32):
        pb = C.ps[2 + (jc % 2)]
        S.run("PE", lambda h, jc=jc, pb=pb: h.transpose(
            pb[:, 0:128], C.vg[:, 128 * jc:128 * jc + 128], IDENT[:, :]),
            reads=["vg", "ident", f"ps{2 + (jc % 2)}"], writes=[f"ps{2 + (jc % 2)}"])
        S.run("DVE", lambda h, jc=jc, pb=pb: h.tensor_copy(C.fw[jc % 2][:, 128 * (jc // 2):128 * (jc // 2) + 128], pb[:, 0:128]),
            reads=[f"ps{2 + (jc % 2)}", f"fw{jc % 2}"], writes=[f"fw{jc % 2}"])

    for s in range(2):
        _l1_head(nc, S, C, pair, s, 2 * pair + s)


def _l1_head(nc, S, C, pair, s, head):
    ap = C.ap
    ONES = C.ONES
    IDENT = C.IDENT
    ql = C.tokp[64 * s:64 * s + 64, 0:64]
    kl = C.tokp[64 * s:64 * s + 64, 64:128]
    qh = C.qg[64 * s:64 * s + 64, :]
    khh = C.kg[64 * s:64 * s + 64, :]
    U1 = (C.stg if s == 0 else C.xtile)[0:64, 0:4096]
    U1k = (STG if s == 0 else XT)

    S.run("DVE", lambda h: h.tensor_tensor(C.qsw[64 * s:64 * s + 64, :], qh, qh, ALU.mult),
          reads=["qg", "qsw"], writes=["qsw"])
    for i2 in range(8):
        S.run("PE", lambda h, i2=i2: h.matmul(C.ps[4][0:1, :], ONES[64 * s:64 * s + 64, 0:1],
                                              C.qsw[64 * s:64 * s + 64, 512 * i2:512 * i2 + 512],
                                              start=True, stop=True),
              reads=["qsw", "ones", "ps4"], writes=["ps4"])
        S.run("DVE", lambda h: h.reduce_max(C.srow[0:1, 15:16], C.ps[4][0:1, 0:512], axis=AX.X),
              reads=["ps4", "srow"], writes=["srow"])
        if i2 == 0:
            S.run("DVE", lambda h: h.tensor_copy(C.srow[0:1, 0:1], C.srow[0:1, 15:16]),
                  reads=["srow"], writes=["srow"])
        else:
            S.run("DVE", lambda h: h.tensor_tensor(C.srow[0:1, 0:1], C.srow[0:1, 0:1],
                                                   C.srow[0:1, 15:16], ALU.max),
                  reads=["srow"], writes=["srow"])
    S.run("DVE", lambda h: h.tensor_tensor(C.qsw[64 * s:64 * s + 64, 0:64], kl, kl, ALU.mult),
          reads=["tokp", "qsw"], writes=["qsw"])
    S.run("PE", lambda h: h.matmul(C.ps[4][0:1, 0:64], ONES[64 * s:64 * s + 64, 0:1],
                                   C.qsw[64 * s:64 * s + 64, 0:64], start=True, stop=True),
          reads=["qsw", "ones", "ps4"], writes=["ps4"])
    S.run("DVE", lambda h: h.reduce_max(C.srow[0:1, 1:2], C.ps[4][0:1, 0:64], axis=AX.X),
          reads=["ps4", "srow"], writes=["srow"])
    S.run("DVE", lambda h: h.tensor_tensor(C.srow[0:1, 2:3], C.srow[0:1, 0:1], C.srow[0:1, 1:2], ALU.add),
          reads=["srow"], writes=["srow"])
    S.run("DVE", lambda h: h.tensor_scalar(C.srow[0:1, 4:5], C.srow[0:1, 2:3], -0.5, None, ALU.mult, ALU.bypass),
          reads=["srow"], writes=["srow"])
    S.run("PE", lambda h: h.matmul(C.ps[4][:, 0:1], ONES[0:1, 0:128], C.srow[0:1, 4:5], start=True, stop=True),
          reads=["srow", "ones", "ps4"], writes=["ps4"])
    S.run("DVE", lambda h: h.tensor_copy(C.cb[:, 0:1], C.ps[4][:, 0:1]),
          reads=["ps4", "cb"], writes=["cb"])

    for i2 in range(8):
        pb = C.ps[2 + (i2 % 2)]
        S.run("PE", lambda h, i2=i2, pb=pb: h.matmul(
            pb[0:64, :], kl, qh[:, 512 * i2:512 * i2 + 512], start=True, stop=True),
            reads=["tokp", "qg", f"ps{2 + (i2 % 2)}"], writes=[f"ps{2 + (i2 % 2)}"])
        S.run("ACT", lambda h, i2=i2, pb=pb: h.activation(
            out=U1[:, 512 * i2:512 * i2 + 512], in_=pb[0:64, :], func=AF.Exp, bias=C.cb[0:64, 0:1]),
            reads=[f"ps{2 + (i2 % 2)}", "cb"] + U1k, writes=U1k)

    for i2 in range(8):
        pb = C.ps[2 + (i2 % 2)]
        S.run("PE", lambda h, i2=i2, pb=pb: h.matmul(
            pb[0:64, :], ql, khh[:, 512 * i2:512 * i2 + 512], start=True, stop=True),
            reads=["tokp", "kg", f"ps{2 + (i2 % 2)}"], writes=[f"ps{2 + (i2 % 2)}"])
        S.run("DVE", lambda h, i2=i2, pb=pb: h.tensor_copy(C.qsw[0:64, 512 * i2:512 * i2 + 512], pb[0:64, :]),
            reads=[f"ps{2 + (i2 % 2)}", "qsw"], writes=["qsw"])
    S.run("DVE", lambda h: h.reduce_max(C.srow[0:64, 15:16], C.qsw[0:64, 0:4096], axis=AX.X),
          reads=["qsw", "srow"], writes=["srow"])
    S.run("DVE", lambda h: h.tensor_scalar(C.srow[0:64, 14:15], C.srow[0:64, 15:16], -1.0, None,
                                           ALU.mult, ALU.bypass),
          reads=["srow"], writes=["srow"])
    S.run("ACT", lambda h: h.activation(out=C.qsw[0:64, 0:4096], in_=C.qsw[0:64, 0:4096], func=AF.Exp,
                                        bias=C.srow[0:64, 14:15]),
          reads=["qsw", "srow"], writes=["qsw"])
    S.run("DVE", lambda h: h.tensor_reduce(C.srow[0:64, 15:16], C.qsw[0:64, 0:4096], axis=AX.X, op=ALU.add),
          reads=["qsw", "srow"], writes=["srow"])
    S.run("DVE", lambda h: h.reciprocal(C.srow[0:64, 14:15], C.srow[0:64, 15:16]),
          reads=["srow"], writes=["srow"])
    S.run("DVE", lambda h: h.tensor_scalar(C.qsw[0:64, 0:4096], C.qsw[0:64, 0:4096], C.srow[0:64, 14:15],
                                           None, ALU.mult, ALU.bypass),
          reads=["qsw", "srow"], writes=["qsw"])

    s2t = C.ysb[0:64, 320:384]
    S.run("PE", lambda h: h.matmul(C.ps[4][0:64, 0:64], ql, kl, start=True, stop=True),
          reads=["tokp", "ps4"], writes=["ps4"])
    S.run("DVE", lambda h: h.tensor_copy(s2t, C.ps[4][0:64, 0:64]),
          reads=["ps4", "ysb"], writes=["ysb"])
    S.run("DVE", lambda h: h.reduce_max(C.srow[0:64, 15:16], s2t, axis=AX.X),
          reads=["ysb", "srow"], writes=["srow"])
    S.run("DVE", lambda h: h.tensor_scalar(C.srow[0:64, 14:15], C.srow[0:64, 15:16], -1.0, None,
                                           ALU.mult, ALU.bypass),
          reads=["srow"], writes=["srow"])
    S.run("ACT", lambda h: h.activation(out=s2t, in_=s2t, func=AF.Exp, bias=C.srow[0:64, 14:15]),
          reads=["ysb", "srow"], writes=["ysb"])
    S.run("DVE", lambda h: h.tensor_reduce(C.srow[0:64, 15:16], s2t, axis=AX.X, op=ALU.add),
          reads=["ysb", "srow"], writes=["srow"])
    S.run("DVE", lambda h: h.reciprocal(C.srow[0:64, 14:15], C.srow[0:64, 15:16]),
          reads=["srow"], writes=["srow"])
    S.run("DVE", lambda h: h.tensor_scalar(s2t, s2t, C.srow[0:64, 14:15], None, ALU.mult, ALU.bypass),
          reads=["ysb", "srow"], writes=["ysb"])

    s3T = C.scr if s == 0 else C.upool
    s3Tk = ["scrA", "scrB", "scrC", "scrD"] if s == 0 else ["upool"]
    for jc in range(32):
        pb = C.ps[2 + (jc % 2)]
        S.run("PE", lambda h, jc=jc, pb=pb: h.transpose(
            pb[:, 0:64], C.qsw[0:64, 128 * jc:128 * jc + 128], IDENT[0:64, 0:64]),
            reads=["qsw", "ident", f"ps{2 + (jc % 2)}"], writes=[f"ps{2 + (jc % 2)}"])
        S.run("DVE", lambda h, jc=jc, pb=pb: h.tensor_copy(s3T[:, 64 * jc:64 * jc + 64], pb[:, 0:64]),
            reads=[f"ps{2 + (jc % 2)}"] + s3Tk, writes=s3Tk)

    def mm_s3v(h):
        last = None
        for jc in range(32):
            last = h.matmul(C.ps[5][0:64, 0:64], s3T[:, 64 * jc:64 * jc + 64],
                            C.fw[jc % 2][:, 128 * (jc // 2) + 64 * s:128 * (jc // 2) + 64 * s + 64],
                            start=(jc == 0), stop=(jc == 31))
        return last
    S.run("PE", mm_s3v, reads=s3Tk + ["fw0", "fw1", "ps5"], writes=["ps5"])
    s3vt = C.ysb[0:64, 384:448]
    S.run("DVE", lambda h: h.tensor_copy(s3vt, C.ps[5][0:64, 0:64]),
          reads=["ps5", "ysb"], writes=["ysb"])

    _l1_pinv(nc, S, C, s2t, s3vt)

    def mm_r1b(h):
        last = None
        for c in range(8):
            last = h.matmul(C.ps[5][0:8, :], C.SEL8[0:64, 8 * c:8 * c + 8],
                            U1[:, 512 * c:512 * c + 512], start=(c == 0), stop=(c == 7))
        return last
    S.run("PE", mm_r1b, reads=U1k + ["sel8", "ps5"], writes=["ps5"])
    S.run("DVE", lambda h: h.reciprocal(C.ysb[0:8, 512:1024], C.ps[5][0:8, 0:512]),
          reads=["ps5", "ysb"], writes=["ysb"])

    wpt = C.ysb[0:64, 448:512]
    for c in range(8):
        S.run("PE", lambda h, c=c: h.matmul(C.ps[7][0:64, :], wpt, U1[:, 512 * c:512 * c + 512],
                                            start=True, stop=True),
              reads=["ysb", "wp_h", "ps7"] + U1k, writes=["ps7"])
        S.run("PE", lambda h, c=c: h.matmul(C.ps[6][0:64, :], C.SEL8T[0:8, 64 * c:64 * c + 64],
                                            C.ysb[0:8, 512:1024], start=True, stop=True),
              reads=["ysb", "sel8t", "ps6"], writes=["ps6"])
        S.run("DVE", lambda h, c=c: h.tensor_copy(C.vg[0:64, 512 * c:512 * c + 512], C.ps[6][0:64, :]),
              reads=["ps6", "vg"], writes=["vg"])
        S.run("DVE", lambda h, c=c: h.tensor_tensor(
            C.vg[0:64, 512 * c:512 * c + 512], C.vg[0:64, 512 * c:512 * c + 512],
            C.ps[7][0:64, :], ALU.mult),
            reads=["ps7", "vg"], writes=["vg"])
    S.dma("POOL", lambda h, head=head: h.dma_start(
        out=ap["att_sp"][64 * head:64 * head + 64, :], in_=C.vg[0:64, 0:4096]),
        reads=["vg"], writes=[f"att_{head}"], slot="atto")


def _l1_pinv(nc, S, C, s2t, s3vt):
    ONES = C.ONES
    IDENT = C.IDENT
    z = C.ysb[0:64, 0:64]
    zT = C.ysb[0:64, 64:128]
    xzs = C.ysb[0:64, 128:192]
    xzT = C.ysb[0:64, 192:256]
    tts = C.ysb[0:64, 256:320]
    s2T = C.ysb[0:64, 320:384]
    S.run("DVE", lambda h: h.tensor_reduce(C.srow[0:64, 15:16], s2t, axis=AX.X, op=ALU.add,
                                           apply_absolute_value=True),
          reads=["ysb", "srow"], writes=["srow"])
    S.run("PE", lambda h: h.transpose(C.ps[4][0:1, 0:64], C.srow[0:64, 15:16], C.IDENT[0:64, 0:64]),
          reads=["srow", "ident", "ps4"], writes=["ps4"])
    S.run("DVE", lambda h: h.reduce_max(C.srow[0:1, 14:15], C.ps[4][0:1, 0:64], axis=AX.X),
          reads=["ps4", "srow"], writes=["srow"])
    S.run("PE", lambda h: h.matmul(C.ps[4][0:1, 0:64], ONES[0:64, 0:1], s2t, start=True, stop=True),
          reads=["ysb", "ones", "ps4"], writes=["ps4"])
    S.run("DVE", lambda h: h.reduce_max(C.srow[0:1, 15:16], C.ps[4][0:1, 0:64], axis=AX.X,
                                        apply_absolute_value=True),
          reads=["ps4", "srow"], writes=["srow"])
    S.run("PE", lambda h: h.transpose(C.ps[4][0:64, 0:64], s2t, IDENT[0:64, 0:64]),
          reads=["ysb", "ident", "ps4"], writes=["ps4"])
    S.run("DVE", lambda h: h.tensor_copy(s2T, C.ps[4][0:64, 0:64]),
          reads=["ps4", "ysb"], writes=["ysb"])
    S.run("DVE", lambda h: h.tensor_tensor(C.srow[0:1, 13:14], C.srow[0:1, 14:15], C.srow[0:1, 15:16],
                                           ALU.mult),
          reads=["srow"], writes=["srow"])
    S.run("DVE", lambda h: h.reciprocal(C.srow[0:1, 12:13], C.srow[0:1, 13:14]),
          reads=["srow"], writes=["srow"])
    S.run("PE", lambda h: h.matmul(C.ps[4][0:64, 0:1], ONES[0:1, 0:64], C.srow[0:1, 12:13],
                                   start=True, stop=True),
          reads=["srow", "ones", "ps4"], writes=["ps4"])
    S.run("DVE", lambda h: h.tensor_copy(C.srow[0:64, 11:12], C.ps[4][0:64, 0:1]),
          reads=["ps4", "srow"], writes=["srow"])
    S.run("DVE", lambda h: h.tensor_scalar(z, s2T, C.srow[0:64, 11:12], None, ALU.mult, ALU.bypass),
          reads=["ysb", "srow"], writes=["ysb"])
    for it in range(6):
        S.run("PE", lambda h: h.matmul(C.ps[4][0:64, 0:64], s2T, z, start=True, stop=True),
              reads=["ysb", "ps4"], writes=["ps4"])
        S.run("DVE", lambda h: h.tensor_copy(xzs, C.ps[4][0:64, 0:64]),
              reads=["ps4", "ysb"], writes=["ysb"])
        S.run("PE", lambda h: h.transpose(C.ps[4][0:64, 0:64], xzs, IDENT[0:64, 0:64]),
              reads=["ident", "ps4", "ysb"], writes=["ps4"])
        S.run("DVE", lambda h: h.tensor_copy(xzT, C.ps[4][0:64, 0:64]),
              reads=["ps4", "ysb"], writes=["ysb"])
        S.run("DVE", lambda h: h.scalar_tensor_tensor(tts, IDENT[0:64, 0:64], 7.0, xzs,
                                                      ALU.mult, ALU.subtract),
              reads=["ident", "ysb"], writes=["ysb"])
        S.run("PE", lambda h: h.matmul(C.ps[4][0:64, 0:64], xzT, tts, start=True, stop=True),
              reads=["ysb", "ps4"], writes=["ps4"])
        S.run("DVE", lambda h: h.scalar_tensor_tensor(tts, IDENT[0:64, 0:64], 15.0, C.ps[4][0:64, 0:64],
                                                      ALU.mult, ALU.subtract),
              reads=["ident", "ps4", "ysb"], writes=["ysb"])
        S.run("PE", lambda h: h.matmul(C.ps[4][0:64, 0:64], xzT, tts, start=True, stop=True),
              reads=["ysb", "ps4"], writes=["ps4"])
        S.run("DVE", lambda h: h.scalar_tensor_tensor(tts, IDENT[0:64, 0:64], 13.0, C.ps[4][0:64, 0:64],
                                                      ALU.mult, ALU.subtract),
              reads=["ident", "ps4", "ysb"], writes=["ysb"])
        S.run("PE", lambda h: h.transpose(C.ps[4][0:64, 0:64], z, IDENT[0:64, 0:64]),
              reads=["ident", "ps4", "ysb"], writes=["ps4"])
        S.run("DVE", lambda h: h.tensor_copy(zT, C.ps[4][0:64, 0:64]),
              reads=["ps4", "ysb"], writes=["ysb"])
        S.run("PE", lambda h: h.matmul(C.ps[4][0:64, 0:64], zT, tts, start=True, stop=True),
              reads=["ysb", "ps4"], writes=["ps4"])
        S.run("DVE", lambda h: h.tensor_scalar(z, C.ps[4][0:64, 0:64], 0.25, None, ALU.mult, ALU.bypass),
              reads=["ps4", "ysb"], writes=["ysb"])
    S.run("PE", lambda h: h.transpose(C.ps[4][0:64, 0:64], z, IDENT[0:64, 0:64]),
          reads=["ident", "ps4", "ysb"], writes=["ps4"])
    S.run("DVE", lambda h: h.tensor_copy(zT, C.ps[4][0:64, 0:64]),
          reads=["ps4", "ysb"], writes=["ysb"])
    S.run("PE", lambda h: h.matmul(C.ps[4][0:64, 0:64], zT, s3vt, start=True, stop=True),
          reads=["ysb", "ysb", "ps4"], writes=["ps4"])
    S.run("DVE", lambda h: h.tensor_copy(C.ysb[0:64, 448:512], C.ps[4][0:64, 0:64]),
          reads=["ps4", "ysb"], writes=["ysb", "wp_h"])
